# version 11
# baseline (speedup 1.0000x reference)
import numpy as np
import concourse.bass as bass
import concourse.mybir as mybir
from concourse.bass_utils import run_bass_kernel_spmd

F32 = mybir.dt.float32
F16 = mybir.dt.float16
BF16 = mybir.dt.bfloat16
AF = mybir.ActivationFunctionType
ALU = mybir.AluOpType
AX = mybir.AxisListType

T = 2048
D = 1024
NEG = -1.0e30


ENGINES = ("tensor", "vector", "scalar", "gpsimd", "sync")


class _Op:
    __slots__ = ("eng", "fn", "deps", "is_dma", "sem_val", "need_inc", "tag",
                 "dma_sem")

    def __init__(self, eng, fn, deps, is_dma, tag=""):
        self.eng = eng
        self.fn = fn
        self.deps = deps
        self.is_dma = is_dma
        self.sem_val = None
        self.need_inc = False
        self.tag = tag


class Prog:
    def __init__(self):
        self.ops = {e: [] for e in ENGINES}
        self.cur_barrier = []
        self.joined = set(ENGINES)

    def op(self, eng, fn, deps=(), tag=""):
        dl = [d for d in _flat(deps) if d is not None]
        if eng not in self.joined:
            dl = dl + self.cur_barrier
            self.joined.add(eng)
        o = _Op(eng, fn, dl, False, tag)
        self.ops[eng].append(o)
        return o

    def dma(self, eng, out, in_, deps=(), tag=""):
        dl = [d for d in _flat(deps) if d is not None]
        if eng not in self.joined:
            dl = dl + self.cur_barrier
            self.joined.add(eng)
        o = _Op(eng, lambda E, out=out, in_=in_: E.dma_start(out=out, in_=in_),
                dl, True, tag)
        self.ops[eng].append(o)
        return o

    def barrier(self):
        b = []
        for e in ENGINES:
            ops = self.ops[e]
            if not ops:
                continue
            for o in reversed(ops):
                if not o.is_dma:
                    b.append(o)
                    break
            cnt = 0
            for o in reversed(ops):
                if o.is_dma:
                    b.append(o)
                    cnt += 1
                    if cnt >= 24:
                        break
        self.cur_barrier = b
        self.joined = set()
        return b

    def emit(self, nc, es):
        DMA_POOL = 24
        dma_lists = {}
        for e in ENGINES:
            dl = [o for o in self.ops[e] if o.is_dma]
            dma_lists[e] = dl
            for i, o in enumerate(dl):
                o.dma_sem = f"q_{e}_{i % DMA_POOL}"
                o.sem_val = 16 * (i // DMA_POOL + 1)
                if i >= DMA_POOL:
                    o.deps = o.deps + [dl[i - DMA_POOL]]
        for e in ENGINES:
            for o in self.ops[e]:
                for d in o.deps:
                    if d.is_dma:
                        continue
                    if d.eng == "tensor" and o.eng == "tensor":
                        continue
                    d.need_inc = True
        sems = {}
        for e in ENGINES:
            if self.ops[e]:
                sems["c_" + e] = es.enter_context(nc.semaphore("c_" + e))
            for i in range(min(DMA_POOL, len(dma_lists[e]))):
                nm = f"q_{e}_{i}"
                sems[nm] = es.enter_context(nc.semaphore(nm))
        totals = {}
        for e in ENGINES:
            c_cnt = 0
            for o in self.ops[e]:
                if (not o.is_dma) and o.need_inc:
                    c_cnt += 1
                    o.sem_val = c_cnt
            totals[e] = c_cnt
        dma_totals = {}
        for e in ENGINES:
            for i, o in enumerate(dma_lists[e]):
                dma_totals[o.dma_sem] = max(dma_totals.get(o.dma_sem, 0),
                                            o.sem_val)

        block = es.enter_context(nc.Block())
        prog = self

        def mk_section(e):
            def section(E):
                waited = {}
                for o in prog.ops[e]:
                    for dep in o.deps:
                        if dep.eng == "tensor" and e == "tensor" \
                                and not dep.is_dma:
                            continue
                        sem_name = dep.dma_sem if dep.is_dma else "c_" + dep.eng
                        assert dep.sem_val is not None, (
                            f"dep without sem: {dep.tag} <- {o.tag}")
                        if waited.get(sem_name, 0) < dep.sem_val:
                            E.wait_ge(sems[sem_name], dep.sem_val)
                            waited[sem_name] = dep.sem_val
                    inst = o.fn(E)
                    if o.is_dma:
                        inst.then_inc(sems[o.dma_sem], 16)
                    elif o.need_inc:
                        inst.then_inc(sems["c_" + e], 1)
                if e == "gpsimd":
                    for nm, v in sorted(dma_totals.items()):
                        E.wait_ge(sems[nm], v)
                    for e2 in ENGINES:
                        if totals[e2] and e2 != "gpsimd":
                            E.wait_ge(sems["c_" + e2], totals[e2])
            return section

        for e in ENGINES:
            if prog.ops[e] or e == "gpsimd":
                getattr(block, e)(mk_section(e))


def _flat(x):
    out = []
    stack = [x]
    while stack:
        v = stack.pop()
        if v is None:
            continue
        if isinstance(v, (list, tuple)):
            stack.extend(v)
        else:
            out.append(v)
    return out



class Slab:

    def __init__(self, nc, name, words_per_partition):
        self.t = nc.alloc_sbuf_tensor(name, [128, words_per_partition], F32)
        self.words = words_per_partition
        self.free_list = [(0, words_per_partition)]
        self.live = {}
        self.hwm = 0

    def take(self, words, dtype=F32, name=None, side="low"):
        assert name is not None and name not in self.live, name
        blocks = self.free_list if side == "low" else list(reversed(
            self.free_list))
        for off0, ln in blocks:
            if ln >= words:
                i = self.free_list.index((off0, ln))
                if side == "low":
                    off = off0
                    rest = (off0 + words, ln - words)
                else:
                    off = off0 + ln - words
                    rest = (off0, ln - words)
                if ln == words:
                    self.free_list.pop(i)
                else:
                    self.free_list[i] = rest
                self.live[name] = (off, words)
                v = self.t.ap()[:, off:off + words]
                return v if dtype == F32 else v.bitcast(dtype)
        raise AssertionError(
            f"slab overflow for {name}({words}); live="
            f"{sorted((n, o, l) for n, (o, l) in self.live.items())}")

    def free(self, *names):
        for name in names:
            off, ln = self.live.pop(name)
            self.free_list.append((off, ln))
        self.free_list.sort()
        merged = []
        for off, ln in self.free_list:
            if merged and merged[-1][0] + merged[-1][1] == off:
                merged[-1] = (merged[-1][0], merged[-1][1] + ln)
            else:
                merged.append((off, ln))
        self.free_list = [tuple(x) for x in merged]



def prep_weights(I):
    H = {}
    ct = np.ascontiguousarray
    H["w0_Win"] = ct(I["L0_attn_W_in"])
    H["w0_dwT"] = ct(np.asarray(I["L0_attn_dw_kernel"]).T.reshape(16, 128, 4)
                     .transpose(1, 0, 2).reshape(128, 64))
    H["w0_cb"] = ct(np.asarray(I["L0_attn_conv_bias"]).reshape(16, 128).T)
    H["w0_pw"] = ct(I["L0_attn_pw_kernel"])
    H["w0_Wx"] = ct(I["L0_attn_W_x"])
    H["w0_Wdt"] = ct(I["L0_attn_W_dt"])
    H["w0_bdt"] = ct(np.asarray(I["L0_attn_b_dt"]).reshape(16, 128).T)
    H["w0_Alog"] = ct(np.asarray(I["L0_attn_A_log"]).reshape(16, 128, 8)
                      .transpose(1, 0, 2).reshape(128, 128))
    H["w0_D"] = ct(np.asarray(I["L0_attn_D"]).reshape(16, 128).T)
    H["w0_Wout"] = ct(I["L0_attn_W_out"])
    H["wf0_gwT"] = ct(I["L0_ffn_gate_w"].T)
    H["wf0_gb"] = ct(np.broadcast_to(I["L0_ffn_gate_b"].reshape(1, -1), (128, 16)))
    H["wf0_We1"] = ct(I["L0_ffn_We1"])
    H["wf0_be1"] = ct(np.asarray(I["L0_ffn_be1"]).reshape(16, 4, 128)
                      .transpose(0, 2, 1))
    H["wf0_We3"] = ct(I["L0_ffn_We3"])
    H["wf0_be3"] = ct(np.asarray(I["L0_ffn_be3"]).reshape(16, 4, 128)
                      .transpose(0, 2, 1))
    H["wf0_We2"] = ct(I["L0_ffn_We2"])
    H["wf0_be2s"] = ct(np.concatenate(
        [I["L0_ffn_be2"], I["L0_ffn_bs2"].reshape(1, -1)], axis=0))
    H["wf0_ws1"] = ct(I["L0_ffn_ws1"])
    H["wf0_bs1"] = ct(np.asarray(I["L0_ffn_bs1"]).reshape(8, 128).T)
    H["wf0_ws3"] = ct(I["L0_ffn_ws3"])
    H["wf0_bs3"] = ct(np.asarray(I["L0_ffn_bs3"]).reshape(8, 128).T)
    H["wf0_ws2"] = ct(I["L0_ffn_ws2"])
    H["w1_A"] = ct(np.asarray(I["L1_attn_A"]).reshape(8, 128, 16)
                   .transpose(1, 0, 2).reshape(128, 128))
    H["w1_WdT"] = ct(I["L1_attn_Wd"].T)
    H["w1_bd"] = ct(np.asarray(I["L1_attn_bd"]).reshape(8, 128).T)
    H["w1_WbT"] = ct(I["L1_attn_Wb"].T)
    H["w1_bb"] = ct(I["L1_attn_bb"].reshape(-1, 1))
    H["w1_WcT"] = ct(I["L1_attn_Wc"].T)
    H["w1_bc"] = ct(I["L1_attn_bc"].reshape(-1, 1))
    H["wf1_rwT"] = ct(I["L1_ffn_router_weight"].T)
    H["wf1_eb"] = ct(np.broadcast_to(I["L1_ffn_e_score_bias"].reshape(1, -1), (128, 32)))
    H["wf1_gupT"] = ct(np.transpose(I["L1_ffn_gate_up_proj"], (0, 2, 1)))
    H["wf1_downT"] = ct(np.transpose(I["L1_ffn_down_proj"], (0, 2, 1)))
    H["wf1_sgwT"] = ct(I["L1_ffn_shared_gate_w"].T)
    H["wf1_suwT"] = ct(I["L1_ffn_shared_up_w"].T)
    H["wf1_sdwT"] = ct(I["L1_ffn_shared_down_w"].T)
    H["c_eye_bf"] = np.eye(128, dtype=np.float32)
    H["c_eye_f16"] = np.eye(128, dtype=np.float32)
    H["c_sel16"] = np.repeat(np.eye(16, dtype=np.float32), 128, axis=1)
    H["c_sel32"] = np.repeat(np.eye(32, dtype=np.float32), 128, axis=1)
    H["c_ones1"] = np.ones((1, T), dtype=np.float32)
    cons = np.zeros((128, 8), dtype=np.float32)
    cons[:, 0] = 1e-5
    cons[:, 1] = -0.5
    cons[:, 2] = 1.0
    H["c_cons"] = cons
    return H


class Drams(dict):
    def __init__(self, nc):
        super().__init__()
        self.nc = nc

    def inp(self, name, arr):
        self[name] = self.nc.dram_tensor(name, list(arr.shape), F32,
                                         kind="ExternalInput").ap()
        return self[name]

    def out(self, name, shape):
        self[name] = self.nc.dram_tensor(name, list(shape), F32,
                                         kind="ExternalOutput").ap()
        return self[name]

    def scratch(self, name, shape, dtype=F32):
        kind = "ExternalOutput" if getattr(self, "dbg_external", False) else "Internal"
        self[name] = self.nc.dram_tensor(name, list(shape), dtype,
                                         kind=kind).ap()
        return self[name]



def ln_stats_ops(p, sb_x, mean_negms, rstd, scratch_row, scratch_col, deps,
                 dim, dr=None):
    ssum = scratch_col[:, 0:1]
    ssq = scratch_col[:, 1:2]
    o1 = p.op("vector", lambda E: E.tensor_reduce(ssum, sb_x, AX.X, ALU.add),
              deps, tag="ln_sum")
    o2 = p.op("scalar", lambda E: E.activation(scratch_row, sb_x, AF.Square,
                                               accum_out=ssq),
              deps, tag="ln_sq")
    o3 = p.op("vector", lambda E: E.tensor_scalar(mean_negms, ssum, 1.0 / dim,
                                                  None, ALU.mult),
              [o1], tag="ln_m")
    o4 = p.op("vector", lambda E: E.tensor_tensor(ssum, mean_negms, mean_negms,
                                                  ALU.mult),
              [o3], tag="ln_m2")
    o5 = p.op("vector", lambda E: E.scalar_tensor_tensor(
        ssq, ssq, 1.0 / dim, ssum, ALU.mult, ALU.subtract),
        [o2, o4], tag="ln_var")
    cb = dr["sb_cons"]
    o6 = p.op("scalar", lambda E: E.activation(rstd, ssq, AF.Ln,
                                               bias=cb[:, 0:1]),
              [o5, dr["sb_cons_ld"]], tag="ln_lnv")
    o7 = p.op("scalar", lambda E: E.activation(rstd, rstd, AF.Exp,
                                               scale=cb[:, 1:2]),
              [o6], tag="ln_rstd")
    o8 = p.op("vector", lambda E: E.scalar_tensor_tensor(
        mean_negms, mean_negms, -1.0, rstd, ALU.mult, ALU.mult),
        [o7, o3], tag="ln_negms")
    return [o7, o8]



def emit_ln_to_fm(p, nc, dr, sl, ps, x_hbm, xfm, lo=None, name="",
                  tile_deps=None):
    nm = name + "lnfm_"
    xtiles = [sl.take(1024, F32, name=nm + f"x{i}", side="high") for i in range(2)]
    ln32 = [sl.take(1024, F32, name=nm + f"l{i}", side="high") for i in range(2)]
    scratch_row = sl.take(1024, F32, name=nm + "sc", side="high")
    cols = sl.take(16, F32, name=nm + "cols", side="high")
    pA = [ps["A"], ps["B"]]
    evac_ops = []
    war_x = {0: None, 1: None}
    war_ln = {0: None, 1: None}
    war_p = {0: None, 1: None}
    war_stats = {}
    xfm3 = xfm.rearrange("p (f t) -> p f t", f=8)
    lo3 = lo.rearrange("p (f t) -> p f t", f=8) if lo is not None else None
    for tt in range(16):
        bi = tt % 2
        xtile = xtiles[bi]
        ld = p.dma("sync", xtile, x_hbm[tt * 128:(tt + 1) * 128, :],
                   deps=[war_x[bi],
                         tile_deps[tt] if tile_deps is not None else None],
                   tag=f"{name}ln_ld{tt}")
        mcol = cols[:, 4 * bi:4 * bi + 1]
        rcol = cols[:, 4 * bi + 1:4 * bi + 2]
        scol = cols[:, 4 * bi + 2:4 * bi + 4]
        stats = ln_stats_ops(p, xtile, mcol, rcol, scratch_row, scol,
                             [ld, war_stats.get(bi)], D, dr=dr)
        ap32 = p.op("scalar", lambda E, o=ln32[bi], i=xtile, r=rcol, m=mcol:
                    E.activation(o, i, AF.Identity, bias=m, scale=r),
                    [stats, war_ln[bi]], tag=f"{name}ln_ap{tt}")
        war_stats[bi] = stats + [ap32]
        war_x[bi] = ap32
        trs = []
        for f in range(8):
            t = p.op("tensor", lambda E, o=pA[bi][:, f * 128:(f + 1) * 128],
                     i=ln32[bi][:, f * 128:(f + 1) * 128], idn=dr["sb_eye"]:
                     E.transpose(o, i, idn),
                     [ap32, war_p[bi], dr["sb_eye_ld"]],
                     tag=f"{name}ln_tr{tt}_{f}")
            trs.append(t)
        war_ln[bi] = trs
        src = pA[bi][:].rearrange("p (f t) -> p f t", f=8)
        ev = p.op("vector", lambda E, o=xfm3[:, :, tt * 128:(tt + 1) * 128],
                  i=src: E.tensor_copy(o, i), trs, tag=f"{name}ln_ev{tt}")
        evac_ops.append(ev)
        war_p[bi] = ev
        if lo is not None:
            ev2 = p.op("vector",
                       lambda E, o=lo3[:, :, tt * 128:(tt + 1) * 128],
                       i=src, h=xfm3[:, :, tt * 128:(tt + 1) * 128]:
                       E.tensor_tensor(o, i, h, ALU.subtract),
                       [ev], tag=f"{name}ln_lo{tt}")
            evac_ops.append(ev2)
            war_p[bi] = ev2
    sl.free(nm + "x0", nm + "x1", nm + "l0", nm + "l1", nm + "sc", nm + "cols")
    return {"evac": evac_ops}


def ln_res_add_store(p, nc, dr, sl, ps, src_sb_fn, x_old_hbm, x_new_hbm,
                     readers_out=None, name=""):
    nm = name + "ra_"
    xtiles = [sl.take(1024, F32, name=nm + f"x{i}") for i in range(2)]
    t1 = [sl.take(1024, F32, name=nm + f"t{i}") for i in range(2)]
    scratch_row = sl.take(1024, F32, name=nm + "sc")
    cols = sl.take(16, F32, name=nm + "cols")
    dr["_defer"] = dr.get("_defer", []) + [nm + "x0", nm + "x1", nm + "t0",
                                           nm + "t1", nm + "sc", nm + "cols"]
    stores = []
    war_x = {0: None, 1: None}
    war_t = {0: None, 1: None}
    war_st = {}
    for tt in range(16):
        bi = tt % 2
        ld = p.dma("sync", xtiles[bi], x_old_hbm[tt * 128:(tt + 1) * 128, :],
                   deps=[war_x[bi]], tag=f"{name}ra_ld{tt}")
        ftile, fdeps = src_sb_fn(tt)
        mcol = cols[:, 4 * bi:4 * bi + 1]
        rcol = cols[:, 4 * bi + 1:4 * bi + 2]
        scol = cols[:, 4 * bi + 2:4 * bi + 4]
        stats = ln_stats_ops(p, ftile, mcol, rcol, scratch_row, scol,
                             [fdeps, war_st.get(bi)], D, dr=dr)
        apt = p.op("scalar", lambda E, o=t1[bi], i=ftile, r=rcol, m=mcol:
                   E.activation(o, i, AF.Identity, bias=m, scale=r),
                   [stats, war_t[bi]], tag=f"{name}ra_ap{tt}")
        war_st[bi] = stats + [apt]
        addo = p.op("vector", lambda E, o=t1[bi], a=t1[bi], b=xtiles[bi]:
                    E.tensor_tensor(o, a, b, ALU.add),
                    [apt, ld], tag=f"{name}ra_add{tt}")
        war_x[bi] = addo
        st = p.dma("sync", x_new_hbm[tt * 128:(tt + 1) * 128, :], t1[bi],
                   deps=[addo], tag=f"{name}ra_st{tt}")
        stores.append(st)
        war_t[bi] = st
        if readers_out is not None:
            readers_out.append(stats + [apt])
    return stores


def _scan_yadd(p, yslice, hc_ap, hc_op, yprev, war, warkey, tag):
    if yprev is None:
        y_o = p.op("gpsimd", lambda E, o=yslice, i=hc_ap: E.tensor_copy(o, i),
                   [hc_op], tag=tag)
    else:
        y_o = p.op("gpsimd", lambda E, o=yslice, a=hc_ap, b=yslice:
                   E.tensor_tensor(o, a, b, ALU.add),
                   [hc_op, yprev], tag=tag)
    war[warkey] = y_o
    return y_o



def emit_l0_attn(p, nc, dr, sl, ps, x_in_hbm, x_out_hbm, name="a0_",
                 prev_stores=None):
    import os
    UPTO = int(os.environ.get("L0A_UPTO", "9"))
    DI = 2048
    NCH = 4
    CW = 512
    xin_hbm = dr["s_xin"]
    z_hbm = dr["s_z"]
    xc_hbm = dr["s_xc"]
    dl_hbm = dr["s_dl"]

    bmcsb = sl.take(1024, BF16, name="a_bmc")

    xfm = sl.take(8 * 1024, BF16, name="a_xfm", side="high")
    winb = [sl.take(512, BF16, name=f"a_win{i}") for i in range(2)]
    stg = [sl.take(256, F16, name=f"a_stg{i}") for i in range(2)]
    pwv = sl.take(16 * 1024, BF16, name="a_pw").rearrange("p (k c) -> p k c", k=16)
    pw_ld = p.dma("gpsimd", pwv, dr["w0_pw"].rearrange("(k p) c -> p k c", p=128),
                  tag=f"{name}pw_ld")
    win_pre = []
    for m in range(2):
        wv = winb[m].rearrange("p (k c) -> p k c", k=8)
        win_pre.append(p.dma("gpsimd", wv,
                             dr["w0_Win"][:, m * 128:(m + 1) * 128]
                             .rearrange("(k p) c -> p k c", p=128),
                             tag=f"{name}win_ld{m}"))
    ln = emit_ln_to_fm(p, nc, dr, sl, ps, x_in_hbm, xfm, name=name,
                       tile_deps=prev_stores)
    p.barrier()
    for nmf in dr.pop("_defer", []):
        sl.free(nmf)
    xfm3 = xfm.rearrange("p (f t) -> p f t", f=8)

    pX = [ps["C"], ps["D"]]
    war_w = {0: None, 1: None}
    war_s = {0: None, 1: None}
    war_px = {0: None, 1: None}
    st1_stores = []
    for m in range(32):
        wb = m % 2
        wv = winb[wb].rearrange("p (k c) -> p k c", k=8)
        if m < 2:
            wld = win_pre[m]
        else:
            wld = p.dma("gpsimd", wv,
                        dr["w0_Win"][:, m * 128:(m + 1) * 128]
                        .rearrange("(k p) c -> p k c", p=128),
                        deps=[war_w[wb]], tag=f"{name}win_ld{m}")
        wreaders = []
        for ch in range(NCH):
            pb = (m * NCH + ch) % 2
            mms = []
            for k in range(8):
                mm = p.op("tensor", lambda E, o=pX[pb], w=wv[:, k, :],
                          x=xfm3[:, k, ch * CW:(ch + 1) * CW], s=(k == 0),
                          t=(k == 7): E.matmul(o, w, x, start=s, stop=t),
                          [wld, war_px[pb]] + (ln["evac"] if m == 0 else []),
                          tag=f"{name}xz{m}_{ch}_{k}")
                mms.append(mm)
            wreaders += mms
            sb = (m * NCH + ch) % 2
            if m < 16:
                ev = p.op("vector", lambda E, o=stg[sb], i=pX[pb]:
                          E.tensor_copy(o, i), [mms[-1], war_s[sb]],
                          tag=f"{name}xin_ev{m}_{ch}")
                st = p.dma("sync", xin_hbm[m, :, ch * CW:(ch + 1) * CW],
                           stg[sb], deps=[ev], tag=f"{name}xin_st{m}_{ch}")
            else:
                ev = p.op("scalar", lambda E, o=stg[sb], i=pX[pb]:
                          E.activation(o, i, AF.Silu), [mms[-1], war_s[sb]],
                          tag=f"{name}z_ev{m}_{ch}")
                st = p.dma("sync", z_hbm[m - 16, :, ch * CW:(ch + 1) * CW],
                           stg[sb], deps=[ev], tag=f"{name}z_st{m}_{ch}")
            st1_stores.append(st)
            war_s[sb] = st
            war_px[pb] = ev
        war_w[wb] = wreaders
    sl.free("a_xfm", "a_win0", "a_win1", "a_stg0", "a_stg1")
    p.barrier()
    if UPTO <= 1:
        return []

    xinb = [sl.take(16 * 264, F16, name=f"a_xin{i}")
            .rearrange("p (d c) -> p d c", d=16) for i in range(2)]
    cvb = [sl.take(16 * 256, BF16, name=f"a_cv{i}")
           .rearrange("p (d c) -> p d c", d=16) for i in range(2)]
    xcstg = [sl.take(256, BF16, name=f"a_xcs{i}") for i in range(2)]
    dstg = [sl.take(256, F16, name=f"a_dst{i}") for i in range(2)]
    wxsb = sl.take(16 * 40, BF16, name="a_wx").rearrange("p (k c) -> p k c", k=16)
    wdtsb = sl.take(1024, BF16, name="a_wdt")
    drsb = sl.take(1024, BF16, name="a_dr")
    dwsb = sl.take(16 * 4, F32, name="a_dw").rearrange("p (d c) -> p d c", d=16)
    cbsb = sl.take(16, F32, name="a_cb").rearrange("p (d c) -> p d c", d=16)
    bdtsb = sl.take(16, F32, name="a_bdt").rearrange("p (d c) -> p d c", d=16)
    tmp32 = [sl.take(512, F32, name=f"a_tmp{i}") for i in range(2)]

    wx_ld = p.dma("gpsimd", wxsb, dr["w0_Wx"].rearrange("(k p) c -> p k c", p=128),
                  tag=f"{name}wx_ld")
    wdt_ld = p.dma("gpsimd", wdtsb[0:64, :], dr["w0_Wdt"], tag=f"{name}wdt_ld")
    dw_ld = p.dma("sync", dwsb, dr["w0_dwT"].rearrange("p (d c) -> p d c", d=16),
                  tag=f"{name}dw_ld")
    cb_ld = p.dma("sync", cbsb, dr["w0_cb"].rearrange("p (d c) -> p d c", d=16),
                  tag=f"{name}cb_ld")
    bdt_ld = p.dma("sync", bdtsb, dr["w0_bdt"].rearrange("p (d c) -> p d c", d=16),
                   tag=f"{name}bdt_ld")

    pP = [ps["C"], ps["D"]]
    pXD = ps["A"][0:80, 0:512]
    pDT = [ps["B"][:, 0:512], ps["B"][:, 512:1024]]
    war_xin = {0: None, 1: None}
    war_cv = {0: None, 1: None}
    war_pp = {0: None, 1: None}
    war_pdt = {0: None, 1: None}
    war_xcs = {0: None, 1: None}
    war_tmp = {0: None, 1: None}
    war_dst = {0: None, 1: None}
    war_pxd = None
    bmc_evs = []
    for ch in range(NCH):
        cbuf = ch % 2
        conv_ops = []
        for dt_ in range(16):
            xv = xinb[cbuf][:, dt_, :]
            if ch == 0:
                zp = p.op("gpsimd", lambda E, o=xv[:, 0:3]: E.memset(o, 0.0),
                          [war_xin[cbuf]], tag=f"{name}cvzp{dt_}")
                ldd = p.dma("sync", xv[:, 3:515], xin_hbm[dt_, :, 0:512],
                            deps=[war_xin[cbuf], st1_stores],
                            tag=f"{name}cvld{ch}_{dt_}")
                ld = [ldd, zp]
            else:
                ld = [p.dma("sync", xv[:, 0:515],
                            xin_hbm[dt_, :, ch * CW - 3:(ch + 1) * CW],
                            deps=[war_xin[cbuf], st1_stores],
                            tag=f"{name}cvld{ch}_{dt_}")]
            cv = cvb[cbuf][:, dt_, :]
            cprev = p.op("vector", lambda E, o=cv, i=xv[:, 0:512],
                         s=dwsb[:, dt_, 0:1]:
                         E.tensor_scalar(o, i, s, None, ALU.mult),
                         [ld, war_cv[cbuf], dw_ld], tag=f"{name}cv0_{ch}_{dt_}")
            for tap in range(1, 4):
                cprev = p.op("vector", lambda E, o=cv, i=xv[:, tap:tap + 512],
                             s=dwsb[:, dt_, tap:tap + 1], a=cv:
                             E.scalar_tensor_tensor(o, i, s, a, ALU.mult, ALU.add),
                             [cprev], tag=f"{name}cv{tap}_{ch}_{dt_}")
            conv_ops.append(cprev)
        war_xin[cbuf] = conv_ops
        xd_mms = []
        xc_evs = []
        for m2 in range(16):
            pb = (ch * 16 + m2) % 2
            mms = []
            for k in range(16):
                mm = p.op("tensor", lambda E, o=pP[pb],
                          w=pwv[:, k, m2 * 128:(m2 + 1) * 128],
                          x=cvb[cbuf][:, k, :], s=(k == 0), t=(k == 15):
                          E.matmul(o, w, x, start=s, stop=t),
                          [pw_ld if (ch == 0 and m2 == 0) else None,
                           conv_ops[k], war_pp[pb]],
                          tag=f"{name}pw{ch}_{m2}_{k}")
                mms.append(mm)
            sb = m2 % 2
            ev = p.op("scalar", lambda E, o=xcstg[sb], i=pP[pb],
                      b=cbsb[:, m2, 0:1]:
                      E.activation(o, i, AF.Silu, bias=b),
                      [mms[-1], war_xcs[sb], cb_ld], tag=f"{name}xc_ev{ch}_{m2}")
            xc_evs.append(ev)
            war_pp[pb] = ev
            st = p.dma("sync", xc_hbm[m2, :, ch * CW:(ch + 1) * CW],
                       xcstg[sb], deps=[ev], tag=f"{name}xc_st{ch}_{m2}")
            mmx = p.op("tensor", lambda E, o=pXD, w=wxsb[:, m2, :], x=xcstg[sb],
                       s=(m2 == 0), t=(m2 == 15):
                       E.matmul(o, w, x, start=s, stop=t),
                       [ev, wx_ld, war_pxd if m2 == 0 else None],
                       tag=f"{name}wx{ch}_{m2}")
            xd_mms.append(mmx)
            war_xcs[sb] = [st, mmx]
        ev_dr = p.op("vector", lambda E, o=drsb[0:64, ch * CW:(ch + 1) * CW],
                     i=pXD[0:64, :]: E.tensor_copy(o, i),
                     [xd_mms[-1]], tag=f"{name}dr_ev{ch}")
        ev_bc = p.op("vector", lambda E, o=bmcsb[0:16, ch * CW:(ch + 1) * CW],
                     i=ps["A"][64:80, 0:512]: E.tensor_copy(o, i),
                     [xd_mms[-1]], tag=f"{name}bc_ev{ch}")
        bmc_evs.append(ev_bc)
        war_pxd = [ev_dr, ev_bc]
        for m in range(16):
            pb = m % 2
            mm = p.op("tensor", lambda E, o=pDT[pb],
                      w=wdtsb[0:64, m * 128:(m + 1) * 128],
                      x=drsb[0:64, ch * CW:(ch + 1) * CW]:
                      E.matmul(o, w, x, start=True, stop=True),
                      [ev_dr, wdt_ld, war_pdt[pb]], tag=f"{name}dt{ch}_{m}")
            sb = m % 2
            e1 = p.op("scalar", lambda E, o=tmp32[sb], i=pDT[pb],
                      b=bdtsb[:, m, 0:1]:
                      E.activation(o, i, AF.Exp, bias=b),
                      [mm, war_tmp[sb], bdt_ld], tag=f"{name}dte{ch}_{m}")
            war_pdt[pb] = e1
            e2 = p.op("scalar", lambda E, o=dstg[sb], i=tmp32[sb],
                      b=dr["sb_cons"][:, 2:3]:
                      E.activation(o, i, AF.Ln, bias=b),
                      [e1, war_dst[sb], dr["sb_cons_ld"]],
                      tag=f"{name}dtl{ch}_{m}")
            war_tmp[sb] = e2
            st = p.dma("sync", dl_hbm[m, :, ch * CW:(ch + 1) * CW],
                       dstg[sb], deps=[e2], tag=f"{name}dt_st{ch}_{m}")
            war_dst[sb] = st
    sl.free("a_pw", "a_xin0", "a_xin1", "a_cv0", "a_cv1", "a_xcs0", "a_xcs1",
            "a_dst0", "a_dst1", "a_wx", "a_wdt", "a_dr", "a_dw", "a_cb",
            "a_bdt", "a_tmp0", "a_tmp1")
    p.barrier()
    if UPTO <= 2:
        return []

    negA = sl.take(16 * 8, F32, name="a_negA").rearrange("p (d c) -> p d c", d=16)
    bca = sl.take(16 * 1024, F16, name="a_bca").rearrange("p (r c) -> p r c", r=16)
    selb = sl.take(1024, BF16, name="a_sel")

    al_ld = p.dma("sync", negA, dr["w0_Alog"].rearrange("p (d c) -> p d c", d=16),
                  tag=f"{name}al_ld")
    aex = p.op("scalar", lambda E, o=negA, i=negA: E.activation(o, i, AF.Exp),
               [al_ld], tag=f"{name}al_exp")
    aneg = p.op("vector", lambda E, o=negA, i=negA:
                E.tensor_scalar(o, i, -1.0, None, ALU.mult),
                [aex], tag=f"{name}al_neg")
    sel_ld = p.dma("gpsimd", selb[0:16, :], dr["c_sel16"], tag=f"{name}sel_ld")
    pBC = ps["C"]
    war_pbc = None
    bc_ready = []
    for r in range(16):
        evs = []
        for q in range(4):
            mm = p.op("tensor", lambda E, o=pBC,
                      w=selb[0:16, r * 128:(r + 1) * 128],
                      x=bmcsb[0:16, q * 512:(q + 1) * 512]:
                      E.matmul(o, w, x, start=True, stop=True),
                      [sel_ld, bmc_evs, war_pbc], tag=f"{name}bc{r}_{q}")
            ev = p.op("scalar", lambda E, o=bca[:, r, q * 512:(q + 1) * 512],
                      i=pBC: E.activation(o, i, AF.Identity),
                      [mm], tag=f"{name}bcev{r}_{q}")
            war_pbc = ev
            evs.append(ev)
        bc_ready.append(evs)

    yac = sl.take(16 * 1024, F16, name="a_yac").rearrange("p (d c) -> p d c", d=16)
    dl_b = [sl.take(1024, F16, name=f"a_sdl{i}") for i in range(2)]
    xc_b = [sl.take(1024, BF16, name=f"a_sxc{i}") for i in range(2)]
    u_b = [sl.take(1024, F16, name=f"a_su{i}") for i in range(2)]
    a_b = [sl.take(2048, F32, name=f"a_sa{i}") for i in range(2)]
    b_b = [sl.take(1024, F16, name=f"a_sb{i}") for i in range(2)]
    h_b = [sl.take(1024, F16, name=f"a_sh{i}") for i in range(2)]
    hc_b = [sl.take(1024, F16, name=f"a_shc{i}") for i in range(2)]
    war = {}
    yac_done = []
    for d in range(16):
        db = d % 2
        ld_dl = p.dma("sync", dl_b[db], dl_hbm[d], deps=[war.get(("dl", db))],
                      tag=f"{name}s_dl{d}")
        ld_xc = p.dma("sync", xc_b[db], xc_hbm[d], deps=[war.get(("xc", db))],
                      tag=f"{name}s_xc{d}")
        uop = p.op("gpsimd", lambda E, o=u_b[db], a=dl_b[db], b=xc_b[db]:
                   E.tensor_tensor(o, a, b, ALU.mult),
                   [ld_dl, ld_xc, war.get(("u", db))], tag=f"{name}s_u{d}")
        war[("xc", db)] = uop
        dl_readers = [uop]
        u_readers = []
        yprev = None
        pend = None
        for n in range(8):
            nb = n % 2
            a_o = p.op("scalar", lambda E, o=a_b[nb], i=dl_b[db],
                       s=negA[:, d, n:n + 1]:
                       E.activation(o, i, AF.Exp, scale=s),
                       [ld_dl, aneg, war.get(("a", nb))],
                       tag=f"{name}s_a{d}_{n}")
            dl_readers.append(a_o)
            b_o = p.op("vector", lambda E, o=b_b[nb], a=u_b[db],
                       b=bca[:, n, :]: E.tensor_tensor(o, a, b, ALU.mult),
                       [uop, bc_ready[n], war.get(("b", nb))],
                       tag=f"{name}s_b{d}_{n}")
            u_readers.append(b_o)
            h_o = p.op("vector", lambda E, o=h_b[nb], a=a_b[nb], b=b_b[nb]:
                       E.tensor_tensor_scan(o, a, b, 0.0, ALU.mult, ALU.add),
                       [a_o, b_o, war.get(("h", nb))],
                       tag=f"{name}s_h{d}_{n}")
            war[("a", nb)] = h_o
            war[("b", nb)] = h_o
            hc_o = p.op("vector" if n % 2 == 0 else "gpsimd",
                        lambda E, o=hc_b[nb], a=h_b[nb],
                        b=bca[:, 8 + n, :]: E.tensor_tensor(o, a, b, ALU.mult),
                        [h_o, bc_ready[8 + n], war.get(("hc", nb))],
                        tag=f"{name}s_hc{d}_{n}")
            war[("h", nb)] = hc_o
            if pend is not None:
                po, pap, pnb, pn = pend
                yprev = _scan_yadd(p, yac[:, d, :], pap, po, yprev, war,
                                   ("hc", pnb), f"{name}s_y{d}_{pn}")
            pend = (hc_o, hc_b[nb], nb, n)
        po, pap, pnb, pn = pend
        yprev = _scan_yadd(p, yac[:, d, :], pap, po, yprev, war,
                           ("hc", pnb), f"{name}s_y{d}_{pn}")
        war[("dl", db)] = dl_readers
        war[("u", db)] = u_readers
        yac_done.append(yprev)
    sl.free("a_bmc", "a_negA", "a_bca", "a_sel", "a_sdl0", "a_sdl1", "a_sxc0",
            "a_sxc1", "a_su0", "a_su1", "a_sa0", "a_sa1", "a_sb0", "a_sb1",
            "a_sh0", "a_sh1", "a_shc0", "a_shc1")
    p.barrier()
    if UPTO <= 4:
        return []

    yf = sl.take(16 * 1024, BF16, name="a_yf").rearrange("p (d c) -> p d c", d=16)
    xc_b = [sl.take(1024, BF16, name=f"a_fxc{i}") for i in range(2)]
    z_b = [sl.take(1024, F16, name=f"a_fz{i}") for i in range(2)]
    s1_b = [sl.take(2048, F32, name=f"a_fs{i}") for i in range(2)]
    Dsb = sl.take(16, F32, name="a_D").rearrange("p (d c) -> p d c", d=16)
    d_ld = p.dma("sync", Dsb, dr["w0_D"].rearrange("p (d c) -> p d c", d=16),
                 tag=f"{name}D_ld")
    war5 = {}
    yf_done = []
    for d in range(16):
        db = d % 2
        ld_xc = p.dma("sync", xc_b[db], xc_hbm[d], deps=[war5.get(("xc", db))],
                      tag=f"{name}f_xc{d}")
        ld_z = p.dma("sync", z_b[db], z_hbm[d], deps=[war5.get(("z", db))],
                     tag=f"{name}f_z{d}")
        s1 = p.op("vector", lambda E, o=s1_b[db], i=xc_b[db],
                  s=Dsb[:, d, 0:1], a=yac[:, d, :]:
                  E.scalar_tensor_tensor(o, i, s, a, ALU.mult, ALU.add),
                  [ld_xc, d_ld, yac_done[d], war5.get(("s1", db))],
                  tag=f"{name}f_s1{d}")
        war5[("xc", db)] = s1
        yfo = p.op("vector", lambda E, o=yf[:, d, :], a=s1_b[db], b=z_b[db]:
                   E.tensor_tensor(o, a, b, ALU.mult),
                   [s1, ld_z], tag=f"{name}f_yf{d}")
        war5[("z", db)] = yfo
        war5[("s1", db)] = yfo
        yf_done.append(yfo)
    sl.free("a_yac", "a_fxc0", "a_fxc1", "a_fz0", "a_fz1", "a_fs0", "a_fs1",
            "a_D")
    p.barrier()
    if UPTO <= 5:
        return []

    wout = sl.take(16 * 512, BF16, name="a_wo").rearrange("p (k c) -> p k c", k=16)
    wo_ld = p.dma("gpsimd", wout, dr["w0_Wout"].rearrange("(k p) c -> p k c", p=128),
                  tag=f"{name}wo_ld")
    otile = [sl.take(1024, F32, name="a_ot0")] * 2
    pO = [ps["A"], ps["B"]]
    war6 = {}

    def src_fn(tt):
        pb = tt % 2
        ob = tt % 2
        mms = []
        for half in range(2):
            for k in range(16):
                mm = p.op("tensor", lambda E, o=pO[pb][:, half * 512:(half + 1) * 512],
                          w=yf[:, k, tt * 128:(tt + 1) * 128],
                          x=wout[:, k, half * 512:(half + 1) * 512],
                          s=(k == 0), t=(k == 15):
                          E.matmul(o, w, x, start=s, stop=t),
                          [wo_ld, yf_done[k] if tt == 0 else None,
                           war6.get(("p", pb))],
                          tag=f"{name}o{tt}_{half}_{k}")
                mms.append(mm)
        war_o = readers6[tt - 2] if tt >= 2 else None
        ev = p.op("vector", lambda E, o=otile[ob], i=pO[pb]:
                  E.tensor_copy(o, i), [mms, war_o],
                  tag=f"{name}o_ev{tt}")
        war6[("p", pb)] = ev
        return otile[ob], [ev]

    readers6 = []
    stores = ln_res_add_store(p, nc, dr, sl, ps, src_fn, x_in_hbm, x_out_hbm,
                              readers_out=readers6, name=name)
    dr["_defer"] += ["a_yf", "a_wo", "a_ot0"]
    return stores



SLAB_WORDS = 53184


def _declare_common(nc, dr, H, stages):
    dr["eye32"] = nc.dram_tensor("c_eye32", [128, 128], F32,
                                 kind="ExternalInput").ap()
    for k in ("c_sel16", "c_sel32", "c_ones1", "c_cons"):
        dr[k] = nc.dram_tensor(k, list(H[k].shape), F32,
                               kind="ExternalInput").ap()
    stage_w = {
        "l0_attn": ["w0_Win", "w0_dwT", "w0_cb", "w0_pw", "w0_Wx", "w0_Wdt",
                    "w0_bdt", "w0_Alog", "w0_D", "w0_Wout"],
        "l0_ffn": ["wf0_gwT", "wf0_gb", "wf0_We1", "wf0_be1", "wf0_We3",
                   "wf0_be3", "wf0_We2", "wf0_be2s", "wf0_ws1", "wf0_bs1",
                   "wf0_ws3", "wf0_bs3", "wf0_ws2"],
        "l1_attn": ["w1_A", "w1_WdT", "w1_bd", "w1_WbT", "w1_bb", "w1_WcT",
                    "w1_bc"],
        "l1_ffn": ["wf1_rwT", "wf1_eb", "wf1_gupT", "wf1_downT", "wf1_sgwT",
                   "wf1_suwT", "wf1_sdwT"],
    }
    used = ["c_eye32", "c_sel16", "c_sel32", "c_ones1", "c_cons"]
    for s in stages:
        for w in stage_w[s]:
            dr[w] = nc.dram_tensor(w, list(H[w].shape), F32,
                                   kind="ExternalInput").ap()
            used.append(w)
    return used


def _declare_scratch(dr, stages):
    n = dr.nc
    if "l0_attn" in stages:
        dr.scratch("s_xin", [16, 128, T], F16)
        dr.scratch("s_z", [16, 128, T], F16)
        dr.scratch("s_xc", [16, 128, T], BF16)
        dr.scratch("s_dl", [16, 128, T], F16)
    if "l1_attn" in stages:
        dr.scratch("s_dl1", [8, 128, T], F32)


STAGE_EMITTERS = {}


def build_program(stages, in_name, out_name, dbg_external=False):
    import contextlib
    nc = bass.Bass("TRN2", target_bir_lowering=False)
    dr = Drams(nc)
    dr.dbg_external = dbg_external
    H = build_program.H
    used = _declare_common(nc, dr, H, stages)
    x_io = {}
    x_io[in_name] = dr.inp(in_name, np.zeros((T, D), np.float32))
    x_io[out_name] = dr.out(out_name, (T, D))
    chain = [in_name]
    for i in range(len(stages) - 1):
        nm = f"xmid{i}"
        dr.scratch(nm, [T, D], F32)
        chain.append(nm)
    chain.append(out_name)
    _declare_scratch(dr, stages)

    build_program.last_inputs = None
    es = contextlib.ExitStack()
    with es:
        sl = Slab(nc, "slab", SLAB_WORDS)
        pA = nc.alloc_psum_tensor("pA", [128, 1024], F32).ap()
        pB = nc.alloc_psum_tensor("pB", [128, 1024], F32).ap()
        pC = nc.alloc_psum_tensor("pC", [128, 512], F32).ap()
        pD = nc.alloc_psum_tensor("pD", [128, 512], F32).ap()
        pE = nc.alloc_psum_tensor("pE", [128, 512], F32).ap()
        pF = nc.alloc_psum_tensor("pF", [128, 512], F32).ap()
        ps = {"A": pA, "B": pB, "C": pC, "D": pD, "E": pE, "F": pF}
        p = Prog()
        eyesb = sl.take(128, F32, name="g_eye")
        eye_ld = p.dma("sync", eyesb, dr["eye32"], tag="eye_ld")
        dr["sb_eye"] = eyesb
        dr["sb_eye_ld"] = eye_ld
        consb = sl.take(8, F32, name="g_cons")
        cons_ld = p.dma("sync", consb, dr["c_cons"], tag="cons_ld")
        dr["sb_cons"] = consb
        dr["sb_cons_ld"] = cons_ld
        prev_stores = None
        for i, s in enumerate(stages):
            x_in = dr[chain[i]]
            x_out = dr[chain[i + 1]]
            prev_stores = STAGE_EMITTERS[s](p, nc, dr, sl, ps, x_in, x_out,
                                            name=s[1] + s[3] + "_",
                                            prev_stores=prev_stores)
        p.emit(nc, es)
    return nc, used + [in_name]


STAGE_EMITTERS["l0_attn"] = emit_l0_attn


def _in_map_for(H, x_arr, in_name, used):
    m = {}
    for k in used:
        if k == in_name:
            m[k] = np.ascontiguousarray(x_arr, dtype=np.float32)
        elif k == "c_eye32":
            m[k] = np.eye(128, dtype=np.float32)
        else:
            m[k] = np.ascontiguousarray(H[k], dtype=np.float32)
    return m


FUSED = True


def kernel(**inputs):
    H = prep_weights(inputs)
    build_program.H = H
    stages = ["l0_attn", "l0_ffn", "l1_attn", "l1_ffn"]
    x = np.ascontiguousarray(np.asarray(inputs["L0_attn_x"], np.float32))
    cur = [x[c] for c in range(8)]
    if FUSED:
        nc, used = build_program(stages, "x0", "x1")
        in_maps = [_in_map_for(H, cur[c], "x0", used) for c in range(8)]
        res = run_bass_kernel_spmd(nc, in_maps, core_ids=list(range(8)))
        cur = [np.asarray(r["x1"], np.float32) for r in res.results]
    else:
        for s in stages:
            nc, used = build_program([s], "x0", "x1")
            in_maps = [_in_map_for(H, cur[c], "x0", used) for c in range(8)]
            res = run_bass_kernel_spmd(nc, in_maps, core_ids=list(range(8)))
            cur = [np.asarray(r["x1"], np.float32) for r in res.results]
    return np.stack(cur, 0)



def emit_l1_attn(p, nc, dr, sl, ps, x_in_hbm, x_out_hbm, name="a1_",
                 prev_stores=None):
    NCH = 4
    CW = 512
    NS = 16
    dl_hbm = dr["s_dl1"]

    xfm = sl.take(8 * 1024, BF16, name="b_xfm", side="high")
    wdb = sl.take(8 * 512, BF16, name="b_wd").rearrange("p (k c) -> p k c", k=8)
    wbc = sl.take(2 * 64, BF16, name="b_wbc").rearrange("p (w c) -> p w c", w=2)
    bdsb = sl.take(8, F32, name="b_bd").rearrange("p (d c) -> p d c", d=8)
    bbc = sl.take(2, F32, name="b_bbc")
    bm1 = sl.take(1024, BF16, name="b_bm")
    cm1 = sl.take(1024, BF16, name="b_cm")
    wd_ld = p.dma("gpsimd", wdb, dr["w1_WdT"].rearrange("(k p) c -> p k c", p=128),
                  tag=f"{name}wd_ld")
    wb_ld = p.dma("gpsimd", wbc[:, 0:1, :].squeeze(1)
                  .rearrange("p (k c) -> p k c", k=8),
                  dr["w1_WbT"].rearrange("(k p) c -> p k c", p=128),
                  tag=f"{name}wb_ld")
    wc_ld = p.dma("gpsimd", wbc[:, 1:2, :].squeeze(1)
                  .rearrange("p (k c) -> p k c", k=8),
                  dr["w1_WcT"].rearrange("(k p) c -> p k c", p=128),
                  tag=f"{name}wc_ld")
    bd_ld = p.dma("sync", bdsb, dr["w1_bd"].rearrange("p (d c) -> p d c", d=8),
                  tag=f"{name}bd_ld")
    bb_ld = p.dma("sync", bbc[0:16, 0:1], dr["w1_bb"], tag=f"{name}bb_ld")
    bc_ld = p.dma("sync", bbc[0:16, 1:2], dr["w1_bc"], tag=f"{name}bc_ld")
    ln = emit_ln_to_fm(p, nc, dr, sl, ps, x_in_hbm, xfm, name=name,
                       tile_deps=prev_stores)
    p.barrier()
    for nmf in dr.pop("_defer", []):
        sl.free(nmf)
    xfm3 = xfm.rearrange("p (f t) -> p f t", f=8)

    dstg = [sl.take(512, F32, name=f"b_ds{i}") for i in range(2)]
    tmp32 = [sl.take(512, F32, name=f"b_tm{i}") for i in range(2)]

    pBC = ps["E"][0:16, :]
    war_pbc = None
    bc_done = []
    for w_i, (dst, bcol) in enumerate([(bm1, bbc[0:16, 0:1]),
                                       (cm1, bbc[0:16, 1:2])]):
        for ch in range(NCH):
            mms = []
            for k in range(8):
                mm = p.op("tensor", lambda E, o=pBC,
                          w=wbc[:, w_i, k * 16:(k + 1) * 16],
                          x=xfm3[:, k, ch * CW:(ch + 1) * CW],
                          s=(k == 0), t=(k == 7):
                          E.matmul(o, w, x, start=s, stop=t),
                          [wb_ld, wc_ld, war_pbc], tag=f"{name}bc{w_i}_{ch}_{k}")
                mms.append(mm)
            ev = p.op("scalar", lambda E, o=dst[0:16, ch * CW:(ch + 1) * CW],
                      i=pBC, b=bcol: E.activation(o, i, AF.Identity, bias=b),
                      [mms[-1], bb_ld, bc_ld], tag=f"{name}bcev{w_i}_{ch}")
            war_pbc = ev
            bc_done.append(ev)
    pDl = [ps["C"], ps["D"]]
    war_pd = {0: None, 1: None}
    war_tmp = {0: None, 1: None}
    war_dst = {0: None, 1: None}
    for m in range(8):
        for ch in range(NCH):
            pb = (m * NCH + ch) % 2
            mms = []
            for k in range(8):
                mm = p.op("tensor", lambda E, o=pDl[pb],
                          w=wdb[:, k, m * 128:(m + 1) * 128],
                          x=xfm3[:, k, ch * CW:(ch + 1) * CW],
                          s=(k == 0), t=(k == 7):
                          E.matmul(o, w, x, start=s, stop=t),
                          [wd_ld, war_pd[pb]] + (ln["evac"] if m == 0 else []),
                          tag=f"{name}dl{m}_{ch}_{k}")
                mms.append(mm)
            sb = ch % 2
            e1 = p.op("scalar", lambda E, o=tmp32[sb], i=pDl[pb],
                      b=bdsb[:, m, 0:1]: E.activation(o, i, AF.Exp, bias=b),
                      [mms[-1], war_tmp[sb], bd_ld], tag=f"{name}de{m}_{ch}")
            war_pd[pb] = e1
            e2 = p.op("scalar", lambda E, o=dstg[sb], i=tmp32[sb],
                      b=dr["sb_cons"][:, 2:3]: E.activation(o, i, AF.Ln, bias=b),
                      [e1, war_dst[sb], dr["sb_cons_ld"]], tag=f"{name}dn{m}_{ch}")
            war_tmp[sb] = e2
            st = p.dma("sync", dl_hbm[m, :, ch * CW:(ch + 1) * CW], dstg[sb],
                       deps=[e2], tag=f"{name}d_st{m}_{ch}")
            war_dst[sb] = st
    sl.free("b_wd", "b_ds0", "b_ds1", "b_tm0", "b_tm1", "b_wbc", "b_bd")
    p.barrier()

    NS = 16
    selb = sl.take(1024, BF16, name="b_sel")
    Asb = sl.take(8 * 16, F32, name="b_A").rearrange("p (d c) -> p d c", d=8)
    a_ld = p.dma("sync", Asb, dr["w1_A"].rearrange("p (d c) -> p d c", d=8),
                 tag=f"{name}a_ld")
    sel_ld = p.dma("gpsimd", selb[0:16, :], dr["c_sel16"], tag=f"{name}sel_ld")
    bca = sl.take(16 * 1024, F16, name="b_bca").rearrange("p (r c) -> p r c", r=16)
    yac = sl.take(8 * 1024, F16, name="b_yac").rearrange("p (d c) -> p d c", d=8)
    dl_b = [sl.take(2048, F32, name=f"b_sdl{i}") for i in range(2)]
    u_b = [sl.take(1024, F16, name=f"b_su{i}") for i in range(2)]
    a_b = [sl.take(2048, F32, name=f"b_sa{i}") for i in range(2)]
    b_b = [sl.take(1024, F16, name=f"b_sb{i}") for i in range(2)]
    h_b = [sl.take(1024, F16, name=f"b_sh{i}") for i in range(2)]
    hc_b = [sl.take(1024, F16, name=f"b_shc{i}") for i in range(2)]
    pB2 = ps["C"]
    war = {}
    war_pb2 = None
    yac_last = {}
    for half in range(2):
        bc_ready = []
        for j in range(16):
            r = half * 8 + (j % 8)
            src_sb = bm1 if j < 8 else cm1
            evs = []
            for q in range(4):
                mm = p.op("tensor", lambda E, o=pB2,
                          w=selb[0:16, r * 128:(r + 1) * 128],
                          x=src_sb[0:16, q * 512:(q + 1) * 512]:
                          E.matmul(o, w, x, start=True, stop=True),
                          [sel_ld, bc_done, war_pb2],
                          tag=f"{name}bb{half}_{j}_{q}")
                ev = p.op("scalar",
                          lambda E, o=bca[:, j, q * 512:(q + 1) * 512],
                          i=pB2: E.activation(o, i, AF.Identity),
                          [mm, war.get(("bca", j))],
                          tag=f"{name}bbev{half}_{j}_{q}")
                war_pb2 = ev
                evs.append(ev)
            bc_ready.append(evs)
        for d in range(8):
            db = d % 2
            ld_dl = p.dma("sync", dl_b[db], dl_hbm[d],
                          deps=[war.get(("dl", db))], tag=f"{name}s_dl{half}_{d}")
            uop = p.op("gpsimd", lambda E, o=u_b[db], a=dl_b[db],
                       b=xfm3[:, d, :]: E.tensor_tensor(o, a, b, ALU.mult),
                       [ld_dl, war.get(("u", db))], tag=f"{name}s_u{half}_{d}")
            dl_readers = [uop]
            u_readers = []
            yprev = yac_last.get(d)
            bca_readers = {j: [] for j in range(16)}
            pend = None
            for j in range(8):
                n = half * 8 + j
                nb = j % 2
                a_o = p.op("scalar", lambda E, o=a_b[nb], i=dl_b[db],
                           s=Asb[:, d, n:n + 1]:
                           E.activation(o, i, AF.Exp, scale=s),
                           [ld_dl, a_ld, war.get(("a", nb))],
                           tag=f"{name}s_a{half}_{d}_{j}")
                dl_readers.append(a_o)
                b_o = p.op("vector", lambda E, o=b_b[nb], a=u_b[db],
                           b=bca[:, j, :]: E.tensor_tensor(o, a, b, ALU.mult),
                           [uop, bc_ready[j], war.get(("b", nb))],
                           tag=f"{name}s_b{half}_{d}_{j}")
                u_readers.append(b_o)
                bca_readers[j].append(b_o)
                h_o = p.op("vector", lambda E, o=h_b[nb], a=a_b[nb], b=b_b[nb]:
                           E.tensor_tensor_scan(o, a, b, 0.0, ALU.mult, ALU.add),
                           [a_o, b_o, war.get(("h", nb))],
                           tag=f"{name}s_h{half}_{d}_{j}")
                war[("a", nb)] = h_o
                war[("b", nb)] = h_o
                hc_o = p.op("vector" if j % 2 == 0 else "gpsimd",
                            lambda E, o=hc_b[nb], a=h_b[nb],
                            b=bca[:, 8 + j, :]:
                            E.tensor_tensor(o, a, b, ALU.mult),
                            [h_o, bc_ready[8 + j], war.get(("hc", nb))],
                            tag=f"{name}s_hc{half}_{d}_{j}")
                bca_readers[8 + j].append(hc_o)
                war[("h", nb)] = hc_o
                if pend is not None:
                    po, pap, pnb, pj = pend
                    yprev = _scan_yadd(p, yac[:, d, :], pap, po, yprev, war,
                                       ("hc", pnb), f"{name}s_y{half}_{d}_{pj}")
                pend = (hc_o, hc_b[nb], nb, j)
            po, pap, pnb, pj = pend
            yprev = _scan_yadd(p, yac[:, d, :], pap, po, yprev, war,
                               ("hc", pnb), f"{name}s_y{half}_{d}_{pj}")
            war[("dl", db)] = dl_readers
            war[("u", db)] = u_readers
            yac_last[d] = yprev
            if d == 7:
                for j in range(16):
                    war[("bca", j)] = bca_readers[j]
    yac_done = [yac_last[d] for d in range(8)]
    sl.free("b_xfm", "b_bca", "b_sel", "b_sdl0", "b_sdl1", "b_su0", "b_su1",
            "b_sa0", "b_sa1", "b_sb0", "b_sb1", "b_sh0", "b_sh1", "b_shc0",
            "b_shc1", "b_bm", "b_cm", "b_bbc", "b_A")
    p.barrier()

    ycp = sl.take(8 * 2048, F32, name="b_ycp").rearrange("p (d c) -> p d c", d=8)
    cps = []
    for d in range(8):
        c = p.op("vector", lambda E, o=ycp[:, d, :], i=yac[:, d, :]:
                 E.tensor_copy(o, i), [yac_done[d]], tag=f"{name}ycp{d}")
        cps.append(c)
    otile = [sl.take(1024, F32, name="b_ot0")] * 2
    pO = [ps["A"], ps["B"]]
    war6 = {}
    readers6 = []

    def src_fn(tt):
        pb = tt % 2
        trs = []
        for d in range(8):
            t = p.op("tensor", lambda E, o=pO[pb][:, d * 128:(d + 1) * 128],
                     i=ycp[:, d, tt * 128:(tt + 1) * 128], idn=dr["sb_eye"]:
                     E.transpose(o, i, idn),
                     [cps[d] if tt < 2 else None, war6.get(("p", pb)),
                      dr["sb_eye_ld"]],
                     tag=f"{name}ytr{tt}_{d}")
            trs.append(t)
        war_o = readers6[tt - 2] if tt >= 2 else None
        ev = p.op("vector", lambda E, o=otile[tt % 2], i=pO[pb]:
                  E.tensor_copy(o, i), [trs, war_o], tag=f"{name}yev{tt}")
        war6[("p", pb)] = ev
        return otile[tt % 2], [ev]

    stores = ln_res_add_store(p, nc, dr, sl, ps, src_fn, x_in_hbm, x_out_hbm,
                              readers_out=readers6, name=name)
    dr["_defer"] += ["b_yac", "b_ycp", "b_ot0"]
    return stores


STAGE_EMITTERS["l1_attn"] = emit_l1_attn



def emit_l0_ffn(p, nc, dr, sl, ps, x_in_hbm, x_out_hbm, name="f0_",
                prev_stores=None):
    E = 16
    NCH = 4
    CW = 512
    GE = 2

    xfm = sl.take(8 * 1024, BF16, name="c_xfm", side="high")
    xlo = sl.take(8 * 1024, BF16, name="c_xlo", side="high")
    ln = emit_ln_to_fm(p, nc, dr, sl, ps, x_in_hbm, xfm, lo=xlo, name=name,
                       tile_deps=prev_stores)
    p.barrier()
    for nmf in dr.pop("_defer", []):
        sl.free(nmf)
    xfm3 = xfm.rearrange("p (f t) -> p f t", f=8)
    xlo3 = xlo.rearrange("p (f t) -> p f t", f=8)

    gw32 = sl.take(8 * 16, F32, name="c_gw32").rearrange("p (k c) -> p k c", k=8)
    gwh = sl.take(8 * 8, BF16, name="c_gwh").rearrange("p (k c) -> p k c", k=8)
    gwl = sl.take(8 * 8, BF16, name="c_gwl").rearrange("p (k c) -> p k c", k=8)
    gbsb = sl.take(16, F32, name="c_gb")
    sc32 = sl.take(16, F32, name="c_sc")
    orig = sl.take(16, F32, name="c_orig")
    smask = sl.take(16, F32, name="c_smask")
    eqb = sl.take(16, F32, name="c_eq")
    gsum = sl.take(4, F32, name="c_gsum")
    keep = sl.take(4, F32, name="c_keep")
    gtmp = sl.take(4, F32, name="c_gtmp")
    colx = sl.take(8, F32, name="c_colx")
    comb_tm = sl.take(256, F32, name="c_comb")
    combfm = sl.take(1024, BF16, name="c_combfm")

    gw_ld = p.dma("sync", gw32, dr["wf0_gwT"].rearrange("(k p) c -> p k c", p=128),
                  tag=f"{name}gw_ld")
    gb_ld = p.dma("sync", gbsb, dr["wf0_gb"], tag=f"{name}gb_ld")
    ones_ld = p.dma("gpsimd", combfm[16:17, :], dr["c_ones1"],
                    tag=f"{name}ones_ld")
    ghi = p.op("vector", lambda E: E.tensor_copy(gwh, gw32), [gw_ld],
               tag=f"{name}gwh")
    glo = p.op("vector", lambda E: E.tensor_tensor(gwl, gw32, gwh, ALU.subtract),
               [ghi], tag=f"{name}gwl")
    pG = ps["C"][0:128, :]
    war_pg = None
    war_ct = None
    comb_done = []
    for tt in range(16):
        mms = []
        for ph, (xs, ws) in enumerate([(xfm3, gwh), (xlo3, gwh), (xfm3, gwl)]):
            for k in range(8):
                mm = p.op("tensor", lambda E, o=pG[:, 0:16],
                          w=xs[:, k, tt * 128:(tt + 1) * 128],
                          x=ws[:, k, :], s=(ph == 0 and k == 0),
                          t=(ph == 2 and k == 7):
                          E.matmul(o, w, x, start=s, stop=t),
                          [ghi, glo, war_pg, ln["evac"] if tt == 0 else None],
                          tag=f"{name}g{tt}_{ph}_{k}")
                mms.append(mm)
        deps = [mms[-1], war_ct]
        ev = p.op("vector", lambda E: E.tensor_copy(sc32, pG[:, 0:16]), deps,
                  tag=f"{name}gev{tt}")
        war_pg = ev
        negm = colx[:, 0:1]
        ssum = colx[:, 1:2]
        rcp = colx[:, 2:3]
        o1 = p.op("vector", lambda E: E.tensor_reduce(negm, sc32, AX.X, ALU.max,
                                                      negate=True),
                  [ev], tag=f"{name}gx{tt}_m")
        o2 = p.op("scalar", lambda E: E.activation(orig, sc32, AF.Exp, bias=negm,
                                                   accum_out=ssum),
                  [o1], tag=f"{name}gx{tt}_e")
        o3 = p.op("vector", lambda E: E.reciprocal(rcp, ssum), [o2],
                  tag=f"{name}gx{tt}_r")
        o4 = p.op("vector", lambda E: E.tensor_scalar(orig, orig, rcp, None,
                                                      ALU.mult),
                  [o3, o2], tag=f"{name}gx{tt}_n")
        o5 = p.op("vector", lambda E: E.tensor_tensor(smask, orig, gbsb, ALU.add),
                  [o4, gb_ld], tag=f"{name}gx{tt}_s")
        prev = o5
        for g in range(4):
            sg = smask[:, g * 4:(g + 1) * 4]
            m1 = gtmp[:, g:g + 1]
            a1 = p.op("vector", lambda E, sg=sg, m1=m1:
                      E.tensor_reduce(m1, sg, AX.X, ALU.max), [prev],
                      tag=f"{name}gx{tt}_g{g}a")
            a2 = p.op("vector", lambda E, sg=sg, m1=m1:
                      E.scalar_tensor_tensor(eqb[:, 0:4], sg, m1, sg,
                                             ALU.is_equal, ALU.bypass),
                      [a1], tag=f"{name}gx{tt}_g{g}b")
            a3 = p.op("vector", lambda E, sg=sg:
                      E.scalar_tensor_tensor(eqb[:, 4:8], eqb[:, 0:4], NEG, sg,
                                             ALU.mult, ALU.add),
                      [a2], tag=f"{name}gx{tt}_g{g}c")
            a4 = p.op("vector", lambda E, g=g:
                      E.tensor_reduce(eqb[:, 8:9], eqb[:, 4:8], AX.X, ALU.max),
                      [a3], tag=f"{name}gx{tt}_g{g}d")
            a5 = p.op("vector", lambda E, g=g, m1=m1:
                      E.tensor_tensor(gsum[:, g:g + 1], m1, eqb[:, 8:9],
                                      ALU.add),
                      [a4], tag=f"{name}gx{tt}_g{g}e")
            prev = a5
        b1 = p.op("vector", lambda E: E.tensor_reduce(colx[:, 3:4], gsum, AX.X,
                                                      ALU.max), [prev],
                  tag=f"{name}gx{tt}_k1")
        b2 = p.op("vector", lambda E: E.scalar_tensor_tensor(
            keep, gsum, colx[:, 3:4], gsum, ALU.is_equal, ALU.bypass),
            [b1], tag=f"{name}gx{tt}_k2")
        b3 = p.op("vector", lambda E: E.scalar_tensor_tensor(
            gtmp, keep, NEG, gsum, ALU.mult, ALU.add),
            [b2], tag=f"{name}gx{tt}_k3")
        b4 = p.op("vector", lambda E: E.tensor_reduce(colx[:, 4:5], gtmp, AX.X,
                                                      ALU.max), [b3],
                  tag=f"{name}gx{tt}_k4")
        b5 = p.op("vector", lambda E: E.scalar_tensor_tensor(
            gtmp, gtmp, colx[:, 4:5], gtmp, ALU.is_equal, ALU.bypass),
            [b4], tag=f"{name}gx{tt}_k5")
        b6 = p.op("vector", lambda E: E.tensor_tensor(keep, keep, gtmp, ALU.add),
                  [b5], tag=f"{name}gx{tt}_k6")
        b7 = p.op("vector", lambda E: E.tensor_scalar(gtmp, keep, -NEG, NEG,
                                                      ALU.mult, ALU.add),
                  [b6], tag=f"{name}gx{tt}_k7")
        prev = b7
        for g in range(4):
            c1 = p.op("vector", lambda E, g=g: E.tensor_scalar(
                smask[:, g * 4:(g + 1) * 4], smask[:, g * 4:(g + 1) * 4],
                gtmp[:, g:g + 1], None, ALU.add),
                [prev], tag=f"{name}gx{tt}_p{g}")
            prev = c1
        combv = comb_tm[:, tt * 16:(tt + 1) * 16]
        for it in range(2):
            d1 = p.op("vector", lambda E: E.tensor_reduce(colx[:, 5:6], smask,
                                                          AX.X, ALU.max),
                      [prev], tag=f"{name}gx{tt}_t{it}a")
            d2 = p.op("vector", lambda E: E.scalar_tensor_tensor(
                eqb, smask, colx[:, 5:6], smask, ALU.is_equal, ALU.bypass),
                [d1], tag=f"{name}gx{tt}_t{it}b")
            if it == 0:
                d3 = p.op("vector", lambda E, cv=combv: E.tensor_copy(cv, eqb),
                          [d2], tag=f"{name}gx{tt}_t{it}c")
            else:
                d3 = p.op("vector", lambda E, cv=combv: E.tensor_tensor(
                    cv, cv, eqb, ALU.add), [d2], tag=f"{name}gx{tt}_t{it}c")
            d4 = p.op("vector", lambda E: E.scalar_tensor_tensor(
                smask, eqb, NEG, smask, ALU.mult, ALU.add),
                [d3, d2], tag=f"{name}gx{tt}_t{it}d")
            prev = d4
        d5 = p.op("vector", lambda E, cv=combv: E.tensor_tensor(
            cv, cv, orig, ALU.mult), [prev], tag=f"{name}gx{tt}_cm")
        war_ct = d5
        comb_done.append(d5)
    pT = ps["D"][0:16, 0:128]
    war_pt = None
    combfm_done = [ones_ld]
    ctmp = sl.take(64, F32, name="c_ctf")
    for tt in range(16):
        t1 = p.op("vector", lambda E, tt=tt: E.tensor_copy(
            ctmp[:, 0:16], comb_tm[:, tt * 16:(tt + 1) * 16]),
            [comb_done[tt], war_pt], tag=f"{name}ctc{tt}")
        t2 = p.op("tensor", lambda E, tt=tt: E.transpose(
            pT, ctmp[:, 0:16], dr["sb_eye"]),
            [t1, dr["sb_eye_ld"], war_pt], tag=f"{name}ctt{tt}")
        t3 = p.op("scalar", lambda E, tt=tt: E.activation(
            combfm[0:16, tt * 128:(tt + 1) * 128], pT, AF.Identity),
            [t2], tag=f"{name}cte{tt}")
        war_pt = t3
        combfm_done.append(t3)
    sl.free("c_xlo", "c_gw32", "c_gwh", "c_gwl", "c_gb", "c_sc", "c_orig",
            "c_smask", "c_eq", "c_gsum", "c_keep", "c_gtmp", "c_colx", "c_ctf")
    p.barrier()

    yacc = sl.take(16 * 1024, F32, name="c_yacc").rearrange(
        "p (t c) -> p t c", t=16)
    ws1 = sl.take(8 * 512, BF16, name="c_ws1").rearrange("p (k c) -> p k c", k=8)
    ws3 = sl.take(8 * 512, BF16, name="c_ws3").rearrange("p (k c) -> p k c", k=8)
    bs1 = sl.take(8, F32, name="c_bs1").rearrange("p (d c) -> p d c", d=8)
    bs3 = sl.take(8, F32, name="c_bs3").rearrange("p (d c) -> p d c", d=8)
    z1 = sl.take(8 * 1024, BF16, name="c_z1").rearrange("p (m c) -> p m c", m=8)
    z3 = sl.take(8 * 1024, BF16, name="c_z3").rearrange("p (m c) -> p m c", m=8)
    ws1_ld = p.dma("gpsimd", ws1, dr["wf0_ws1"].rearrange("(k p) c -> p k c", p=128),
                   tag=f"{name}ws1_ld")
    ws3_ld = p.dma("gpsimd", ws3, dr["wf0_ws3"].rearrange("(k p) c -> p k c", p=128),
                   tag=f"{name}ws3_ld")
    bs1_ld = p.dma("sync", bs1, dr["wf0_bs1"].rearrange("p (d c) -> p d c", d=8),
                   tag=f"{name}bs1_ld")
    bs3_ld = p.dma("sync", bs3, dr["wf0_bs3"].rearrange("p (d c) -> p d c", d=8),
                   tag=f"{name}bs3_ld")
    pZ = [ps["C"], ps["D"]]
    war_pz = {0: None, 1: None}
    z_done = []
    for src_i, (wv, wld, bv, bld, zv, act) in enumerate([
            (ws1, ws1_ld, bs1, bs1_ld, z1, AF.Silu),
            (ws3, ws3_ld, bs3, bs3_ld, z3, AF.Identity)]):
        for m in range(8):
            for ch in range(NCH):
                pb = (m * NCH + ch) % 2
                mms = []
                for k in range(8):
                    mm = p.op("tensor", lambda E, o=pZ[pb],
                              w=wv[:, k, m * 128:(m + 1) * 128],
                              x=xfm3[:, k, ch * CW:(ch + 1) * CW],
                              s=(k == 0), t=(k == 7):
                              E.matmul(o, w, x, start=s, stop=t),
                              [wld, war_pz[pb]], tag=f"{name}z{src_i}_{m}_{ch}_{k}")
                    mms.append(mm)
                ev = p.op("scalar", lambda E, o=zv[:, m, ch * CW:(ch + 1) * CW],
                          i=pZ[pb], b=bv[:, m, 0:1], a=act:
                          E.activation(o, i, a, bias=b),
                          [mms[-1], bld], tag=f"{name}zev{src_i}_{m}_{ch}")
                war_pz[pb] = ev
                z_done.append(ev)
    zm_done = []
    for m in range(8):
        zz = p.op("vector", lambda E, m=m: E.tensor_tensor(
            z1[:, m, :], z1[:, m, :], z3[:, m, :], ALU.mult),
            [z_done], tag=f"{name}zm{m}")
        zm_done.append(zz)
    sl.free("c_ws1", "c_ws3")
    ws2 = sl.take(8 * 512, BF16, name="c_ws2").rearrange("p (k c) -> p k c", k=8)
    be2s = sl.take(512, BF16, name="c_be2s")
    ws2_ld = p.dma("gpsimd", ws2, dr["wf0_ws2"].rearrange("(k p) c -> p k c", p=128),
                   deps=[z_done], tag=f"{name}ws2_ld")
    be2s_ld = p.dma("gpsimd", be2s[0:17, :], dr["wf0_be2s"],
                    deps=[z_done], tag=f"{name}be2s_ld")
    pY = [ps["A"], ps["B"]]
    war_py = {0: None, 1: None}
    yinit = []
    for tt in range(16):
        pb = tt % 2
        mms = []
        for half in range(2):
            po = pY[pb][:, half * 512:(half + 1) * 512]
            for k in range(8):
                mm = p.op("tensor", lambda E, o=po,
                          w=z1[:, k, tt * 128:(tt + 1) * 128],
                          x=ws2[:, k, half * 512:(half + 1) * 512],
                          s=(k == 0), t=False:
                          E.matmul(o, w, x, start=s, stop=t),
                          [ws2_ld, zm_done[k] if tt == 0 else None,
                           war_py[pb]], tag=f"{name}y{tt}_{half}_{k}")
                mms.append(mm)
            mm = p.op("tensor", lambda E, o=po,
                      w=combfm[0:17, tt * 128:(tt + 1) * 128],
                      x=be2s[0:17, half * 512:(half + 1) * 512]:
                      E.matmul(o, w, x, start=False, stop=True),
                      [be2s_ld, combfm_done], tag=f"{name}yb{tt}_{half}")
            mms.append(mm)
        ev = p.op("vector", lambda E, tt=tt, pb=pb: E.tensor_copy(
            yacc[:, tt, :], pY[pb]), [mms], tag=f"{name}yev{tt}")
        war_py[pb] = ev
        yinit.append(ev)
    sl.free("c_bs1", "c_bs3", "c_z1", "c_z3", "c_ws2", "c_be2s")
    p.barrier()

    we13 = [sl.take(2 * 2048, BF16, name=f"c_we13_{i}") for i in range(2)]
    we2 = [sl.take(4 * 512, BF16, name=f"c_we2_{i}") for i in range(3)]
    hm = [sl.take(4 * 1024, BF16, name=f"c_hm_{i}").rearrange(
        "p (k c) -> p k c", k=4) for i in range(2)]
    cbb = [sl.take(1024, BF16, name=f"c_cb_{i}") for i in range(2)]
    be1a = sl.take(16 * 4, F32, name="c_be1").rearrange("p (e c) -> p e c", e=16)
    be3a = sl.take(16 * 4, F32, name="c_be3").rearrange("p (e c) -> p e c", e=16)
    s1stg = [sl.take(256, BF16, name=f"c_s1_{i}") for i in range(2)]
    s3stg = [sl.take(256, BF16, name=f"c_s3_{i}") for i in range(2)]
    selb = sl.take(1024, BF16, name="c_selb")
    be1_ld = p.dma("sync", be1a, dr["wf0_be1"].rearrange("e p c -> p e c"),
                   tag=f"{name}be1_ld")
    be3_ld = p.dma("sync", be3a, dr["wf0_be3"].rearrange("e p c -> p e c"),
                   tag=f"{name}be3_ld")
    sel_ld = p.dma("gpsimd", selb[0:16, :], dr["c_sel16"], tag=f"{name}selb_ld")

    war_w13 = {0: None, 1: None}
    war_w2 = {0: None, 1: None, 2: None}
    war_hm = {0: None, 1: None}
    war_cb = {0: None, 1: None}
    war_s1 = {0: None, 1: None}
    war_s3 = {0: None, 1: None}
    war_pc = {0: None, 1: None, 2: None}
    war_pb = None
    pH = [ps["D"], ps["E"], ps["F"]]
    pBC2 = ps["C"]
    yfin = list(yinit)
    w2_lds = {}
    hm_done_by_e = {}
    for e in range(E):
        wb = e % 2
        w2b = e % 3
        w13v = we13[wb].rearrange("p (w k c) -> p w k c", w=2, k=8)
        w13_ld = p.dma("gpsimd", w13v[:, 0, :, :],
                       dr["wf0_We1"][e].rearrange("(k p) c -> p k c", p=128),
                       deps=[war_w13[wb]], tag=f"{name}we1_{e}")
        w13_ld2 = p.dma("gpsimd", w13v[:, 1, :, :],
                        dr["wf0_We3"][e].rearrange("(k p) c -> p k c", p=128),
                        deps=[war_w13[wb]], tag=f"{name}we3_{e}")
        w2_ld = p.dma("gpsimd", we2[w2b].rearrange("p (k c) -> p k c", k=4),
                      dr["wf0_We2"][e].rearrange("(k p) c -> p k c", p=128),
                      deps=[war_w2[w2b]], tag=f"{name}we2_{e}")
        w2_lds[e] = w2_ld
        cb_evs = [None] * 4
        hm_done = []
        for im in range(4):
            for ch in range(NCH):
                if im == 0:
                    q = ch
                    mm = p.op("tensor", lambda E_, o=pBC2, q=q,
                              w=selb[0:16, e * 128:(e + 1) * 128],
                              x=combfm[0:16, q * 512:(q + 1) * 512]:
                              E_.matmul(o, w, x, start=True, stop=True),
                              [sel_ld, combfm_done, war_pb],
                              tag=f"{name}cbm{e}_{q}")
                    ev = p.op("scalar", lambda E_, q=q,
                              o=cbb[wb][:, q * 512:(q + 1) * 512],
                              i=pBC2: E_.activation(o, i, AF.Identity),
                              [mm, war_cb[wb]], tag=f"{name}cbe{e}_{q}")
                    war_pb = ev
                    cb_evs[q] = ev
                hs = 2 * (im * NCH + ch)
                hb1 = hs % 3
                hb3 = (hs + 1) % 3
                sb = (im * NCH + ch) % 2
                mm1s = []
                for k in range(8):
                    mm = p.op("tensor", lambda E_, o=pH[hb1],
                              w=w13v[:, 0, k, im * 128:(im + 1) * 128],
                              x=xfm3[:, k, ch * CW:(ch + 1) * CW],
                              s=(k == 0), t=(k == 7):
                              E_.matmul(o, w, x, start=s, stop=t),
                              [w13_ld, war_pc[hb1]], tag=f"{name}h1_{e}_{im}_{ch}_{k}")
                    mm1s.append(mm)
                ev1 = p.op("scalar", lambda E_, o=s1stg[sb], i=pH[hb1],
                           b=be1a[:, e, im:im + 1]:
                           E_.activation(o, i, AF.Silu, bias=b),
                           [mm1s[-1], be1_ld, war_s1[sb]],
                           tag=f"{name}h1e_{e}_{im}_{ch}")
                war_pc[hb1] = ev1
                mm3s = []
                for k in range(8):
                    mm = p.op("tensor", lambda E_, o=pH[hb3],
                              w=w13v[:, 1, k, im * 128:(im + 1) * 128],
                              x=xfm3[:, k, ch * CW:(ch + 1) * CW],
                              s=(k == 0), t=(k == 7):
                              E_.matmul(o, w, x, start=s, stop=t),
                              [w13_ld2, war_pc[hb3]], tag=f"{name}h3_{e}_{im}_{ch}_{k}")
                    mm3s.append(mm)
                ev3 = p.op("scalar", lambda E_, o=s3stg[sb], i=pH[hb3],
                           b=be3a[:, e, im:im + 1]:
                           E_.activation(o, i, AF.Identity, bias=b),
                           [mm3s[-1], be3_ld, war_s3[sb]],
                           tag=f"{name}h3e_{e}_{im}_{ch}")
                war_pc[hb3] = ev3
                hv = hm[wb][:, im, ch * CW:(ch + 1) * CW]
                m1 = p.op("vector", lambda E_, o=hv, a=s1stg[sb], b=s3stg[sb]:
                          E_.tensor_tensor(o, a, b, ALU.mult),
                          [ev1, ev3, war_hm[wb]], tag=f"{name}hm_{e}_{im}_{ch}")
                war_s1[sb] = m1
                war_s3[sb] = m1
                m2 = p.op("gpsimd", lambda E_, o=hv, a=hv,
                          b=cbb[wb][:, ch * CW:(ch + 1) * CW]:
                          E_.tensor_tensor(o, a, b, ALU.mult),
                          [m1, cb_evs[ch]], tag=f"{name}hs_{e}_{im}_{ch}")
                hm_done.append(m2)
        war_cb[wb] = hm_done
        hm_done_by_e[e] = hm_done
        if e % GE == GE - 1:
            group = [e - 1, e]
            new_yfin = []
            for tt in range(16):
                pb = tt % 2
                mms = []
                for gi, ee in enumerate(group):
                    wbb = ee % 2
                    w2v = we2[ee % 3].rearrange("p (k c) -> p k c", k=4)
                    for half in range(2):
                        po = pY[pb][:, half * 512:(half + 1) * 512]
                        for k in range(4):
                            mm = p.op("tensor", lambda E_, o=po,
                                      w=hm[wbb][:, k, tt * 128:(tt + 1) * 128],
                                      x=w2v[:, k, half * 512:(half + 1) * 512],
                                      s=(gi == 0 and k == 0),
                                      t=(gi == 1 and k == 3):
                                      E_.matmul(o, w, x, start=s, stop=t),
                                      [w2_lds[ee],
                                       hm_done_by_e[ee] if tt == 0 else None,
                                       war_py[pb]],
                                      tag=f"{name}dn{ee}_{tt}_{half}_{k}")
                            mms.append(mm)
                ad = p.op("vector", lambda E_, tt=tt, pb=pb:
                          E_.tensor_tensor(yacc[:, tt, :], pY[pb],
                                           yacc[:, tt, :], ALU.add),
                          [mms, yfin[tt]], tag=f"{name}ya{e}_{tt}")
                war_py[pb] = ad
                new_yfin.append(ad)
            yfin = new_yfin
            war_w13[0] = hm_done_by_e[e - 1]
            war_w13[1] = hm_done_by_e[e]
            war_hm[0] = new_yfin
            war_hm[1] = new_yfin
            war_w2[(e - 1) % 3] = new_yfin
            war_w2[e % 3] = new_yfin
    sl.free("c_we13_0", "c_we13_1", "c_we2_0", "c_we2_1", "c_we2_2", "c_hm_0",
            "c_hm_1", "c_cb_0", "c_cb_1", "c_be1", "c_be3", "c_s1_0", "c_s1_1",
            "c_s3_0", "c_s3_1", "c_selb", "c_xfm", "c_combfm", "c_comb")
    p.barrier()

    def src_fn(tt):
        return yacc[:, tt, :], [yfin[tt]]

    readers6 = []
    stores = ln_res_add_store(p, nc, dr, sl, ps, src_fn, x_in_hbm, x_out_hbm,
                              readers_out=readers6, name=name)
    dr["_defer"] += ["c_yacc"]
    return stores


STAGE_EMITTERS["l0_ffn"] = emit_l0_ffn



def emit_l1_ffn(p, nc, dr, sl, ps, x_in_hbm, x_out_hbm, name="f1_",
                prev_stores=None):
    E = 32
    NCH = 4
    CW = 512
    GE = 2

    xfm = sl.take(8 * 1024, BF16, name="d_xfm", side="high")
    xlo = sl.take(8 * 1024, BF16, name="d_xlo", side="high")
    ln = emit_ln_to_fm(p, nc, dr, sl, ps, x_in_hbm, xfm, lo=xlo, name=name,
                       tile_deps=prev_stores)
    p.barrier()
    for nmf in dr.pop("_defer", []):
        sl.free(nmf)
    xfm3 = xfm.rearrange("p (f t) -> p f t", f=8)
    xlo3 = xlo.rearrange("p (f t) -> p f t", f=8)

    rw32 = sl.take(8 * 32, F32, name="d_rw32").rearrange("p (k c) -> p k c", k=8)
    rwh = sl.take(8 * 16, BF16, name="d_rwh").rearrange("p (k c) -> p k c", k=8)
    rwl = sl.take(8 * 16, BF16, name="d_rwl").rearrange("p (k c) -> p k c", k=8)
    ebsb = sl.take(32, F32, name="d_eb")
    sc32 = sl.take(32, F32, name="d_sc")
    orig = sl.take(32, F32, name="d_orig")
    smask = sl.take(32, F32, name="d_smask")
    eqb = sl.take(32, F32, name="d_eq")
    maskb = sl.take(32, F32, name="d_mask")
    colx = sl.take(8, F32, name="d_colx")
    comb_tm = sl.take(512, F32, name="d_comb")
    combfm = sl.take(1024, BF16, name="d_combfm")

    rw_ld = p.dma("sync", rw32, dr["wf1_rwT"].rearrange("(k p) c -> p k c", p=128),
                  tag=f"{name}rw_ld")
    eb_ld = p.dma("sync", ebsb, dr["wf1_eb"], tag=f"{name}eb_ld")
    rhi = p.op("vector", lambda E: E.tensor_copy(rwh, rw32), [rw_ld],
               tag=f"{name}rwh")
    rlo = p.op("vector", lambda E: E.tensor_tensor(rwl, rw32, rwh, ALU.subtract),
               [rhi], tag=f"{name}rwl")
    pG = ps["C"]
    war_pg = None
    war_ct = None
    comb_done = []
    for tt in range(16):
        mms = []
        for ph, (xs, ws) in enumerate([(xfm3, rwh), (xlo3, rwh), (xfm3, rwl)]):
            for k in range(8):
                mm = p.op("tensor", lambda E, o=pG[:, 0:32],
                          w=xs[:, k, tt * 128:(tt + 1) * 128],
                          x=ws[:, k, :], s=(ph == 0 and k == 0),
                          t=(ph == 2 and k == 7):
                          E.matmul(o, w, x, start=s, stop=t),
                          [rhi, rlo, war_pg, ln["evac"] if tt == 0 else None],
                          tag=f"{name}r{tt}_{ph}_{k}")
                mms.append(mm)
        ev = p.op("vector", lambda E: E.tensor_copy(sc32, pG[:, 0:32]),
                  [mms[-1], war_ct], tag=f"{name}rev{tt}")
        war_pg = ev
        negm = colx[:, 0:1]
        ssum = colx[:, 1:2]
        rcp = colx[:, 2:3]
        o1 = p.op("vector", lambda E: E.tensor_reduce(negm, sc32, AX.X, ALU.max,
                                                      negate=True),
                  [ev], tag=f"{name}rx{tt}_m")
        o2 = p.op("scalar", lambda E: E.activation(orig, sc32, AF.Exp, bias=negm,
                                                   accum_out=ssum),
                  [o1], tag=f"{name}rx{tt}_e")
        o3 = p.op("vector", lambda E: E.reciprocal(rcp, ssum), [o2],
                  tag=f"{name}rx{tt}_r")
        o4 = p.op("vector", lambda E: E.tensor_scalar(orig, orig, rcp, None,
                                                      ALU.mult),
                  [o3, o2], tag=f"{name}rx{tt}_n")
        o5 = p.op("vector", lambda E: E.tensor_tensor(smask, orig, ebsb, ALU.add),
                  [o4, eb_ld], tag=f"{name}rx{tt}_s")
        prev = o5
        first = True
        for it in range(6):
            d1 = p.op("vector", lambda E: E.tensor_reduce(colx[:, 3:4], smask,
                                                          AX.X, ALU.max),
                      [prev], tag=f"{name}rx{tt}_t{it}a")
            d2 = p.op("vector", lambda E: E.scalar_tensor_tensor(
                eqb, smask, colx[:, 3:4], smask, ALU.is_equal, ALU.bypass),
                [d1], tag=f"{name}rx{tt}_t{it}b")
            if first:
                d3 = p.op("vector", lambda E: E.tensor_copy(maskb, eqb), [d2],
                          tag=f"{name}rx{tt}_t{it}c")
                first = False
            else:
                d3 = p.op("vector", lambda E: E.tensor_tensor(maskb, maskb, eqb,
                                                              ALU.add),
                          [d2], tag=f"{name}rx{tt}_t{it}c")
            d4 = p.op("vector", lambda E: E.scalar_tensor_tensor(
                smask, eqb, NEG, smask, ALU.mult, ALU.add),
                [d3, d2], tag=f"{name}rx{tt}_t{it}d")
            prev = d4
        combv = comb_tm[:, tt * 32:(tt + 1) * 32]
        e1 = p.op("vector", lambda E, cv=combv: E.tensor_tensor(
            cv, orig, maskb, ALU.mult), [prev], tag=f"{name}rx{tt}_w")
        e2 = p.op("vector", lambda E, cv=combv: E.tensor_reduce(
            colx[:, 4:5], cv, AX.X, ALU.add), [e1], tag=f"{name}rx{tt}_ws")
        e3 = p.op("vector", lambda E: E.tensor_scalar(
            colx[:, 4:5], colx[:, 4:5], 1e-12, None, ALU.max),
            [e2], tag=f"{name}rx{tt}_cl")
        e4 = p.op("vector", lambda E: E.reciprocal(colx[:, 5:6], colx[:, 4:5]),
                  [e3], tag=f"{name}rx{tt}_rc")
        e5 = p.op("vector", lambda E, cv=combv: E.tensor_scalar(
            cv, cv, colx[:, 5:6], None, ALU.mult),
            [e4], tag=f"{name}rx{tt}_cm")
        war_ct = e5
        comb_done.append(e5)
    pT = ps["D"][0:32, 0:128]
    war_pt = None
    combfm_done = []
    ctmp = sl.take(64, F32, name="d_ctf")
    for tt in range(16):
        t1 = p.op("vector", lambda E, tt=tt: E.tensor_copy(
            ctmp[:, 0:32], comb_tm[:, tt * 32:(tt + 1) * 32]),
            [comb_done[tt], war_pt], tag=f"{name}ctc{tt}")
        t2 = p.op("tensor", lambda E, tt=tt: E.transpose(
            pT, ctmp[:, 0:32], dr["sb_eye"]),
            [t1, dr["sb_eye_ld"], war_pt], tag=f"{name}ctt{tt}")
        t3 = p.op("scalar", lambda E, tt=tt: E.activation(
            combfm[0:32, tt * 128:(tt + 1) * 128], pT, AF.Identity),
            [t2], tag=f"{name}cte{tt}")
        war_pt = t3
        combfm_done.append(t3)
    sl.free("d_xlo", "d_rw32", "d_rwh", "d_rwl", "d_eb", "d_sc", "d_orig",
            "d_smask", "d_eq", "d_mask", "d_colx", "d_ctf", "d_comb")
    p.barrier()

    yacc = sl.take(16 * 1024, F32, name="d_yacc").rearrange(
        "p (t c) -> p t c", t=16)
    ws1 = sl.take(8 * 512, BF16, name="d_ws1").rearrange("p (k c) -> p k c", k=8)
    ws3 = sl.take(8 * 512, BF16, name="d_ws3").rearrange("p (k c) -> p k c", k=8)
    z1 = sl.take(8 * 1024, BF16, name="d_z1").rearrange("p (m c) -> p m c", m=8)
    z3 = sl.take(8 * 1024, BF16, name="d_z3").rearrange("p (m c) -> p m c", m=8)
    ws1_ld = p.dma("gpsimd", ws1, dr["wf1_sgwT"].rearrange("(k p) c -> p k c", p=128),
                   tag=f"{name}sgw_ld")
    ws3_ld = p.dma("gpsimd", ws3, dr["wf1_suwT"].rearrange("(k p) c -> p k c", p=128),
                   tag=f"{name}suw_ld")
    pZ = [ps["C"], ps["D"]]
    war_pz = {0: None, 1: None}
    z_done = []
    for src_i, (wv, wld, zv, act) in enumerate([
            (ws1, ws1_ld, z1, AF.Silu), (ws3, ws3_ld, z3, AF.Identity)]):
        for m in range(8):
            for ch in range(NCH):
                pb = (m * NCH + ch) % 2
                mms = []
                for k in range(8):
                    mm = p.op("tensor", lambda E, o=pZ[pb],
                              w=wv[:, k, m * 128:(m + 1) * 128],
                              x=xfm3[:, k, ch * CW:(ch + 1) * CW],
                              s=(k == 0), t=(k == 7):
                              E.matmul(o, w, x, start=s, stop=t),
                              [wld, war_pz[pb]], tag=f"{name}z{src_i}_{m}_{ch}_{k}")
                    mms.append(mm)
                ev = p.op("scalar", lambda E, o=zv[:, m, ch * CW:(ch + 1) * CW],
                          i=pZ[pb], a=act: E.activation(o, i, a),
                          [mms[-1]], tag=f"{name}zev{src_i}_{m}_{ch}")
                war_pz[pb] = ev
                z_done.append(ev)
    zm_done = []
    for m in range(8):
        zz = p.op("vector", lambda E, m=m: E.tensor_tensor(
            z1[:, m, :], z1[:, m, :], z3[:, m, :], ALU.mult),
            [z_done], tag=f"{name}zm{m}")
        zm_done.append(zz)
    sl.free("d_ws1", "d_ws3")
    ws2 = sl.take(8 * 512, BF16, name="d_ws2").rearrange("p (k c) -> p k c", k=8)
    ws2_ld = p.dma("gpsimd", ws2, dr["wf1_sdwT"].rearrange("(k p) c -> p k c", p=128),
                   deps=[z_done], tag=f"{name}sdw_ld")
    pY = [ps["A"], ps["B"]]
    war_py = {0: None, 1: None}
    yinit = []
    for tt in range(16):
        pb = tt % 2
        mms = []
        for half in range(2):
            po = pY[pb][:, half * 512:(half + 1) * 512]
            for k in range(8):
                mm = p.op("tensor", lambda E, o=po,
                          w=z1[:, k, tt * 128:(tt + 1) * 128],
                          x=ws2[:, k, half * 512:(half + 1) * 512],
                          s=(k == 0), t=(k == 7):
                          E.matmul(o, w, x, start=s, stop=t),
                          [ws2_ld, zm_done[k] if tt == 0 else None,
                           war_py[pb]], tag=f"{name}y{tt}_{half}_{k}")
                mms.append(mm)
        ev = p.op("vector", lambda E, tt=tt, pb=pb: E.tensor_copy(
            yacc[:, tt, :], pY[pb]), [mms], tag=f"{name}yev{tt}")
        war_py[pb] = ev
        yinit.append(ev)
    sl.free("d_z1", "d_z3", "d_ws2")
    p.barrier()

    wgup = [sl.take(8 * 512, BF16, name=f"d_wg_{i}") for i in range(2)]
    wdn = [sl.take(4 * 512, BF16, name=f"d_wd_{i}") for i in range(3)]
    hm = [sl.take(4 * 1024, BF16, name=f"d_hm_{i}").rearrange(
        "p (k c) -> p k c", k=4) for i in range(2)]
    cbb = [sl.take(1024, BF16, name=f"d_cb_{i}") for i in range(2)]
    s1stg = [sl.take(256, BF16, name=f"d_s1_{i}") for i in range(2)]
    s3stg = [sl.take(256, BF16, name=f"d_s3_{i}") for i in range(2)]
    selb = [sl.take(64, BF16, name=f"d_selb{i}") for i in range(2)]

    war_wg = {0: None, 1: None}
    war_wd = {0: None, 1: None, 2: None}
    war_sel = {0: None, 1: None}
    war_hm = {0: None, 1: None}
    war_cb = {0: None, 1: None}
    war_s1 = {0: None, 1: None}
    war_s3 = {0: None, 1: None}
    war_pc = {0: None, 1: None, 2: None}
    war_pb = None
    pH = [ps["D"], ps["E"], ps["F"]]
    pBC2 = ps["C"]
    yfin = list(yinit)
    wd_lds = {}
    hm_done_by_e = {}
    for e in range(E):
        wb = e % 2
        w2b = e % 3
        wgv = wgup[wb].rearrange("p (k c) -> p k c", k=8)
        wg_ld = p.dma("gpsimd", wgv,
                      dr["wf1_gupT"][e].rearrange("(k p) c -> p k c", p=128),
                      deps=[war_wg[wb]], tag=f"{name}wg_{e}")
        wd_ld = p.dma("gpsimd", wdn[w2b].rearrange("p (k c) -> p k c", k=4),
                      dr["wf1_downT"][e].rearrange("(k p) c -> p k c", p=128),
                      deps=[war_wd[w2b]], tag=f"{name}wd_{e}")
        wd_lds[e] = wd_ld
        sel_ld = p.dma("gpsimd", selb[wb][0:32, :],
                       dr["c_sel32"][:, e * 128:(e + 1) * 128],
                       deps=[war_sel[wb]], tag=f"{name}sel_{e}")
        cb_evs = [None] * 4
        hm_done = []
        for im in range(4):
            for ch in range(NCH):
                if im == 0:
                    q = ch
                    mm = p.op("tensor", lambda E_, o=pBC2, q=q,
                              w=selb[wb][0:32, :],
                              x=combfm[0:32, q * 512:(q + 1) * 512]:
                              E_.matmul(o, w, x, start=True, stop=True),
                              [sel_ld, combfm_done, war_pb],
                              tag=f"{name}cbm{e}_{q}")
                    ev = p.op("scalar", lambda E_, q=q,
                              o=cbb[wb][:, q * 512:(q + 1) * 512],
                              i=pBC2: E_.activation(o, i, AF.Identity),
                              [mm, war_cb[wb]], tag=f"{name}cbe{e}_{q}")
                    war_pb = ev
                    cb_evs[q] = ev
                hs = 2 * (im * NCH + ch)
                hb1 = hs % 3
                hb3 = (hs + 1) % 3
                sb = (im * NCH + ch) % 2
                mm1s = []
                for k in range(8):
                    mm = p.op("tensor", lambda E_, o=pH[hb1],
                              w=wgv[:, k, im * 128:(im + 1) * 128],
                              x=xfm3[:, k, ch * CW:(ch + 1) * CW],
                              s=(k == 0), t=(k == 7):
                              E_.matmul(o, w, x, start=s, stop=t),
                              [wg_ld, war_pc[hb1]], tag=f"{name}g1_{e}_{im}_{ch}_{k}")
                    mm1s.append(mm)
                ev1 = p.op("scalar", lambda E_, o=s1stg[sb], i=pH[hb1]:
                           E_.activation(o, i, AF.Silu),
                           [mm1s[-1], war_s1[sb]], tag=f"{name}g1e_{e}_{im}_{ch}")
                war_pc[hb1] = ev1
                mm3s = []
                for k in range(8):
                    mm = p.op("tensor", lambda E_, o=pH[hb3],
                              w=wgv[:, k, (4 + im) * 128:(5 + im) * 128],
                              x=xfm3[:, k, ch * CW:(ch + 1) * CW],
                              s=(k == 0), t=(k == 7):
                              E_.matmul(o, w, x, start=s, stop=t),
                              [wg_ld, war_pc[hb3]], tag=f"{name}g3_{e}_{im}_{ch}_{k}")
                    mm3s.append(mm)
                ev3 = p.op("scalar", lambda E_, o=s3stg[sb], i=pH[hb3]:
                           E_.activation(o, i, AF.Identity),
                           [mm3s[-1], war_s3[sb]], tag=f"{name}g3e_{e}_{im}_{ch}")
                war_pc[hb3] = ev3
                hv = hm[wb][:, im, ch * CW:(ch + 1) * CW]
                m1 = p.op("vector", lambda E_, o=hv, a=s1stg[sb], b=s3stg[sb]:
                          E_.tensor_tensor(o, a, b, ALU.mult),
                          [ev1, ev3, war_hm[wb]], tag=f"{name}gm_{e}_{im}_{ch}")
                war_s1[sb] = m1
                war_s3[sb] = m1
                m2 = p.op("gpsimd", lambda E_, o=hv, a=hv,
                          b=cbb[wb][:, ch * CW:(ch + 1) * CW]:
                          E_.tensor_tensor(o, a, b, ALU.mult),
                          [m1, cb_evs[ch]], tag=f"{name}gs_{e}_{im}_{ch}")
                hm_done.append(m2)
        war_cb[wb] = hm_done
        war_sel[wb] = cb_evs
        hm_done_by_e[e] = hm_done
        if e % GE == GE - 1:
            group = [e - 1, e]
            new_yfin = []
            for tt in range(16):
                pb = tt % 2
                mms = []
                for gi, ee in enumerate(group):
                    wbb = ee % 2
                    wdv = wdn[ee % 3].rearrange("p (k c) -> p k c", k=4)
                    for half in range(2):
                        po = pY[pb][:, half * 512:(half + 1) * 512]
                        for k in range(4):
                            mm = p.op("tensor", lambda E_, o=po,
                                      w=hm[wbb][:, k, tt * 128:(tt + 1) * 128],
                                      x=wdv[:, k, half * 512:(half + 1) * 512],
                                      s=(gi == 0 and k == 0),
                                      t=(gi == 1 and k == 3):
                                      E_.matmul(o, w, x, start=s, stop=t),
                                      [wd_lds[ee],
                                       hm_done_by_e[ee] if tt == 0 else None,
                                       war_py[pb]],
                                      tag=f"{name}dn{ee}_{tt}_{half}_{k}")
                            mms.append(mm)
                ad = p.op("vector", lambda E_, tt=tt, pb=pb:
                          E_.tensor_tensor(yacc[:, tt, :], pY[pb],
                                           yacc[:, tt, :], ALU.add),
                          [mms, yfin[tt]], tag=f"{name}ya{e}_{tt}")
                war_py[pb] = ad
                new_yfin.append(ad)
            yfin = new_yfin
            war_wg[0] = hm_done_by_e[e - 1]
            war_wg[1] = hm_done_by_e[e]
            war_hm[0] = new_yfin
            war_hm[1] = new_yfin
            war_wd[(e - 1) % 3] = new_yfin
            war_wd[e % 3] = new_yfin
    sl.free("d_wg_0", "d_wg_1", "d_wd_0", "d_wd_1", "d_wd_2", "d_hm_0",
            "d_hm_1", "d_cb_0", "d_cb_1", "d_s1_0", "d_s1_1", "d_s3_0",
            "d_s3_1", "d_selb0", "d_selb1", "d_xfm", "d_combfm")
    p.barrier()

    def src_fn(tt):
        return yacc[:, tt, :], [yfin[tt]]

    readers6 = []
    stores = ln_res_add_store(p, nc, dr, sl, ps, src_fn, x_in_hbm, x_out_hbm,
                              readers_out=readers6, name=name)
    dr["_defer"] += ["d_yacc"]
    return stores


STAGE_EMITTERS["l1_ffn"] = emit_l1_ffn


# version 14 (final)
# speedup vs baseline: 1.0366x; 1.0366x over previous
import numpy as np
import concourse.bass as bass
import concourse.mybir as mybir
from concourse.bass_utils import run_bass_kernel_spmd

F32 = mybir.dt.float32
F16 = mybir.dt.float16
BF16 = mybir.dt.bfloat16
AF = mybir.ActivationFunctionType
ALU = mybir.AluOpType
AX = mybir.AxisListType

T = 2048
D = 1024
NEG = -1.0e30


ENGINES = ("tensor", "vector", "scalar", "gpsimd", "sync")


class _Op:
    __slots__ = ("eng", "fn", "deps", "is_dma", "sem_val", "need_inc", "tag",
                 "dma_sem")

    def __init__(self, eng, fn, deps, is_dma, tag=""):
        self.eng = eng
        self.fn = fn
        self.deps = deps
        self.is_dma = is_dma
        self.sem_val = None
        self.need_inc = False
        self.tag = tag


class Prog:
    def __init__(self):
        self.ops = {e: [] for e in ENGINES}
        self.cur_barrier = []
        self.joined = set(ENGINES)

    def op(self, eng, fn, deps=(), tag=""):
        dl = [d for d in _flat(deps) if d is not None]
        if eng not in self.joined:
            dl = dl + self.cur_barrier
            self.joined.add(eng)
        o = _Op(eng, fn, dl, False, tag)
        self.ops[eng].append(o)
        return o

    def dma(self, eng, out, in_, deps=(), tag=""):
        dl = [d for d in _flat(deps) if d is not None]
        if eng not in self.joined:
            dl = dl + self.cur_barrier
            self.joined.add(eng)
        o = _Op(eng, lambda E, out=out, in_=in_: E.dma_start(out=out, in_=in_),
                dl, True, tag)
        self.ops[eng].append(o)
        return o

    def barrier(self):
        b = []
        for e in ENGINES:
            ops = self.ops[e]
            if not ops:
                continue
            for o in reversed(ops):
                if not o.is_dma:
                    b.append(o)
                    break
            cnt = 0
            for o in reversed(ops):
                if o.is_dma:
                    b.append(o)
                    cnt += 1
                    if cnt >= 24:
                        break
        self.cur_barrier = b
        self.joined = set()
        return b

    def emit(self, nc, es):
        DMA_POOL = 24
        dma_lists = {}
        for e in ENGINES:
            dl = [o for o in self.ops[e] if o.is_dma]
            dma_lists[e] = dl
            for i, o in enumerate(dl):
                o.dma_sem = f"q_{e}_{i % DMA_POOL}"
                o.sem_val = 16 * (i // DMA_POOL + 1)
                if i >= DMA_POOL:
                    o.deps = o.deps + [dl[i - DMA_POOL]]
        for e in ENGINES:
            for o in self.ops[e]:
                for d in o.deps:
                    if d.is_dma:
                        continue
                    if d.eng == "tensor" and o.eng == "tensor":
                        continue
                    d.need_inc = True
        sems = {}
        for e in ENGINES:
            if self.ops[e]:
                sems["c_" + e] = es.enter_context(nc.semaphore("c_" + e))
            for i in range(min(DMA_POOL, len(dma_lists[e]))):
                nm = f"q_{e}_{i}"
                sems[nm] = es.enter_context(nc.semaphore(nm))
        totals = {}
        for e in ENGINES:
            c_cnt = 0
            for o in self.ops[e]:
                if (not o.is_dma) and o.need_inc:
                    c_cnt += 1
                    o.sem_val = c_cnt
            totals[e] = c_cnt
        dma_totals = {}
        for e in ENGINES:
            for i, o in enumerate(dma_lists[e]):
                dma_totals[o.dma_sem] = max(dma_totals.get(o.dma_sem, 0),
                                            o.sem_val)

        block = es.enter_context(nc.Block())
        prog = self

        def mk_section(e):
            def section(E):
                waited = {}
                for o in prog.ops[e]:
                    for dep in o.deps:
                        if dep.eng == "tensor" and e == "tensor" \
                                and not dep.is_dma:
                            continue
                        sem_name = dep.dma_sem if dep.is_dma else "c_" + dep.eng
                        assert dep.sem_val is not None, (
                            f"dep without sem: {dep.tag} <- {o.tag}")
                        if waited.get(sem_name, 0) < dep.sem_val:
                            E.wait_ge(sems[sem_name], dep.sem_val)
                            waited[sem_name] = dep.sem_val
                    inst = o.fn(E)
                    if o.is_dma:
                        inst.then_inc(sems[o.dma_sem], 16)
                    elif o.need_inc:
                        inst.then_inc(sems["c_" + e], 1)
                if e == "gpsimd":
                    for nm, v in sorted(dma_totals.items()):
                        E.wait_ge(sems[nm], v)
                    for e2 in ENGINES:
                        if totals[e2] and e2 != "gpsimd":
                            E.wait_ge(sems["c_" + e2], totals[e2])
            return section

        for e in ENGINES:
            if prog.ops[e] or e == "gpsimd":
                getattr(block, e)(mk_section(e))


def _flat(x):
    out = []
    stack = [x]
    while stack:
        v = stack.pop()
        if v is None:
            continue
        if isinstance(v, (list, tuple)):
            stack.extend(v)
        else:
            out.append(v)
    return out



class Slab:

    def __init__(self, nc, name, words_per_partition):
        self.t = nc.alloc_sbuf_tensor(name, [128, words_per_partition], F32)
        self.words = words_per_partition
        self.free_list = [(0, words_per_partition)]
        self.live = {}
        self.hwm = 0

    def take(self, words, dtype=F32, name=None):
        assert name is not None and name not in self.live, name
        for i, (off, ln) in enumerate(self.free_list):
            if ln >= words:
                if ln == words:
                    self.free_list.pop(i)
                else:
                    self.free_list[i] = (off + words, ln - words)
                self.live[name] = (off, words)
                self.hwm = max(self.hwm, max(
                    o + l for (o, l) in ([(off, words)] + [
                        (o2, l2) for n2, (o2, l2) in self.live.items()])))
                v = self.t.ap()[:, off:off + words]
                return v if dtype == F32 else v.bitcast(dtype)
        raise AssertionError(
            f"slab overflow for {name}({words}); live="
            f"{sorted((n, o, l) for n, (o, l) in self.live.items())}")

    def free(self, *names):
        for name in names:
            off, ln = self.live.pop(name)
            self.free_list.append((off, ln))
        self.free_list.sort()
        merged = []
        for off, ln in self.free_list:
            if merged and merged[-1][0] + merged[-1][1] == off:
                merged[-1] = (merged[-1][0], merged[-1][1] + ln)
            else:
                merged.append((off, ln))
        self.free_list = [tuple(x) for x in merged]



def prep_weights(I):
    H = {}
    ct = np.ascontiguousarray
    H["w0_Win"] = ct(I["L0_attn_W_in"])
    H["w0_dwT"] = ct(np.asarray(I["L0_attn_dw_kernel"]).T.reshape(16, 128, 4)
                     .transpose(1, 0, 2).reshape(128, 64))
    H["w0_cb"] = ct(np.asarray(I["L0_attn_conv_bias"]).reshape(16, 128).T)
    H["w0_pw"] = ct(I["L0_attn_pw_kernel"])
    H["w0_Wx"] = ct(I["L0_attn_W_x"])
    H["w0_Wdt"] = ct(I["L0_attn_W_dt"])
    H["w0_bdt"] = ct(np.asarray(I["L0_attn_b_dt"]).reshape(16, 128).T)
    H["w0_Alog"] = ct(np.asarray(I["L0_attn_A_log"]).reshape(16, 128, 8)
                      .transpose(1, 0, 2).reshape(128, 128))
    H["w0_D"] = ct(np.asarray(I["L0_attn_D"]).reshape(16, 128).T)
    H["w0_Wout"] = ct(I["L0_attn_W_out"])
    H["wf0_gwT"] = ct(I["L0_ffn_gate_w"].T)
    H["wf0_gb"] = ct(np.broadcast_to(I["L0_ffn_gate_b"].reshape(1, -1), (128, 16)))
    H["wf0_We1"] = ct(I["L0_ffn_We1"])
    H["wf0_be1"] = ct(np.asarray(I["L0_ffn_be1"]).reshape(16, 4, 128)
                      .transpose(0, 2, 1))
    H["wf0_We3"] = ct(I["L0_ffn_We3"])
    H["wf0_be3"] = ct(np.asarray(I["L0_ffn_be3"]).reshape(16, 4, 128)
                      .transpose(0, 2, 1))
    H["wf0_We2"] = ct(I["L0_ffn_We2"])
    H["wf0_be2s"] = ct(np.concatenate(
        [I["L0_ffn_be2"], I["L0_ffn_bs2"].reshape(1, -1)], axis=0))
    H["wf0_ws1"] = ct(I["L0_ffn_ws1"])
    H["wf0_bs1"] = ct(np.asarray(I["L0_ffn_bs1"]).reshape(8, 128).T)
    H["wf0_ws3"] = ct(I["L0_ffn_ws3"])
    H["wf0_bs3"] = ct(np.asarray(I["L0_ffn_bs3"]).reshape(8, 128).T)
    H["wf0_ws2"] = ct(I["L0_ffn_ws2"])
    H["w1_A"] = ct(np.asarray(I["L1_attn_A"]).reshape(8, 128, 16)
                   .transpose(1, 0, 2).reshape(128, 128))
    H["w1_WdT"] = ct(I["L1_attn_Wd"].T)
    H["w1_bd"] = ct(np.asarray(I["L1_attn_bd"]).reshape(8, 128).T)
    H["w1_WbT"] = ct(I["L1_attn_Wb"].T)
    H["w1_bb"] = ct(I["L1_attn_bb"].reshape(-1, 1))
    H["w1_WcT"] = ct(I["L1_attn_Wc"].T)
    H["w1_bc"] = ct(I["L1_attn_bc"].reshape(-1, 1))
    H["wf1_rwT"] = ct(I["L1_ffn_router_weight"].T)
    H["wf1_eb"] = ct(np.broadcast_to(I["L1_ffn_e_score_bias"].reshape(1, -1), (128, 32)))
    H["wf1_gupT"] = ct(np.transpose(I["L1_ffn_gate_up_proj"], (0, 2, 1)))
    H["wf1_downT"] = ct(np.transpose(I["L1_ffn_down_proj"], (0, 2, 1)))
    H["wf1_sgwT"] = ct(I["L1_ffn_shared_gate_w"].T)
    H["wf1_suwT"] = ct(I["L1_ffn_shared_up_w"].T)
    H["wf1_sdwT"] = ct(I["L1_ffn_shared_down_w"].T)
    H["c_eye_bf"] = np.eye(128, dtype=np.float32)
    H["c_eye_f16"] = np.eye(128, dtype=np.float32)
    H["c_sel16"] = np.repeat(np.eye(16, dtype=np.float32), 128, axis=1)
    H["c_sel32"] = np.repeat(np.eye(32, dtype=np.float32), 128, axis=1)
    H["c_ones1"] = np.ones((1, T), dtype=np.float32)
    cons = np.zeros((128, 8), dtype=np.float32)
    cons[:, 0] = 1e-5
    cons[:, 1] = -0.5
    cons[:, 2] = 1.0
    H["c_cons"] = cons
    return H


class Drams(dict):
    def __init__(self, nc):
        super().__init__()
        self.nc = nc

    def inp(self, name, arr):
        self[name] = self.nc.dram_tensor(name, list(arr.shape), F32,
                                         kind="ExternalInput").ap()
        return self[name]

    def out(self, name, shape):
        self[name] = self.nc.dram_tensor(name, list(shape), F32,
                                         kind="ExternalOutput").ap()
        return self[name]

    def scratch(self, name, shape, dtype=F32):
        kind = "ExternalOutput" if getattr(self, "dbg_external", False) else "Internal"
        self[name] = self.nc.dram_tensor(name, list(shape), dtype,
                                         kind=kind).ap()
        return self[name]



def ln_stats_ops(p, sb_x, mean_negms, rstd, scratch_row, scratch_col, deps,
                 dim, dr=None):
    ssum = scratch_col[:, 0:1]
    ssq = scratch_col[:, 1:2]
    o1 = p.op("vector", lambda E: E.tensor_reduce(ssum, sb_x, AX.X, ALU.add),
              deps, tag="ln_sum")
    o2 = p.op("scalar", lambda E: E.activation(scratch_row, sb_x, AF.Square,
                                               accum_out=ssq),
              deps, tag="ln_sq")
    o3 = p.op("vector", lambda E: E.tensor_scalar(mean_negms, ssum, 1.0 / dim,
                                                  None, ALU.mult),
              [o1], tag="ln_m")
    o4 = p.op("vector", lambda E: E.tensor_tensor(ssum, mean_negms, mean_negms,
                                                  ALU.mult),
              [o3], tag="ln_m2")
    o5 = p.op("vector", lambda E: E.scalar_tensor_tensor(
        ssq, ssq, 1.0 / dim, ssum, ALU.mult, ALU.subtract),
        [o2, o4], tag="ln_var")
    cb = dr["sb_cons"]
    o6 = p.op("scalar", lambda E: E.activation(rstd, ssq, AF.Ln,
                                               bias=cb[:, 0:1]),
              [o5, dr["sb_cons_ld"]], tag="ln_lnv")
    o7 = p.op("scalar", lambda E: E.activation(rstd, rstd, AF.Exp,
                                               scale=cb[:, 1:2]),
              [o6], tag="ln_rstd")
    o8 = p.op("vector", lambda E: E.scalar_tensor_tensor(
        mean_negms, mean_negms, -1.0, rstd, ALU.mult, ALU.mult),
        [o7, o3], tag="ln_negms")
    return [o7, o8]



def emit_ln_to_fm(p, nc, dr, sl, ps, x_hbm, xfm, lo=None, name=""):
    nm = name + "lnfm_"
    xtiles = [sl.take(1024, F32, name=nm + f"x{i}") for i in range(2)]
    ln32 = [sl.take(1024, F32, name=nm + f"l{i}") for i in range(2)]
    scratch_row = sl.take(1024, F32, name=nm + "sc")
    cols = sl.take(16, F32, name=nm + "cols")
    pA = [ps["A"], ps["B"]]
    evac_ops = []
    war_x = {0: None, 1: None}
    war_ln = {0: None, 1: None}
    war_p = {0: None, 1: None}
    war_stats = {}
    xfm3 = xfm.rearrange("p (f t) -> p f t", f=8)
    lo3 = lo.rearrange("p (f t) -> p f t", f=8) if lo is not None else None
    for tt in range(16):
        bi = tt % 2
        xtile = xtiles[bi]
        ld = p.dma("sync", xtile, x_hbm[tt * 128:(tt + 1) * 128, :],
                   deps=[war_x[bi]], tag=f"{name}ln_ld{tt}")
        mcol = cols[:, 4 * bi:4 * bi + 1]
        rcol = cols[:, 4 * bi + 1:4 * bi + 2]
        scol = cols[:, 4 * bi + 2:4 * bi + 4]
        stats = ln_stats_ops(p, xtile, mcol, rcol, scratch_row, scol,
                             [ld, war_stats.get(bi)], D, dr=dr)
        ap32 = p.op("scalar", lambda E, o=ln32[bi], i=xtile, r=rcol, m=mcol:
                    E.activation(o, i, AF.Identity, bias=m, scale=r),
                    [stats, war_ln[bi]], tag=f"{name}ln_ap{tt}")
        war_stats[bi] = stats + [ap32]
        war_x[bi] = ap32
        trs = []
        for f in range(8):
            t = p.op("tensor", lambda E, o=pA[bi][:, f * 128:(f + 1) * 128],
                     i=ln32[bi][:, f * 128:(f + 1) * 128], idn=dr["sb_eye"]:
                     E.transpose(o, i, idn),
                     [ap32, war_p[bi], dr["sb_eye_ld"]],
                     tag=f"{name}ln_tr{tt}_{f}")
            trs.append(t)
        war_ln[bi] = trs
        src = pA[bi][:].rearrange("p (f t) -> p f t", f=8)
        ev = p.op("vector", lambda E, o=xfm3[:, :, tt * 128:(tt + 1) * 128],
                  i=src: E.tensor_copy(o, i), trs, tag=f"{name}ln_ev{tt}")
        evac_ops.append(ev)
        war_p[bi] = ev
        if lo is not None:
            ev2 = p.op("vector",
                       lambda E, o=lo3[:, :, tt * 128:(tt + 1) * 128],
                       i=src, h=xfm3[:, :, tt * 128:(tt + 1) * 128]:
                       E.tensor_tensor(o, i, h, ALU.subtract),
                       [ev], tag=f"{name}ln_lo{tt}")
            evac_ops.append(ev2)
            war_p[bi] = ev2
    sl.free(nm + "x0", nm + "x1", nm + "l0", nm + "l1", nm + "sc", nm + "cols")
    return {"evac": evac_ops}


def ln_res_add_store(p, nc, dr, sl, ps, src_sb_fn, x_old_hbm, x_new_hbm,
                     readers_out=None, name=""):
    nm = name + "ra_"
    xtiles = [sl.take(1024, F32, name=nm + f"x{i}") for i in range(2)]
    t1 = [sl.take(1024, F32, name=nm + f"t{i}") for i in range(2)]
    scratch_row = sl.take(1024, F32, name=nm + "sc")
    cols = sl.take(16, F32, name=nm + "cols")
    stores = []
    war_x = {0: None, 1: None}
    war_t = {0: None, 1: None}
    war_st = {}
    for tt in range(16):
        bi = tt % 2
        ld = p.dma("sync", xtiles[bi], x_old_hbm[tt * 128:(tt + 1) * 128, :],
                   deps=[war_x[bi]], tag=f"{name}ra_ld{tt}")
        ftile, fdeps = src_sb_fn(tt)
        mcol = cols[:, 4 * bi:4 * bi + 1]
        rcol = cols[:, 4 * bi + 1:4 * bi + 2]
        scol = cols[:, 4 * bi + 2:4 * bi + 4]
        stats = ln_stats_ops(p, ftile, mcol, rcol, scratch_row, scol,
                             [fdeps, war_st.get(bi)], D, dr=dr)
        apt = p.op("scalar", lambda E, o=t1[bi], i=ftile, r=rcol, m=mcol:
                   E.activation(o, i, AF.Identity, bias=m, scale=r),
                   [stats, war_t[bi]], tag=f"{name}ra_ap{tt}")
        war_st[bi] = stats + [apt]
        addo = p.op("vector", lambda E, o=t1[bi], a=t1[bi], b=xtiles[bi]:
                    E.tensor_tensor(o, a, b, ALU.add),
                    [apt, ld], tag=f"{name}ra_add{tt}")
        war_x[bi] = addo
        st = p.dma("sync", x_new_hbm[tt * 128:(tt + 1) * 128, :], t1[bi],
                   deps=[addo], tag=f"{name}ra_st{tt}")
        stores.append(st)
        war_t[bi] = st
        if readers_out is not None:
            readers_out.append(stats + [apt])
    sl.free(nm + "x0", nm + "x1", nm + "t0", nm + "t1", nm + "sc", nm + "cols")
    return stores


def _scan_yadd(p, yslice, hc_ap, hc_op, yprev, war, warkey, tag):
    if yprev is None:
        y_o = p.op("gpsimd", lambda E, o=yslice, i=hc_ap: E.tensor_copy(o, i),
                   [hc_op], tag=tag)
    else:
        y_o = p.op("gpsimd", lambda E, o=yslice, a=hc_ap, b=yslice:
                   E.tensor_tensor(o, a, b, ALU.add),
                   [hc_op, yprev], tag=tag)
    war[warkey] = y_o
    return y_o



def emit_l0_attn(p, nc, dr, sl, ps, x_in_hbm, x_out_hbm, name="a0_"):
    import os
    UPTO = int(os.environ.get("L0A_UPTO", "9"))
    DI = 2048
    NCH = 4
    CW = 512
    z_hbm = dr["s_z"]
    xc_hbm = dr["s_xc"]
    dl_hbm = dr["s_dl"]

    bmcsb = sl.take(1024, BF16, name="a_bmc")

    xfm = sl.take(8 * 1024, BF16, name="a_xfm")
    xin_res = sl.take(16 * 1028, F16, name="a_xinr").rearrange(
        "p (d c) -> p d c", d=16)
    winb = [sl.take(512, BF16, name=f"a_win{i}") for i in range(2)]
    stg = [sl.take(256, F16, name=f"a_stg{i}") for i in range(2)]
    pwv = sl.take(16 * 1024, BF16, name="a_pw").rearrange("p (k c) -> p k c", k=16)
    pw_ld = p.dma("gpsimd", pwv, dr["w0_pw"].rearrange("(k p) c -> p k c", p=128),
                  tag=f"{name}pw_ld")
    win_pre = []
    for m in range(2):
        wv = winb[m].rearrange("p (k c) -> p k c", k=8)
        win_pre.append(p.dma("gpsimd", wv,
                             dr["w0_Win"][:, m * 128:(m + 1) * 128]
                             .rearrange("(k p) c -> p k c", p=128),
                             tag=f"{name}win_ld{m}"))
    ln = emit_ln_to_fm(p, nc, dr, sl, ps, x_in_hbm, xfm, name=name)
    p.barrier()
    xfm3 = xfm.rearrange("p (f t) -> p f t", f=8)

    pX = [ps["C"], ps["D"]]
    war_w = {0: None, 1: None}
    war_s = {0: None, 1: None}
    war_px = {0: None, 1: None}
    st1_stores = []
    xin_done = {}
    zp_ops = {}
    for dt_ in range(16):
        zp_ops[dt_] = p.op("gpsimd", lambda E, o=xin_res[:, dt_, 0:3]:
                           E.memset(o, 0.0), [], tag=f"{name}cvzp{dt_}")
    for m in range(32):
        wb = m % 2
        wv = winb[wb].rearrange("p (k c) -> p k c", k=8)
        if m < 2:
            wld = win_pre[m]
        else:
            wld = p.dma("gpsimd", wv,
                        dr["w0_Win"][:, m * 128:(m + 1) * 128]
                        .rearrange("(k p) c -> p k c", p=128),
                        deps=[war_w[wb]], tag=f"{name}win_ld{m}")
        wreaders = []
        for ch in range(NCH):
            pb = (m * NCH + ch) % 2
            mms = []
            for k in range(8):
                mm = p.op("tensor", lambda E, o=pX[pb], w=wv[:, k, :],
                          x=xfm3[:, k, ch * CW:(ch + 1) * CW], s=(k == 0),
                          t=(k == 7): E.matmul(o, w, x, start=s, stop=t),
                          [wld, war_px[pb]] + (ln["evac"] if m == 0 else []),
                          tag=f"{name}xz{m}_{ch}_{k}")
                mms.append(mm)
            wreaders += mms
            sb = (m * NCH + ch) % 2
            if m < 16:
                ev = p.op("vector", lambda E,
                          o=xin_res[:, m, 3 + ch * CW:3 + (ch + 1) * CW],
                          i=pX[pb]: E.tensor_copy(o, i),
                          [mms[-1]], tag=f"{name}xin_ev{m}_{ch}")
                xin_done[(m, ch)] = ev
            else:
                ev = p.op("scalar", lambda E, o=stg[sb], i=pX[pb]:
                          E.activation(o, i, AF.Silu), [mms[-1], war_s[sb]],
                          tag=f"{name}z_ev{m}_{ch}")
                st = p.dma("sync", z_hbm[m - 16, :, ch * CW:(ch + 1) * CW],
                           stg[sb], deps=[ev], tag=f"{name}z_st{m}_{ch}")
                st1_stores.append(st)
                war_s[sb] = st
            war_px[pb] = ev
        war_w[wb] = wreaders
    sl.free("a_xfm", "a_win0", "a_win1", "a_stg0", "a_stg1")
    p.barrier()
    if UPTO <= 1:
        return []

    cvb = [sl.take(16 * 256, BF16, name=f"a_cv{i}")
           .rearrange("p (d c) -> p d c", d=16) for i in range(2)]
    xcstg = [sl.take(256, BF16, name=f"a_xcs{i}") for i in range(2)]
    dstg = [sl.take(256, F16, name=f"a_dst{i}") for i in range(2)]
    wxsb = sl.take(16 * 40, BF16, name="a_wx").rearrange("p (k c) -> p k c", k=16)
    wdtsb = sl.take(1024, BF16, name="a_wdt")
    drsb = sl.take(1024, BF16, name="a_dr")
    dwsb = sl.take(16 * 4, F32, name="a_dw").rearrange("p (d c) -> p d c", d=16)
    cbsb = sl.take(16, F32, name="a_cb").rearrange("p (d c) -> p d c", d=16)
    bdtsb = sl.take(16, F32, name="a_bdt").rearrange("p (d c) -> p d c", d=16)
    tmp32 = [sl.take(512, F32, name=f"a_tmp{i}") for i in range(2)]

    wx_ld = p.dma("gpsimd", wxsb, dr["w0_Wx"].rearrange("(k p) c -> p k c", p=128),
                  tag=f"{name}wx_ld")
    wdt_ld = p.dma("gpsimd", wdtsb[0:64, :], dr["w0_Wdt"], tag=f"{name}wdt_ld")
    dw_ld = p.dma("sync", dwsb, dr["w0_dwT"].rearrange("p (d c) -> p d c", d=16),
                  tag=f"{name}dw_ld")
    cb_ld = p.dma("sync", cbsb, dr["w0_cb"].rearrange("p (d c) -> p d c", d=16),
                  tag=f"{name}cb_ld")
    bdt_ld = p.dma("sync", bdtsb, dr["w0_bdt"].rearrange("p (d c) -> p d c", d=16),
                   tag=f"{name}bdt_ld")

    pP = [ps["C"], ps["D"]]
    pXD = ps["A"][0:80, 0:512]
    pDT = [ps["B"][:, 0:512], ps["B"][:, 512:1024]]
    war_xin = {0: None, 1: None}
    war_cv = {0: None, 1: None}
    war_pp = {0: None, 1: None}
    war_pdt = {0: None, 1: None}
    war_xcs = {0: None, 1: None}
    war_tmp = {0: None, 1: None}
    war_dst = {0: None, 1: None}
    war_pxd = None
    bmc_evs = []
    for ch in range(NCH):
        cbuf = ch % 2
        conv_ops = []
        for dt_ in range(16):
            xv = xin_res[:, dt_, ch * CW:ch * CW + 515]
            deps0 = [xin_done[(dt_, ch)], zp_ops[dt_], war_cv[cbuf], dw_ld]
            if ch > 0:
                deps0.append(xin_done[(dt_, ch - 1)])
            cv = cvb[cbuf][:, dt_, :]
            cprev = p.op("vector", lambda E, o=cv, i=xv[:, 0:512],
                         s=dwsb[:, dt_, 0:1]:
                         E.tensor_scalar(o, i, s, None, ALU.mult),
                         deps0, tag=f"{name}cv0_{ch}_{dt_}")
            for tap in range(1, 4):
                cprev = p.op("vector", lambda E, o=cv, i=xv[:, tap:tap + 512],
                             s=dwsb[:, dt_, tap:tap + 1], a=cv:
                             E.scalar_tensor_tensor(o, i, s, a, ALU.mult, ALU.add),
                             [cprev], tag=f"{name}cv{tap}_{ch}_{dt_}")
            conv_ops.append(cprev)
        war_xin[cbuf] = conv_ops
        xd_mms = []
        xc_evs = []
        chunk_pw_mms = []
        for m2 in range(16):
            pb = (ch * 16 + m2) % 2
            mms = []
            for k in range(16):
                mm = p.op("tensor", lambda E, o=pP[pb],
                          w=pwv[:, k, m2 * 128:(m2 + 1) * 128],
                          x=cvb[cbuf][:, k, :], s=(k == 0), t=(k == 15):
                          E.matmul(o, w, x, start=s, stop=t),
                          [pw_ld if (ch == 0 and m2 == 0) else None,
                           conv_ops[k], war_pp[pb]],
                          tag=f"{name}pw{ch}_{m2}_{k}")
                mms.append(mm)
            chunk_pw_mms += mms
            sb = m2 % 2
            ev = p.op("scalar", lambda E, o=xcstg[sb], i=pP[pb],
                      b=cbsb[:, m2, 0:1]:
                      E.activation(o, i, AF.Silu, bias=b),
                      [mms[-1], war_xcs[sb], cb_ld], tag=f"{name}xc_ev{ch}_{m2}")
            xc_evs.append(ev)
            war_pp[pb] = ev
            st = p.dma("sync", xc_hbm[m2, :, ch * CW:(ch + 1) * CW],
                       xcstg[sb], deps=[ev], tag=f"{name}xc_st{ch}_{m2}")
            mmx = p.op("tensor", lambda E, o=pXD, w=wxsb[:, m2, :], x=xcstg[sb],
                       s=(m2 == 0), t=(m2 == 15):
                       E.matmul(o, w, x, start=s, stop=t),
                       [ev, wx_ld, war_pxd if m2 == 0 else None],
                       tag=f"{name}wx{ch}_{m2}")
            xd_mms.append(mmx)
            war_xcs[sb] = [st, mmx]
        ev_dr = p.op("vector", lambda E, o=drsb[0:64, ch * CW:(ch + 1) * CW],
                     i=pXD[0:64, :]: E.tensor_copy(o, i),
                     [xd_mms[-1]], tag=f"{name}dr_ev{ch}")
        ev_bc = p.op("vector", lambda E, o=bmcsb[0:16, ch * CW:(ch + 1) * CW],
                     i=ps["A"][64:80, 0:512]: E.tensor_copy(o, i),
                     [xd_mms[-1]], tag=f"{name}bc_ev{ch}")
        bmc_evs.append(ev_bc)
        war_pxd = [ev_dr, ev_bc]
        war_cv[cbuf] = chunk_pw_mms
        for m in range(16):
            pb = m % 2
            mm = p.op("tensor", lambda E, o=pDT[pb],
                      w=wdtsb[0:64, m * 128:(m + 1) * 128],
                      x=drsb[0:64, ch * CW:(ch + 1) * CW]:
                      E.matmul(o, w, x, start=True, stop=True),
                      [ev_dr, wdt_ld, war_pdt[pb]], tag=f"{name}dt{ch}_{m}")
            sb = m % 2
            e1 = p.op("scalar", lambda E, o=tmp32[sb], i=pDT[pb],
                      b=bdtsb[:, m, 0:1]:
                      E.activation(o, i, AF.Exp, bias=b),
                      [mm, war_tmp[sb], bdt_ld], tag=f"{name}dte{ch}_{m}")
            war_pdt[pb] = e1
            e2 = p.op("scalar", lambda E, o=dstg[sb], i=tmp32[sb],
                      b=dr["sb_cons"][:, 2:3]:
                      E.activation(o, i, AF.Ln, bias=b),
                      [e1, war_dst[sb], dr["sb_cons_ld"]],
                      tag=f"{name}dtl{ch}_{m}")
            war_tmp[sb] = e2
            st = p.dma("sync", dl_hbm[m, :, ch * CW:(ch + 1) * CW],
                       dstg[sb], deps=[e2], tag=f"{name}dt_st{ch}_{m}")
            war_dst[sb] = st
    sl.free("a_pw", "a_xinr", "a_cv0", "a_cv1", "a_xcs0", "a_xcs1",
            "a_dst0", "a_dst1", "a_wx", "a_wdt", "a_dr", "a_dw", "a_cb",
            "a_bdt", "a_tmp0", "a_tmp1")
    p.barrier()
    if UPTO <= 2:
        return []

    negA = sl.take(16 * 8, F32, name="a_negA").rearrange("p (d c) -> p d c", d=16)
    bca = sl.take(16 * 1024, F16, name="a_bca").rearrange("p (r c) -> p r c", r=16)
    selb = sl.take(1024, BF16, name="a_sel")

    al_ld = p.dma("sync", negA, dr["w0_Alog"].rearrange("p (d c) -> p d c", d=16),
                  tag=f"{name}al_ld")
    aex = p.op("scalar", lambda E, o=negA, i=negA: E.activation(o, i, AF.Exp),
               [al_ld], tag=f"{name}al_exp")
    aneg = p.op("vector", lambda E, o=negA, i=negA:
                E.tensor_scalar(o, i, -1.0, None, ALU.mult),
                [aex], tag=f"{name}al_neg")
    sel_ld = p.dma("gpsimd", selb[0:16, :], dr["c_sel16"], tag=f"{name}sel_ld")
    pBC = ps["C"]
    war_pbc = None
    bc_ready = []
    for r in range(16):
        evs = []
        for q in range(4):
            mm = p.op("tensor", lambda E, o=pBC,
                      w=selb[0:16, r * 128:(r + 1) * 128],
                      x=bmcsb[0:16, q * 512:(q + 1) * 512]:
                      E.matmul(o, w, x, start=True, stop=True),
                      [sel_ld, bmc_evs, war_pbc], tag=f"{name}bc{r}_{q}")
            ev = p.op("scalar", lambda E, o=bca[:, r, q * 512:(q + 1) * 512],
                      i=pBC: E.activation(o, i, AF.Identity),
                      [mm], tag=f"{name}bcev{r}_{q}")
            war_pbc = ev
            evs.append(ev)
        bc_ready.append(evs)

    yac = sl.take(16 * 1024, F16, name="a_yac").rearrange("p (d c) -> p d c", d=16)
    dl_b = [sl.take(1024, F16, name=f"a_sdl{i}") for i in range(2)]
    xc_b = [sl.take(1024, BF16, name=f"a_sxc{i}") for i in range(2)]
    u_b = [sl.take(1024, F16, name=f"a_su{i}") for i in range(2)]
    a_b = [sl.take(2048, F32, name=f"a_sa{i}") for i in range(2)]
    b_b = [sl.take(1024, F16, name=f"a_sb{i}") for i in range(2)]
    h_b = [sl.take(1024, F16, name=f"a_sh{i}") for i in range(2)]
    hc_b = [sl.take(1024, F16, name=f"a_shc{i}") for i in range(2)]
    war = {}
    yac_done = []
    for d in range(16):
        db = d % 2
        ld_dl = p.dma("sync", dl_b[db], dl_hbm[d], deps=[war.get(("dl", db))],
                      tag=f"{name}s_dl{d}")
        ld_xc = p.dma("sync", xc_b[db], xc_hbm[d], deps=[war.get(("xc", db))],
                      tag=f"{name}s_xc{d}")
        uop = p.op("gpsimd", lambda E, o=u_b[db], a=dl_b[db], b=xc_b[db]:
                   E.tensor_tensor(o, a, b, ALU.mult),
                   [ld_dl, ld_xc, war.get(("u", db))], tag=f"{name}s_u{d}")
        war[("xc", db)] = uop
        dl_readers = [uop]
        u_readers = []
        yprev = None
        pend = None
        for n in range(8):
            nb = n % 2
            a_o = p.op("scalar", lambda E, o=a_b[nb], i=dl_b[db],
                       s=negA[:, d, n:n + 1]:
                       E.activation(o, i, AF.Exp, scale=s),
                       [ld_dl, aneg, war.get(("a", nb))],
                       tag=f"{name}s_a{d}_{n}")
            dl_readers.append(a_o)
            b_o = p.op("vector", lambda E, o=b_b[nb], a=u_b[db],
                       b=bca[:, n, :]: E.tensor_tensor(o, a, b, ALU.mult),
                       [uop, bc_ready[n], war.get(("b", nb))],
                       tag=f"{name}s_b{d}_{n}")
            u_readers.append(b_o)
            h_o = p.op("vector", lambda E, o=h_b[nb], a=a_b[nb], b=b_b[nb]:
                       E.tensor_tensor_scan(o, a, b, 0.0, ALU.mult, ALU.add),
                       [a_o, b_o, war.get(("h", nb))],
                       tag=f"{name}s_h{d}_{n}")
            war[("a", nb)] = h_o
            war[("b", nb)] = h_o
            hc_o = p.op("vector", lambda E, o=hc_b[nb], a=h_b[nb],
                        b=bca[:, 8 + n, :]: E.tensor_tensor(o, a, b, ALU.mult),
                        [h_o, bc_ready[8 + n], war.get(("hc", nb))],
                        tag=f"{name}s_hc{d}_{n}")
            war[("h", nb)] = hc_o
            if pend is not None:
                po, pap, pnb, pn = pend
                yprev = _scan_yadd(p, yac[:, d, :], pap, po, yprev, war,
                                   ("hc", pnb), f"{name}s_y{d}_{pn}")
            pend = (hc_o, hc_b[nb], nb, n)
        po, pap, pnb, pn = pend
        yprev = _scan_yadd(p, yac[:, d, :], pap, po, yprev, war,
                           ("hc", pnb), f"{name}s_y{d}_{pn}")
        war[("dl", db)] = dl_readers
        war[("u", db)] = u_readers
        yac_done.append(yprev)
    sl.free("a_bmc", "a_negA", "a_bca", "a_sel", "a_sdl0", "a_sdl1", "a_sxc0",
            "a_sxc1", "a_su0", "a_su1", "a_sa0", "a_sa1", "a_sb0", "a_sb1",
            "a_sh0", "a_sh1", "a_shc0", "a_shc1")
    p.barrier()
    if UPTO <= 4:
        return []

    yf = sl.take(16 * 1024, BF16, name="a_yf").rearrange("p (d c) -> p d c", d=16)
    xc_b = [sl.take(1024, BF16, name=f"a_fxc{i}") for i in range(2)]
    z_b = [sl.take(1024, F16, name=f"a_fz{i}") for i in range(2)]
    s1_b = [sl.take(2048, F32, name=f"a_fs{i}") for i in range(2)]
    Dsb = sl.take(16, F32, name="a_D").rearrange("p (d c) -> p d c", d=16)
    d_ld = p.dma("sync", Dsb, dr["w0_D"].rearrange("p (d c) -> p d c", d=16),
                 tag=f"{name}D_ld")
    war5 = {}
    yf_done = []
    for d in range(16):
        db = d % 2
        ld_xc = p.dma("sync", xc_b[db], xc_hbm[d], deps=[war5.get(("xc", db))],
                      tag=f"{name}f_xc{d}")
        ld_z = p.dma("sync", z_b[db], z_hbm[d], deps=[war5.get(("z", db))],
                     tag=f"{name}f_z{d}")
        s1 = p.op("vector", lambda E, o=s1_b[db], i=xc_b[db],
                  s=Dsb[:, d, 0:1], a=yac[:, d, :]:
                  E.scalar_tensor_tensor(o, i, s, a, ALU.mult, ALU.add),
                  [ld_xc, d_ld, yac_done[d], war5.get(("s1", db))],
                  tag=f"{name}f_s1{d}")
        war5[("xc", db)] = s1
        yfo = p.op("vector", lambda E, o=yf[:, d, :], a=s1_b[db], b=z_b[db]:
                   E.tensor_tensor(o, a, b, ALU.mult),
                   [s1, ld_z], tag=f"{name}f_yf{d}")
        war5[("z", db)] = yfo
        war5[("s1", db)] = yfo
        yf_done.append(yfo)
    sl.free("a_yac", "a_fxc0", "a_fxc1", "a_fz0", "a_fz1", "a_fs0", "a_fs1",
            "a_D")
    p.barrier()
    if UPTO <= 5:
        return []

    wout = sl.take(16 * 512, BF16, name="a_wo").rearrange("p (k c) -> p k c", k=16)
    wo_ld = p.dma("gpsimd", wout, dr["w0_Wout"].rearrange("(k p) c -> p k c", p=128),
                  tag=f"{name}wo_ld")
    otile = [sl.take(1024, F32, name=f"a_ot{i}") for i in range(2)]
    pO = [ps["A"], ps["B"]]
    war6 = {}

    def src_fn(tt):
        pb = tt % 2
        ob = tt % 2
        mms = []
        for half in range(2):
            for k in range(16):
                mm = p.op("tensor", lambda E, o=pO[pb][:, half * 512:(half + 1) * 512],
                          w=yf[:, k, tt * 128:(tt + 1) * 128],
                          x=wout[:, k, half * 512:(half + 1) * 512],
                          s=(k == 0), t=(k == 15):
                          E.matmul(o, w, x, start=s, stop=t),
                          [wo_ld, yf_done[k] if tt == 0 else None,
                           war6.get(("p", pb))],
                          tag=f"{name}o{tt}_{half}_{k}")
                mms.append(mm)
        war_o = readers6[tt - 2] if tt >= 2 else None
        ev = p.op("vector", lambda E, o=otile[ob], i=pO[pb]:
                  E.tensor_copy(o, i), [mms, war_o],
                  tag=f"{name}o_ev{tt}")
        war6[("p", pb)] = ev
        return otile[ob], [ev]

    readers6 = []
    stores = ln_res_add_store(p, nc, dr, sl, ps, src_fn, x_in_hbm, x_out_hbm,
                              readers_out=readers6, name=name)
    sl.free("a_yf", "a_wo", "a_ot0", "a_ot1")
    p.barrier()
    return stores



SLAB_WORDS = 53184


def _declare_common(nc, dr, H, stages):
    dr["eye32"] = nc.dram_tensor("c_eye32", [128, 128], F32,
                                 kind="ExternalInput").ap()
    for k in ("c_sel16", "c_sel32", "c_ones1", "c_cons"):
        dr[k] = nc.dram_tensor(k, list(H[k].shape), F32,
                               kind="ExternalInput").ap()
    stage_w = {
        "l0_attn": ["w0_Win", "w0_dwT", "w0_cb", "w0_pw", "w0_Wx", "w0_Wdt",
                    "w0_bdt", "w0_Alog", "w0_D", "w0_Wout"],
        "l0_ffn": ["wf0_gwT", "wf0_gb", "wf0_We1", "wf0_be1", "wf0_We3",
                   "wf0_be3", "wf0_We2", "wf0_be2s", "wf0_ws1", "wf0_bs1",
                   "wf0_ws3", "wf0_bs3", "wf0_ws2"],
        "l1_attn": ["w1_A", "w1_WdT", "w1_bd", "w1_WbT", "w1_bb", "w1_WcT",
                    "w1_bc"],
        "l1_ffn": ["wf1_rwT", "wf1_eb", "wf1_gupT", "wf1_downT", "wf1_sgwT",
                   "wf1_suwT", "wf1_sdwT"],
    }
    used = ["c_eye32", "c_sel16", "c_sel32", "c_ones1", "c_cons"]
    for s in stages:
        for w in stage_w[s]:
            dr[w] = nc.dram_tensor(w, list(H[w].shape), F32,
                                   kind="ExternalInput").ap()
            used.append(w)
    return used


def _declare_scratch(dr, stages):
    n = dr.nc
    if "l0_attn" in stages:
        dr.scratch("s_z", [16, 128, T], F16)
        dr.scratch("s_xc", [16, 128, T], BF16)
        dr.scratch("s_dl", [16, 128, T], F16)
    if "l1_attn" in stages:
        dr.scratch("s_dl1", [8, 128, T], F32)


STAGE_EMITTERS = {}


def build_program(stages, in_name, out_name, dbg_external=False):
    import contextlib
    nc = bass.Bass("TRN2", target_bir_lowering=False)
    dr = Drams(nc)
    dr.dbg_external = dbg_external
    H = build_program.H
    used = _declare_common(nc, dr, H, stages)
    x_io = {}
    x_io[in_name] = dr.inp(in_name, np.zeros((T, D), np.float32))
    x_io[out_name] = dr.out(out_name, (T, D))
    chain = [in_name]
    for i in range(len(stages) - 1):
        nm = f"xmid{i}"
        dr.scratch(nm, [T, D], F32)
        chain.append(nm)
    chain.append(out_name)
    _declare_scratch(dr, stages)

    build_program.last_inputs = None
    es = contextlib.ExitStack()
    with es:
        sl = Slab(nc, "slab", SLAB_WORDS)
        pA = nc.alloc_psum_tensor("pA", [128, 1024], F32).ap()
        pB = nc.alloc_psum_tensor("pB", [128, 1024], F32).ap()
        pC = nc.alloc_psum_tensor("pC", [128, 512], F32).ap()
        pD = nc.alloc_psum_tensor("pD", [128, 512], F32).ap()
        pE = nc.alloc_psum_tensor("pE", [128, 512], F32).ap()
        pF = nc.alloc_psum_tensor("pF", [128, 512], F32).ap()
        ps = {"A": pA, "B": pB, "C": pC, "D": pD, "E": pE, "F": pF}
        p = Prog()
        eyesb = sl.take(128, F32, name="g_eye")
        eye_ld = p.dma("sync", eyesb, dr["eye32"], tag="eye_ld")
        dr["sb_eye"] = eyesb
        dr["sb_eye_ld"] = eye_ld
        consb = sl.take(8, F32, name="g_cons")
        cons_ld = p.dma("sync", consb, dr["c_cons"], tag="cons_ld")
        dr["sb_cons"] = consb
        dr["sb_cons_ld"] = cons_ld
        for i, s in enumerate(stages):
            x_in = dr[chain[i]]
            x_out = dr[chain[i + 1]]
            STAGE_EMITTERS[s](p, nc, dr, sl, ps, x_in, x_out, name=s[1] + s[3] + "_")
        p.emit(nc, es)
    return nc, used + [in_name]


STAGE_EMITTERS["l0_attn"] = emit_l0_attn


def _in_map_for(H, x_arr, in_name, used):
    m = {}
    for k in used:
        if k == in_name:
            m[k] = np.ascontiguousarray(x_arr, dtype=np.float32)
        elif k == "c_eye32":
            m[k] = np.eye(128, dtype=np.float32)
        else:
            m[k] = np.ascontiguousarray(H[k], dtype=np.float32)
    return m


FUSED = True


def kernel(**inputs):
    H = prep_weights(inputs)
    build_program.H = H
    stages = ["l0_attn", "l0_ffn", "l1_attn", "l1_ffn"]
    x = np.ascontiguousarray(np.asarray(inputs["L0_attn_x"], np.float32))
    cur = [x[c] for c in range(8)]
    if FUSED:
        nc, used = build_program(stages, "x0", "x1")
        in_maps = [_in_map_for(H, cur[c], "x0", used) for c in range(8)]
        res = run_bass_kernel_spmd(nc, in_maps, core_ids=list(range(8)))
        cur = [np.asarray(r["x1"], np.float32) for r in res.results]
    else:
        for s in stages:
            nc, used = build_program([s], "x0", "x1")
            in_maps = [_in_map_for(H, cur[c], "x0", used) for c in range(8)]
            res = run_bass_kernel_spmd(nc, in_maps, core_ids=list(range(8)))
            cur = [np.asarray(r["x1"], np.float32) for r in res.results]
    return np.stack(cur, 0)



def emit_l1_attn(p, nc, dr, sl, ps, x_in_hbm, x_out_hbm, name="a1_"):
    NCH = 4
    CW = 512
    NS = 16
    dl_hbm = dr["s_dl1"]

    xfm = sl.take(8 * 1024, BF16, name="b_xfm")
    wdb = sl.take(8 * 512, BF16, name="b_wd").rearrange("p (k c) -> p k c", k=8)
    wbc = sl.take(2 * 64, BF16, name="b_wbc").rearrange("p (w c) -> p w c", w=2)
    bdsb = sl.take(8, F32, name="b_bd").rearrange("p (d c) -> p d c", d=8)
    bbc = sl.take(2, F32, name="b_bbc")
    bm1 = sl.take(1024, BF16, name="b_bm")
    cm1 = sl.take(1024, BF16, name="b_cm")
    wd_ld = p.dma("gpsimd", wdb, dr["w1_WdT"].rearrange("(k p) c -> p k c", p=128),
                  tag=f"{name}wd_ld")
    wb_ld = p.dma("gpsimd", wbc[:, 0:1, :].squeeze(1)
                  .rearrange("p (k c) -> p k c", k=8),
                  dr["w1_WbT"].rearrange("(k p) c -> p k c", p=128),
                  tag=f"{name}wb_ld")
    wc_ld = p.dma("gpsimd", wbc[:, 1:2, :].squeeze(1)
                  .rearrange("p (k c) -> p k c", k=8),
                  dr["w1_WcT"].rearrange("(k p) c -> p k c", p=128),
                  tag=f"{name}wc_ld")
    bd_ld = p.dma("sync", bdsb, dr["w1_bd"].rearrange("p (d c) -> p d c", d=8),
                  tag=f"{name}bd_ld")
    bb_ld = p.dma("sync", bbc[0:16, 0:1], dr["w1_bb"], tag=f"{name}bb_ld")
    bc_ld = p.dma("sync", bbc[0:16, 1:2], dr["w1_bc"], tag=f"{name}bc_ld")
    ln = emit_ln_to_fm(p, nc, dr, sl, ps, x_in_hbm, xfm, name=name)
    p.barrier()
    xfm3 = xfm.rearrange("p (f t) -> p f t", f=8)

    dstg = [sl.take(512, F32, name=f"b_ds{i}") for i in range(2)]
    tmp32 = [sl.take(512, F32, name=f"b_tm{i}") for i in range(2)]

    pBC = ps["E"][0:16, :]
    war_pbc = None
    bc_done = []
    for w_i, (dst, bcol) in enumerate([(bm1, bbc[0:16, 0:1]),
                                       (cm1, bbc[0:16, 1:2])]):
        for ch in range(NCH):
            mms = []
            for k in range(8):
                mm = p.op("tensor", lambda E, o=pBC,
                          w=wbc[:, w_i, k * 16:(k + 1) * 16],
                          x=xfm3[:, k, ch * CW:(ch + 1) * CW],
                          s=(k == 0), t=(k == 7):
                          E.matmul(o, w, x, start=s, stop=t),
                          [wb_ld, wc_ld, war_pbc], tag=f"{name}bc{w_i}_{ch}_{k}")
                mms.append(mm)
            ev = p.op("scalar", lambda E, o=dst[0:16, ch * CW:(ch + 1) * CW],
                      i=pBC, b=bcol: E.activation(o, i, AF.Identity, bias=b),
                      [mms[-1], bb_ld, bc_ld], tag=f"{name}bcev{w_i}_{ch}")
            war_pbc = ev
            bc_done.append(ev)
    pDl = [ps["C"], ps["D"]]
    war_pd = {0: None, 1: None}
    war_tmp = {0: None, 1: None}
    war_dst = {0: None, 1: None}
    for m in range(8):
        for ch in range(NCH):
            pb = (m * NCH + ch) % 2
            mms = []
            for k in range(8):
                mm = p.op("tensor", lambda E, o=pDl[pb],
                          w=wdb[:, k, m * 128:(m + 1) * 128],
                          x=xfm3[:, k, ch * CW:(ch + 1) * CW],
                          s=(k == 0), t=(k == 7):
                          E.matmul(o, w, x, start=s, stop=t),
                          [wd_ld, war_pd[pb]] + (ln["evac"] if m == 0 else []),
                          tag=f"{name}dl{m}_{ch}_{k}")
                mms.append(mm)
            sb = ch % 2
            e1 = p.op("scalar", lambda E, o=tmp32[sb], i=pDl[pb],
                      b=bdsb[:, m, 0:1]: E.activation(o, i, AF.Exp, bias=b),
                      [mms[-1], war_tmp[sb], bd_ld], tag=f"{name}de{m}_{ch}")
            war_pd[pb] = e1
            e2 = p.op("scalar", lambda E, o=dstg[sb], i=tmp32[sb],
                      b=dr["sb_cons"][:, 2:3]: E.activation(o, i, AF.Ln, bias=b),
                      [e1, war_dst[sb], dr["sb_cons_ld"]], tag=f"{name}dn{m}_{ch}")
            war_tmp[sb] = e2
            st = p.dma("sync", dl_hbm[m, :, ch * CW:(ch + 1) * CW], dstg[sb],
                       deps=[e2], tag=f"{name}d_st{m}_{ch}")
            war_dst[sb] = st
    sl.free("b_wd", "b_ds0", "b_ds1", "b_tm0", "b_tm1", "b_wbc", "b_bd")
    p.barrier()

    NS = 16
    selb = sl.take(1024, BF16, name="b_sel")
    Asb = sl.take(8 * 16, F32, name="b_A").rearrange("p (d c) -> p d c", d=8)
    a_ld = p.dma("sync", Asb, dr["w1_A"].rearrange("p (d c) -> p d c", d=8),
                 tag=f"{name}a_ld")
    sel_ld = p.dma("gpsimd", selb[0:16, :], dr["c_sel16"], tag=f"{name}sel_ld")
    bca = sl.take(16 * 1024, F16, name="b_bca").rearrange("p (r c) -> p r c", r=16)
    yac = sl.take(8 * 1024, F16, name="b_yac").rearrange("p (d c) -> p d c", d=8)
    dl_b = [sl.take(2048, F32, name=f"b_sdl{i}") for i in range(2)]
    u_b = [sl.take(1024, F16, name=f"b_su{i}") for i in range(2)]
    a_b = [sl.take(2048, F32, name=f"b_sa{i}") for i in range(2)]
    b_b = [sl.take(1024, F16, name=f"b_sb{i}") for i in range(2)]
    h_b = [sl.take(1024, F16, name=f"b_sh{i}") for i in range(2)]
    hc_b = [sl.take(1024, F16, name=f"b_shc{i}") for i in range(2)]
    pB2 = ps["C"]
    war = {}
    war_pb2 = None
    yac_last = {}
    for half in range(2):
        bc_ready = []
        for j in range(16):
            r = half * 8 + (j % 8)
            src_sb = bm1 if j < 8 else cm1
            evs = []
            for q in range(4):
                mm = p.op("tensor", lambda E, o=pB2,
                          w=selb[0:16, r * 128:(r + 1) * 128],
                          x=src_sb[0:16, q * 512:(q + 1) * 512]:
                          E.matmul(o, w, x, start=True, stop=True),
                          [sel_ld, bc_done, war_pb2],
                          tag=f"{name}bb{half}_{j}_{q}")
                ev = p.op("scalar",
                          lambda E, o=bca[:, j, q * 512:(q + 1) * 512],
                          i=pB2: E.activation(o, i, AF.Identity),
                          [mm, war.get(("bca", j))],
                          tag=f"{name}bbev{half}_{j}_{q}")
                war_pb2 = ev
                evs.append(ev)
            bc_ready.append(evs)
        for d in range(8):
            db = d % 2
            ld_dl = p.dma("sync", dl_b[db], dl_hbm[d],
                          deps=[war.get(("dl", db))], tag=f"{name}s_dl{half}_{d}")
            uop = p.op("gpsimd", lambda E, o=u_b[db], a=dl_b[db],
                       b=xfm3[:, d, :]: E.tensor_tensor(o, a, b, ALU.mult),
                       [ld_dl, war.get(("u", db))], tag=f"{name}s_u{half}_{d}")
            dl_readers = [uop]
            u_readers = []
            yprev = yac_last.get(d)
            bca_readers = {j: [] for j in range(16)}
            pend = None
            for j in range(8):
                n = half * 8 + j
                nb = j % 2
                a_o = p.op("scalar", lambda E, o=a_b[nb], i=dl_b[db],
                           s=Asb[:, d, n:n + 1]:
                           E.activation(o, i, AF.Exp, scale=s),
                           [ld_dl, a_ld, war.get(("a", nb))],
                           tag=f"{name}s_a{half}_{d}_{j}")
                dl_readers.append(a_o)
                b_o = p.op("vector", lambda E, o=b_b[nb], a=u_b[db],
                           b=bca[:, j, :]: E.tensor_tensor(o, a, b, ALU.mult),
                           [uop, bc_ready[j], war.get(("b", nb))],
                           tag=f"{name}s_b{half}_{d}_{j}")
                u_readers.append(b_o)
                bca_readers[j].append(b_o)
                h_o = p.op("vector", lambda E, o=h_b[nb], a=a_b[nb], b=b_b[nb]:
                           E.tensor_tensor_scan(o, a, b, 0.0, ALU.mult, ALU.add),
                           [a_o, b_o, war.get(("h", nb))],
                           tag=f"{name}s_h{half}_{d}_{j}")
                war[("a", nb)] = h_o
                war[("b", nb)] = h_o
                hc_o = p.op("vector", lambda E, o=hc_b[nb], a=h_b[nb],
                            b=bca[:, 8 + j, :]:
                            E.tensor_tensor(o, a, b, ALU.mult),
                            [h_o, bc_ready[8 + j], war.get(("hc", nb))],
                            tag=f"{name}s_hc{half}_{d}_{j}")
                bca_readers[8 + j].append(hc_o)
                war[("h", nb)] = hc_o
                if pend is not None:
                    po, pap, pnb, pj = pend
                    yprev = _scan_yadd(p, yac[:, d, :], pap, po, yprev, war,
                                       ("hc", pnb), f"{name}s_y{half}_{d}_{pj}")
                pend = (hc_o, hc_b[nb], nb, j)
            po, pap, pnb, pj = pend
            yprev = _scan_yadd(p, yac[:, d, :], pap, po, yprev, war,
                               ("hc", pnb), f"{name}s_y{half}_{d}_{pj}")
            war[("dl", db)] = dl_readers
            war[("u", db)] = u_readers
            yac_last[d] = yprev
            if d == 7:
                for j in range(16):
                    war[("bca", j)] = bca_readers[j]
    yac_done = [yac_last[d] for d in range(8)]
    sl.free("b_xfm", "b_bca", "b_sel", "b_sdl0", "b_sdl1", "b_su0", "b_su1",
            "b_sa0", "b_sa1", "b_sb0", "b_sb1", "b_sh0", "b_sh1", "b_shc0",
            "b_shc1", "b_bm", "b_cm", "b_bbc", "b_A")
    p.barrier()

    ycp = sl.take(8 * 2048, F32, name="b_ycp").rearrange("p (d c) -> p d c", d=8)
    cps = []
    for d in range(8):
        c = p.op("vector", lambda E, o=ycp[:, d, :], i=yac[:, d, :]:
                 E.tensor_copy(o, i), [yac_done[d]], tag=f"{name}ycp{d}")
        cps.append(c)
    otile = [sl.take(1024, F32, name=f"b_ot{i}") for i in range(2)]
    pO = [ps["A"], ps["B"]]
    war6 = {}
    readers6 = []

    def src_fn(tt):
        pb = tt % 2
        trs = []
        for d in range(8):
            t = p.op("tensor", lambda E, o=pO[pb][:, d * 128:(d + 1) * 128],
                     i=ycp[:, d, tt * 128:(tt + 1) * 128], idn=dr["sb_eye"]:
                     E.transpose(o, i, idn),
                     [cps[d] if tt < 2 else None, war6.get(("p", pb)),
                      dr["sb_eye_ld"]],
                     tag=f"{name}ytr{tt}_{d}")
            trs.append(t)
        war_o = readers6[tt - 2] if tt >= 2 else None
        ev = p.op("vector", lambda E, o=otile[tt % 2], i=pO[pb]:
                  E.tensor_copy(o, i), [trs, war_o], tag=f"{name}yev{tt}")
        war6[("p", pb)] = ev
        return otile[tt % 2], [ev]

    stores = ln_res_add_store(p, nc, dr, sl, ps, src_fn, x_in_hbm, x_out_hbm,
                              readers_out=readers6, name=name)
    sl.free("b_yac", "b_ycp", "b_ot0", "b_ot1")
    p.barrier()
    return stores


STAGE_EMITTERS["l1_attn"] = emit_l1_attn



def emit_l0_ffn(p, nc, dr, sl, ps, x_in_hbm, x_out_hbm, name="f0_"):
    E = 16
    NCH = 4
    CW = 512
    GE = 2

    xfm = sl.take(8 * 1024, BF16, name="c_xfm")
    xlo = sl.take(8 * 1024, BF16, name="c_xlo")
    ln = emit_ln_to_fm(p, nc, dr, sl, ps, x_in_hbm, xfm, lo=xlo, name=name)
    p.barrier()
    xfm3 = xfm.rearrange("p (f t) -> p f t", f=8)
    xlo3 = xlo.rearrange("p (f t) -> p f t", f=8)

    gw32 = sl.take(8 * 16, F32, name="c_gw32").rearrange("p (k c) -> p k c", k=8)
    gwh = sl.take(8 * 8, BF16, name="c_gwh").rearrange("p (k c) -> p k c", k=8)
    gwl = sl.take(8 * 8, BF16, name="c_gwl").rearrange("p (k c) -> p k c", k=8)
    gbsb = sl.take(16, F32, name="c_gb")
    sc32 = sl.take(16, F32, name="c_sc")
    orig = sl.take(16, F32, name="c_orig")
    smask = sl.take(16, F32, name="c_smask")
    eqb = sl.take(16, F32, name="c_eq")
    gsum = sl.take(4, F32, name="c_gsum")
    keep = sl.take(4, F32, name="c_keep")
    gtmp = sl.take(4, F32, name="c_gtmp")
    colx = sl.take(8, F32, name="c_colx")
    comb_tm = sl.take(256, F32, name="c_comb")
    combfm = sl.take(1024, BF16, name="c_combfm")

    gw_ld = p.dma("sync", gw32, dr["wf0_gwT"].rearrange("(k p) c -> p k c", p=128),
                  tag=f"{name}gw_ld")
    gb_ld = p.dma("sync", gbsb, dr["wf0_gb"], tag=f"{name}gb_ld")
    ones_ld = p.dma("gpsimd", combfm[16:17, :], dr["c_ones1"],
                    tag=f"{name}ones_ld")
    ghi = p.op("vector", lambda E: E.tensor_copy(gwh, gw32), [gw_ld],
               tag=f"{name}gwh")
    glo = p.op("vector", lambda E: E.tensor_tensor(gwl, gw32, gwh, ALU.subtract),
               [ghi], tag=f"{name}gwl")
    pG = ps["C"][0:128, :]
    war_pg = None
    war_ct = None
    comb_done = []
    for tt in range(16):
        mms = []
        for ph, (xs, ws) in enumerate([(xfm3, gwh), (xlo3, gwh), (xfm3, gwl)]):
            for k in range(8):
                mm = p.op("tensor", lambda E, o=pG[:, 0:16],
                          w=xs[:, k, tt * 128:(tt + 1) * 128],
                          x=ws[:, k, :], s=(ph == 0 and k == 0),
                          t=(ph == 2 and k == 7):
                          E.matmul(o, w, x, start=s, stop=t),
                          [ghi, glo, war_pg, ln["evac"] if tt == 0 else None],
                          tag=f"{name}g{tt}_{ph}_{k}")
                mms.append(mm)
        deps = [mms[-1], war_ct]
        ev = p.op("vector", lambda E: E.tensor_copy(sc32, pG[:, 0:16]), deps,
                  tag=f"{name}gev{tt}")
        war_pg = ev
        negm = colx[:, 0:1]
        ssum = colx[:, 1:2]
        rcp = colx[:, 2:3]
        o1 = p.op("vector", lambda E: E.tensor_reduce(negm, sc32, AX.X, ALU.max,
                                                      negate=True),
                  [ev], tag=f"{name}gx{tt}_m")
        o2 = p.op("scalar", lambda E: E.activation(orig, sc32, AF.Exp, bias=negm,
                                                   accum_out=ssum),
                  [o1], tag=f"{name}gx{tt}_e")
        o3 = p.op("vector", lambda E: E.reciprocal(rcp, ssum), [o2],
                  tag=f"{name}gx{tt}_r")
        o4 = p.op("vector", lambda E: E.tensor_scalar(orig, orig, rcp, None,
                                                      ALU.mult),
                  [o3, o2], tag=f"{name}gx{tt}_n")
        o5 = p.op("vector", lambda E: E.tensor_tensor(smask, orig, gbsb, ALU.add),
                  [o4, gb_ld], tag=f"{name}gx{tt}_s")
        prev = o5
        for g in range(4):
            sg = smask[:, g * 4:(g + 1) * 4]
            m1 = gtmp[:, g:g + 1]
            a1 = p.op("vector", lambda E, sg=sg, m1=m1:
                      E.tensor_reduce(m1, sg, AX.X, ALU.max), [prev],
                      tag=f"{name}gx{tt}_g{g}a")
            a2 = p.op("vector", lambda E, sg=sg, m1=m1:
                      E.scalar_tensor_tensor(eqb[:, 0:4], sg, m1, sg,
                                             ALU.is_equal, ALU.bypass),
                      [a1], tag=f"{name}gx{tt}_g{g}b")
            a3 = p.op("vector", lambda E, sg=sg:
                      E.scalar_tensor_tensor(eqb[:, 4:8], eqb[:, 0:4], NEG, sg,
                                             ALU.mult, ALU.add),
                      [a2], tag=f"{name}gx{tt}_g{g}c")
            a4 = p.op("vector", lambda E, g=g:
                      E.tensor_reduce(eqb[:, 8:9], eqb[:, 4:8], AX.X, ALU.max),
                      [a3], tag=f"{name}gx{tt}_g{g}d")
            a5 = p.op("vector", lambda E, g=g, m1=m1:
                      E.tensor_tensor(gsum[:, g:g + 1], m1, eqb[:, 8:9],
                                      ALU.add),
                      [a4], tag=f"{name}gx{tt}_g{g}e")
            prev = a5
        b1 = p.op("vector", lambda E: E.tensor_reduce(colx[:, 3:4], gsum, AX.X,
                                                      ALU.max), [prev],
                  tag=f"{name}gx{tt}_k1")
        b2 = p.op("vector", lambda E: E.scalar_tensor_tensor(
            keep, gsum, colx[:, 3:4], gsum, ALU.is_equal, ALU.bypass),
            [b1], tag=f"{name}gx{tt}_k2")
        b3 = p.op("vector", lambda E: E.scalar_tensor_tensor(
            gtmp, keep, NEG, gsum, ALU.mult, ALU.add),
            [b2], tag=f"{name}gx{tt}_k3")
        b4 = p.op("vector", lambda E: E.tensor_reduce(colx[:, 4:5], gtmp, AX.X,
                                                      ALU.max), [b3],
                  tag=f"{name}gx{tt}_k4")
        b5 = p.op("vector", lambda E: E.scalar_tensor_tensor(
            gtmp, gtmp, colx[:, 4:5], gtmp, ALU.is_equal, ALU.bypass),
            [b4], tag=f"{name}gx{tt}_k5")
        b6 = p.op("vector", lambda E: E.tensor_tensor(keep, keep, gtmp, ALU.add),
                  [b5], tag=f"{name}gx{tt}_k6")
        b7 = p.op("vector", lambda E: E.tensor_scalar(gtmp, keep, -NEG, NEG,
                                                      ALU.mult, ALU.add),
                  [b6], tag=f"{name}gx{tt}_k7")
        prev = b7
        for g in range(4):
            c1 = p.op("vector", lambda E, g=g: E.tensor_scalar(
                smask[:, g * 4:(g + 1) * 4], smask[:, g * 4:(g + 1) * 4],
                gtmp[:, g:g + 1], None, ALU.add),
                [prev], tag=f"{name}gx{tt}_p{g}")
            prev = c1
        combv = comb_tm[:, tt * 16:(tt + 1) * 16]
        for it in range(2):
            d1 = p.op("vector", lambda E: E.tensor_reduce(colx[:, 5:6], smask,
                                                          AX.X, ALU.max),
                      [prev], tag=f"{name}gx{tt}_t{it}a")
            d2 = p.op("vector", lambda E: E.scalar_tensor_tensor(
                eqb, smask, colx[:, 5:6], smask, ALU.is_equal, ALU.bypass),
                [d1], tag=f"{name}gx{tt}_t{it}b")
            if it == 0:
                d3 = p.op("vector", lambda E, cv=combv: E.tensor_copy(cv, eqb),
                          [d2], tag=f"{name}gx{tt}_t{it}c")
            else:
                d3 = p.op("vector", lambda E, cv=combv: E.tensor_tensor(
                    cv, cv, eqb, ALU.add), [d2], tag=f"{name}gx{tt}_t{it}c")
            d4 = p.op("vector", lambda E: E.scalar_tensor_tensor(
                smask, eqb, NEG, smask, ALU.mult, ALU.add),
                [d3, d2], tag=f"{name}gx{tt}_t{it}d")
            prev = d4
        d5 = p.op("vector", lambda E, cv=combv: E.tensor_tensor(
            cv, cv, orig, ALU.mult), [prev], tag=f"{name}gx{tt}_cm")
        war_ct = d5
        comb_done.append(d5)
    pT = ps["D"][0:16, 0:128]
    war_pt = None
    combfm_done = [ones_ld]
    ctmp = sl.take(64, F32, name="c_ctf")
    for tt in range(16):
        t1 = p.op("vector", lambda E, tt=tt: E.tensor_copy(
            ctmp[:, 0:16], comb_tm[:, tt * 16:(tt + 1) * 16]),
            [comb_done[tt], war_pt], tag=f"{name}ctc{tt}")
        t2 = p.op("tensor", lambda E, tt=tt: E.transpose(
            pT, ctmp[:, 0:16], dr["sb_eye"]),
            [t1, dr["sb_eye_ld"], war_pt], tag=f"{name}ctt{tt}")
        t3 = p.op("scalar", lambda E, tt=tt: E.activation(
            combfm[0:16, tt * 128:(tt + 1) * 128], pT, AF.Identity),
            [t2], tag=f"{name}cte{tt}")
        war_pt = t3
        combfm_done.append(t3)
    sl.free("c_xlo", "c_gw32", "c_gwh", "c_gwl", "c_gb", "c_sc", "c_orig",
            "c_smask", "c_eq", "c_gsum", "c_keep", "c_gtmp", "c_colx", "c_ctf")
    p.barrier()

    yacc = sl.take(16 * 1024, F32, name="c_yacc").rearrange(
        "p (t c) -> p t c", t=16)
    ws1 = sl.take(8 * 512, BF16, name="c_ws1").rearrange("p (k c) -> p k c", k=8)
    ws3 = sl.take(8 * 512, BF16, name="c_ws3").rearrange("p (k c) -> p k c", k=8)
    bs1 = sl.take(8, F32, name="c_bs1").rearrange("p (d c) -> p d c", d=8)
    bs3 = sl.take(8, F32, name="c_bs3").rearrange("p (d c) -> p d c", d=8)
    z1 = sl.take(8 * 1024, BF16, name="c_z1").rearrange("p (m c) -> p m c", m=8)
    z3 = sl.take(8 * 1024, BF16, name="c_z3").rearrange("p (m c) -> p m c", m=8)
    ws1_ld = p.dma("gpsimd", ws1, dr["wf0_ws1"].rearrange("(k p) c -> p k c", p=128),
                   tag=f"{name}ws1_ld")
    ws3_ld = p.dma("gpsimd", ws3, dr["wf0_ws3"].rearrange("(k p) c -> p k c", p=128),
                   tag=f"{name}ws3_ld")
    bs1_ld = p.dma("sync", bs1, dr["wf0_bs1"].rearrange("p (d c) -> p d c", d=8),
                   tag=f"{name}bs1_ld")
    bs3_ld = p.dma("sync", bs3, dr["wf0_bs3"].rearrange("p (d c) -> p d c", d=8),
                   tag=f"{name}bs3_ld")
    pZ = [ps["C"], ps["D"]]
    war_pz = {0: None, 1: None}
    z_done = []
    for src_i, (wv, wld, bv, bld, zv, act) in enumerate([
            (ws1, ws1_ld, bs1, bs1_ld, z1, AF.Silu),
            (ws3, ws3_ld, bs3, bs3_ld, z3, AF.Identity)]):
        for m in range(8):
            for ch in range(NCH):
                pb = (m * NCH + ch) % 2
                mms = []
                for k in range(8):
                    mm = p.op("tensor", lambda E, o=pZ[pb],
                              w=wv[:, k, m * 128:(m + 1) * 128],
                              x=xfm3[:, k, ch * CW:(ch + 1) * CW],
                              s=(k == 0), t=(k == 7):
                              E.matmul(o, w, x, start=s, stop=t),
                              [wld, war_pz[pb]], tag=f"{name}z{src_i}_{m}_{ch}_{k}")
                    mms.append(mm)
                ev = p.op("scalar", lambda E, o=zv[:, m, ch * CW:(ch + 1) * CW],
                          i=pZ[pb], b=bv[:, m, 0:1], a=act:
                          E.activation(o, i, a, bias=b),
                          [mms[-1], bld], tag=f"{name}zev{src_i}_{m}_{ch}")
                war_pz[pb] = ev
                z_done.append(ev)
    zm_done = []
    for m in range(8):
        zz = p.op("vector", lambda E, m=m: E.tensor_tensor(
            z1[:, m, :], z1[:, m, :], z3[:, m, :], ALU.mult),
            [z_done], tag=f"{name}zm{m}")
        zm_done.append(zz)
    sl.free("c_ws1", "c_ws3")
    ws2 = sl.take(8 * 512, BF16, name="c_ws2").rearrange("p (k c) -> p k c", k=8)
    be2s = sl.take(512, BF16, name="c_be2s")
    ws2_ld = p.dma("gpsimd", ws2, dr["wf0_ws2"].rearrange("(k p) c -> p k c", p=128),
                   deps=[z_done], tag=f"{name}ws2_ld")
    be2s_ld = p.dma("gpsimd", be2s[0:17, :], dr["wf0_be2s"],
                    deps=[z_done], tag=f"{name}be2s_ld")
    pY = [ps["A"], ps["B"]]
    war_py = {0: None, 1: None}
    yinit = []
    for tt in range(16):
        pb = tt % 2
        mms = []
        for half in range(2):
            po = pY[pb][:, half * 512:(half + 1) * 512]
            for k in range(8):
                mm = p.op("tensor", lambda E, o=po,
                          w=z1[:, k, tt * 128:(tt + 1) * 128],
                          x=ws2[:, k, half * 512:(half + 1) * 512],
                          s=(k == 0), t=False:
                          E.matmul(o, w, x, start=s, stop=t),
                          [ws2_ld, zm_done[k] if tt == 0 else None,
                           war_py[pb]], tag=f"{name}y{tt}_{half}_{k}")
                mms.append(mm)
            mm = p.op("tensor", lambda E, o=po,
                      w=combfm[0:17, tt * 128:(tt + 1) * 128],
                      x=be2s[0:17, half * 512:(half + 1) * 512]:
                      E.matmul(o, w, x, start=False, stop=True),
                      [be2s_ld, combfm_done], tag=f"{name}yb{tt}_{half}")
            mms.append(mm)
        ev = p.op("vector", lambda E, tt=tt, pb=pb: E.tensor_copy(
            yacc[:, tt, :], pY[pb]), [mms], tag=f"{name}yev{tt}")
        war_py[pb] = ev
        yinit.append(ev)
    sl.free("c_bs1", "c_bs3", "c_z1", "c_z3", "c_ws2", "c_be2s")
    p.barrier()

    we13 = [sl.take(2 * 2048, BF16, name=f"c_we13_{i}") for i in range(2)]
    we2 = [sl.take(4 * 512, BF16, name=f"c_we2_{i}") for i in range(3)]
    hm = [sl.take(4 * 1024, BF16, name=f"c_hm_{i}").rearrange(
        "p (k c) -> p k c", k=4) for i in range(2)]
    cbb = [sl.take(1024, BF16, name=f"c_cb_{i}") for i in range(2)]
    be1a = sl.take(16 * 4, F32, name="c_be1").rearrange("p (e c) -> p e c", e=16)
    be3a = sl.take(16 * 4, F32, name="c_be3").rearrange("p (e c) -> p e c", e=16)
    s1stg = [sl.take(256, BF16, name=f"c_s1_{i}") for i in range(2)]
    s3stg = [sl.take(256, BF16, name=f"c_s3_{i}") for i in range(2)]
    selb = sl.take(1024, BF16, name="c_selb")
    be1_ld = p.dma("sync", be1a, dr["wf0_be1"].rearrange("e p c -> p e c"),
                   tag=f"{name}be1_ld")
    be3_ld = p.dma("sync", be3a, dr["wf0_be3"].rearrange("e p c -> p e c"),
                   tag=f"{name}be3_ld")
    sel_ld = p.dma("gpsimd", selb[0:16, :], dr["c_sel16"], tag=f"{name}selb_ld")

    war_w13 = {0: None, 1: None}
    war_w2 = {0: None, 1: None, 2: None}
    war_hm = {0: None, 1: None}
    war_cb = {0: None, 1: None}
    war_s1 = {0: None, 1: None}
    war_s3 = {0: None, 1: None}
    war_pc = {0: None, 1: None, 2: None}
    war_pb = None
    pH = [ps["D"], ps["E"], ps["F"]]
    pBC2 = ps["C"]
    yfin = list(yinit)
    w2_lds = {}
    hm_done_by_e = {}
    for e in range(E):
        wb = e % 2
        w2b = e % 3
        w13v = we13[wb].rearrange("p (w k c) -> p w k c", w=2, k=8)
        w13_ld = p.dma("gpsimd", w13v[:, 0, :, :],
                       dr["wf0_We1"][e].rearrange("(k p) c -> p k c", p=128),
                       deps=[war_w13[wb]], tag=f"{name}we1_{e}")
        w13_ld2 = p.dma("gpsimd", w13v[:, 1, :, :],
                        dr["wf0_We3"][e].rearrange("(k p) c -> p k c", p=128),
                        deps=[war_w13[wb]], tag=f"{name}we3_{e}")
        w2_ld = p.dma("gpsimd", we2[w2b].rearrange("p (k c) -> p k c", k=4),
                      dr["wf0_We2"][e].rearrange("(k p) c -> p k c", p=128),
                      deps=[war_w2[w2b]], tag=f"{name}we2_{e}")
        w2_lds[e] = w2_ld
        cb_evs = [None] * 4
        hm_done = []
        for im in range(4):
            for ch in range(NCH):
                if im == 0:
                    q = ch
                    mm = p.op("tensor", lambda E_, o=pBC2, q=q,
                              w=selb[0:16, e * 128:(e + 1) * 128],
                              x=combfm[0:16, q * 512:(q + 1) * 512]:
                              E_.matmul(o, w, x, start=True, stop=True),
                              [sel_ld, combfm_done, war_pb],
                              tag=f"{name}cbm{e}_{q}")
                    ev = p.op("scalar", lambda E_, q=q,
                              o=cbb[wb][:, q * 512:(q + 1) * 512],
                              i=pBC2: E_.activation(o, i, AF.Identity),
                              [mm, war_cb[wb]], tag=f"{name}cbe{e}_{q}")
                    war_pb = ev
                    cb_evs[q] = ev
                hs = 2 * (im * NCH + ch)
                hb1 = hs % 3
                hb3 = (hs + 1) % 3
                sb = (im * NCH + ch) % 2
                mm1s = []
                for k in range(8):
                    mm = p.op("tensor", lambda E_, o=pH[hb1],
                              w=w13v[:, 0, k, im * 128:(im + 1) * 128],
                              x=xfm3[:, k, ch * CW:(ch + 1) * CW],
                              s=(k == 0), t=(k == 7):
                              E_.matmul(o, w, x, start=s, stop=t),
                              [w13_ld, war_pc[hb1]], tag=f"{name}h1_{e}_{im}_{ch}_{k}")
                    mm1s.append(mm)
                ev1 = p.op("scalar", lambda E_, o=s1stg[sb], i=pH[hb1],
                           b=be1a[:, e, im:im + 1]:
                           E_.activation(o, i, AF.Silu, bias=b),
                           [mm1s[-1], be1_ld, war_s1[sb]],
                           tag=f"{name}h1e_{e}_{im}_{ch}")
                war_pc[hb1] = ev1
                mm3s = []
                for k in range(8):
                    mm = p.op("tensor", lambda E_, o=pH[hb3],
                              w=w13v[:, 1, k, im * 128:(im + 1) * 128],
                              x=xfm3[:, k, ch * CW:(ch + 1) * CW],
                              s=(k == 0), t=(k == 7):
                              E_.matmul(o, w, x, start=s, stop=t),
                              [w13_ld2, war_pc[hb3]], tag=f"{name}h3_{e}_{im}_{ch}_{k}")
                    mm3s.append(mm)
                ev3 = p.op("scalar", lambda E_, o=s3stg[sb], i=pH[hb3],
                           b=be3a[:, e, im:im + 1]:
                           E_.activation(o, i, AF.Identity, bias=b),
                           [mm3s[-1], be3_ld, war_s3[sb]],
                           tag=f"{name}h3e_{e}_{im}_{ch}")
                war_pc[hb3] = ev3
                hv = hm[wb][:, im, ch * CW:(ch + 1) * CW]
                m1 = p.op("vector", lambda E_, o=hv, a=s1stg[sb], b=s3stg[sb]:
                          E_.tensor_tensor(o, a, b, ALU.mult),
                          [ev1, ev3, war_hm[wb]], tag=f"{name}hm_{e}_{im}_{ch}")
                war_s1[sb] = m1
                war_s3[sb] = m1
                m2 = p.op("gpsimd", lambda E_, o=hv, a=hv,
                          b=cbb[wb][:, ch * CW:(ch + 1) * CW]:
                          E_.tensor_tensor(o, a, b, ALU.mult),
                          [m1, cb_evs[ch]], tag=f"{name}hs_{e}_{im}_{ch}")
                hm_done.append(m2)
        war_cb[wb] = hm_done
        hm_done_by_e[e] = hm_done
        if e % GE == GE - 1:
            group = [e - 1, e]
            new_yfin = []
            for tt in range(16):
                pb = tt % 2
                mms = []
                for gi, ee in enumerate(group):
                    wbb = ee % 2
                    w2v = we2[ee % 3].rearrange("p (k c) -> p k c", k=4)
                    for half in range(2):
                        po = pY[pb][:, half * 512:(half + 1) * 512]
                        for k in range(4):
                            mm = p.op("tensor", lambda E_, o=po,
                                      w=hm[wbb][:, k, tt * 128:(tt + 1) * 128],
                                      x=w2v[:, k, half * 512:(half + 1) * 512],
                                      s=(gi == 0 and k == 0),
                                      t=(gi == 1 and k == 3):
                                      E_.matmul(o, w, x, start=s, stop=t),
                                      [w2_lds[ee],
                                       hm_done_by_e[ee] if tt == 0 else None,
                                       war_py[pb]],
                                      tag=f"{name}dn{ee}_{tt}_{half}_{k}")
                            mms.append(mm)
                ad = p.op("vector", lambda E_, tt=tt, pb=pb:
                          E_.tensor_tensor(yacc[:, tt, :], pY[pb],
                                           yacc[:, tt, :], ALU.add),
                          [mms, yfin[tt]], tag=f"{name}ya{e}_{tt}")
                war_py[pb] = ad
                new_yfin.append(ad)
            yfin = new_yfin
            war_w13[0] = hm_done_by_e[e - 1]
            war_w13[1] = hm_done_by_e[e]
            war_hm[0] = new_yfin
            war_hm[1] = new_yfin
            war_w2[(e - 1) % 3] = new_yfin
            war_w2[e % 3] = new_yfin
    sl.free("c_we13_0", "c_we13_1", "c_we2_0", "c_we2_1", "c_we2_2", "c_hm_0",
            "c_hm_1", "c_cb_0", "c_cb_1", "c_be1", "c_be3", "c_s1_0", "c_s1_1",
            "c_s3_0", "c_s3_1", "c_selb", "c_xfm", "c_combfm", "c_comb")
    p.barrier()

    def src_fn(tt):
        return yacc[:, tt, :], [yfin[tt]]

    readers6 = []
    stores = ln_res_add_store(p, nc, dr, sl, ps, src_fn, x_in_hbm, x_out_hbm,
                              readers_out=readers6, name=name)
    sl.free("c_yacc")
    p.barrier()
    return stores


STAGE_EMITTERS["l0_ffn"] = emit_l0_ffn



def emit_l1_ffn(p, nc, dr, sl, ps, x_in_hbm, x_out_hbm, name="f1_"):
    E = 32
    NCH = 4
    CW = 512
    GE = 2

    xfm = sl.take(8 * 1024, BF16, name="d_xfm")
    xlo = sl.take(8 * 1024, BF16, name="d_xlo")
    ln = emit_ln_to_fm(p, nc, dr, sl, ps, x_in_hbm, xfm, lo=xlo, name=name)
    p.barrier()
    xfm3 = xfm.rearrange("p (f t) -> p f t", f=8)
    xlo3 = xlo.rearrange("p (f t) -> p f t", f=8)

    rw32 = sl.take(8 * 32, F32, name="d_rw32").rearrange("p (k c) -> p k c", k=8)
    rwh = sl.take(8 * 16, BF16, name="d_rwh").rearrange("p (k c) -> p k c", k=8)
    rwl = sl.take(8 * 16, BF16, name="d_rwl").rearrange("p (k c) -> p k c", k=8)
    ebsb = sl.take(32, F32, name="d_eb")
    sc32 = sl.take(32, F32, name="d_sc")
    orig = sl.take(32, F32, name="d_orig")
    smask = sl.take(32, F32, name="d_smask")
    eqb = sl.take(32, F32, name="d_eq")
    maskb = sl.take(32, F32, name="d_mask")
    colx = sl.take(8, F32, name="d_colx")
    comb_tm = sl.take(512, F32, name="d_comb")
    combfm = sl.take(1024, BF16, name="d_combfm")

    rw_ld = p.dma("sync", rw32, dr["wf1_rwT"].rearrange("(k p) c -> p k c", p=128),
                  tag=f"{name}rw_ld")
    eb_ld = p.dma("sync", ebsb, dr["wf1_eb"], tag=f"{name}eb_ld")
    rhi = p.op("vector", lambda E: E.tensor_copy(rwh, rw32), [rw_ld],
               tag=f"{name}rwh")
    rlo = p.op("vector", lambda E: E.tensor_tensor(rwl, rw32, rwh, ALU.subtract),
               [rhi], tag=f"{name}rwl")
    pG = ps["C"]
    war_pg = None
    war_ct = None
    comb_done = []
    for tt in range(16):
        mms = []
        for ph, (xs, ws) in enumerate([(xfm3, rwh), (xlo3, rwh), (xfm3, rwl)]):
            for k in range(8):
                mm = p.op("tensor", lambda E, o=pG[:, 0:32],
                          w=xs[:, k, tt * 128:(tt + 1) * 128],
                          x=ws[:, k, :], s=(ph == 0 and k == 0),
                          t=(ph == 2 and k == 7):
                          E.matmul(o, w, x, start=s, stop=t),
                          [rhi, rlo, war_pg, ln["evac"] if tt == 0 else None],
                          tag=f"{name}r{tt}_{ph}_{k}")
                mms.append(mm)
        ev = p.op("vector", lambda E: E.tensor_copy(sc32, pG[:, 0:32]),
                  [mms[-1], war_ct], tag=f"{name}rev{tt}")
        war_pg = ev
        negm = colx[:, 0:1]
        ssum = colx[:, 1:2]
        rcp = colx[:, 2:3]
        o1 = p.op("vector", lambda E: E.tensor_reduce(negm, sc32, AX.X, ALU.max,
                                                      negate=True),
                  [ev], tag=f"{name}rx{tt}_m")
        o2 = p.op("scalar", lambda E: E.activation(orig, sc32, AF.Exp, bias=negm,
                                                   accum_out=ssum),
                  [o1], tag=f"{name}rx{tt}_e")
        o3 = p.op("vector", lambda E: E.reciprocal(rcp, ssum), [o2],
                  tag=f"{name}rx{tt}_r")
        o4 = p.op("vector", lambda E: E.tensor_scalar(orig, orig, rcp, None,
                                                      ALU.mult),
                  [o3, o2], tag=f"{name}rx{tt}_n")
        o5 = p.op("vector", lambda E: E.tensor_tensor(smask, orig, ebsb, ALU.add),
                  [o4, eb_ld], tag=f"{name}rx{tt}_s")
        prev = o5
        first = True
        for it in range(6):
            d1 = p.op("vector", lambda E: E.tensor_reduce(colx[:, 3:4], smask,
                                                          AX.X, ALU.max),
                      [prev], tag=f"{name}rx{tt}_t{it}a")
            d2 = p.op("vector", lambda E: E.scalar_tensor_tensor(
                eqb, smask, colx[:, 3:4], smask, ALU.is_equal, ALU.bypass),
                [d1], tag=f"{name}rx{tt}_t{it}b")
            if first:
                d3 = p.op("vector", lambda E: E.tensor_copy(maskb, eqb), [d2],
                          tag=f"{name}rx{tt}_t{it}c")
                first = False
            else:
                d3 = p.op("vector", lambda E: E.tensor_tensor(maskb, maskb, eqb,
                                                              ALU.add),
                          [d2], tag=f"{name}rx{tt}_t{it}c")
            d4 = p.op("vector", lambda E: E.scalar_tensor_tensor(
                smask, eqb, NEG, smask, ALU.mult, ALU.add),
                [d3, d2], tag=f"{name}rx{tt}_t{it}d")
            prev = d4
        combv = comb_tm[:, tt * 32:(tt + 1) * 32]
        e1 = p.op("vector", lambda E, cv=combv: E.tensor_tensor(
            cv, orig, maskb, ALU.mult), [prev], tag=f"{name}rx{tt}_w")
        e2 = p.op("vector", lambda E, cv=combv: E.tensor_reduce(
            colx[:, 4:5], cv, AX.X, ALU.add), [e1], tag=f"{name}rx{tt}_ws")
        e3 = p.op("vector", lambda E: E.tensor_scalar(
            colx[:, 4:5], colx[:, 4:5], 1e-12, None, ALU.max),
            [e2], tag=f"{name}rx{tt}_cl")
        e4 = p.op("vector", lambda E: E.reciprocal(colx[:, 5:6], colx[:, 4:5]),
                  [e3], tag=f"{name}rx{tt}_rc")
        e5 = p.op("vector", lambda E, cv=combv: E.tensor_scalar(
            cv, cv, colx[:, 5:6], None, ALU.mult),
            [e4], tag=f"{name}rx{tt}_cm")
        war_ct = e5
        comb_done.append(e5)
    pT = ps["D"][0:32, 0:128]
    war_pt = None
    combfm_done = []
    ctmp = sl.take(64, F32, name="d_ctf")
    for tt in range(16):
        t1 = p.op("vector", lambda E, tt=tt: E.tensor_copy(
            ctmp[:, 0:32], comb_tm[:, tt * 32:(tt + 1) * 32]),
            [comb_done[tt], war_pt], tag=f"{name}ctc{tt}")
        t2 = p.op("tensor", lambda E, tt=tt: E.transpose(
            pT, ctmp[:, 0:32], dr["sb_eye"]),
            [t1, dr["sb_eye_ld"], war_pt], tag=f"{name}ctt{tt}")
        t3 = p.op("scalar", lambda E, tt=tt: E.activation(
            combfm[0:32, tt * 128:(tt + 1) * 128], pT, AF.Identity),
            [t2], tag=f"{name}cte{tt}")
        war_pt = t3
        combfm_done.append(t3)
    sl.free("d_xlo", "d_rw32", "d_rwh", "d_rwl", "d_eb", "d_sc", "d_orig",
            "d_smask", "d_eq", "d_mask", "d_colx", "d_ctf", "d_comb")
    p.barrier()

    yacc = sl.take(16 * 1024, F32, name="d_yacc").rearrange(
        "p (t c) -> p t c", t=16)
    ws1 = sl.take(8 * 512, BF16, name="d_ws1").rearrange("p (k c) -> p k c", k=8)
    ws3 = sl.take(8 * 512, BF16, name="d_ws3").rearrange("p (k c) -> p k c", k=8)
    z1 = sl.take(8 * 1024, BF16, name="d_z1").rearrange("p (m c) -> p m c", m=8)
    z3 = sl.take(8 * 1024, BF16, name="d_z3").rearrange("p (m c) -> p m c", m=8)
    ws1_ld = p.dma("gpsimd", ws1, dr["wf1_sgwT"].rearrange("(k p) c -> p k c", p=128),
                   tag=f"{name}sgw_ld")
    ws3_ld = p.dma("gpsimd", ws3, dr["wf1_suwT"].rearrange("(k p) c -> p k c", p=128),
                   tag=f"{name}suw_ld")
    pZ = [ps["C"], ps["D"]]
    war_pz = {0: None, 1: None}
    z_done = []
    for src_i, (wv, wld, zv, act) in enumerate([
            (ws1, ws1_ld, z1, AF.Silu), (ws3, ws3_ld, z3, AF.Identity)]):
        for m in range(8):
            for ch in range(NCH):
                pb = (m * NCH + ch) % 2
                mms = []
                for k in range(8):
                    mm = p.op("tensor", lambda E, o=pZ[pb],
                              w=wv[:, k, m * 128:(m + 1) * 128],
                              x=xfm3[:, k, ch * CW:(ch + 1) * CW],
                              s=(k == 0), t=(k == 7):
                              E.matmul(o, w, x, start=s, stop=t),
                              [wld, war_pz[pb]], tag=f"{name}z{src_i}_{m}_{ch}_{k}")
                    mms.append(mm)
                ev = p.op("scalar", lambda E, o=zv[:, m, ch * CW:(ch + 1) * CW],
                          i=pZ[pb], a=act: E.activation(o, i, a),
                          [mms[-1]], tag=f"{name}zev{src_i}_{m}_{ch}")
                war_pz[pb] = ev
                z_done.append(ev)
    zm_done = []
    for m in range(8):
        zz = p.op("vector", lambda E, m=m: E.tensor_tensor(
            z1[:, m, :], z1[:, m, :], z3[:, m, :], ALU.mult),
            [z_done], tag=f"{name}zm{m}")
        zm_done.append(zz)
    sl.free("d_ws1", "d_ws3")
    ws2 = sl.take(8 * 512, BF16, name="d_ws2").rearrange("p (k c) -> p k c", k=8)
    ws2_ld = p.dma("gpsimd", ws2, dr["wf1_sdwT"].rearrange("(k p) c -> p k c", p=128),
                   deps=[z_done], tag=f"{name}sdw_ld")
    pY = [ps["A"], ps["B"]]
    war_py = {0: None, 1: None}
    yinit = []
    for tt in range(16):
        pb = tt % 2
        mms = []
        for half in range(2):
            po = pY[pb][:, half * 512:(half + 1) * 512]
            for k in range(8):
                mm = p.op("tensor", lambda E, o=po,
                          w=z1[:, k, tt * 128:(tt + 1) * 128],
                          x=ws2[:, k, half * 512:(half + 1) * 512],
                          s=(k == 0), t=(k == 7):
                          E.matmul(o, w, x, start=s, stop=t),
                          [ws2_ld, zm_done[k] if tt == 0 else None,
                           war_py[pb]], tag=f"{name}y{tt}_{half}_{k}")
                mms.append(mm)
        ev = p.op("vector", lambda E, tt=tt, pb=pb: E.tensor_copy(
            yacc[:, tt, :], pY[pb]), [mms], tag=f"{name}yev{tt}")
        war_py[pb] = ev
        yinit.append(ev)
    sl.free("d_z1", "d_z3", "d_ws2")
    p.barrier()

    wgup = [sl.take(8 * 512, BF16, name=f"d_wg_{i}") for i in range(2)]
    wdn = [sl.take(4 * 512, BF16, name=f"d_wd_{i}") for i in range(3)]
    hm = [sl.take(4 * 1024, BF16, name=f"d_hm_{i}").rearrange(
        "p (k c) -> p k c", k=4) for i in range(2)]
    cbb = [sl.take(1024, BF16, name=f"d_cb_{i}") for i in range(2)]
    s1stg = [sl.take(256, BF16, name=f"d_s1_{i}") for i in range(2)]
    s3stg = [sl.take(256, BF16, name=f"d_s3_{i}") for i in range(2)]
    selb = [sl.take(64, BF16, name=f"d_selb{i}") for i in range(2)]

    war_wg = {0: None, 1: None}
    war_wd = {0: None, 1: None, 2: None}
    war_sel = {0: None, 1: None}
    war_hm = {0: None, 1: None}
    war_cb = {0: None, 1: None}
    war_s1 = {0: None, 1: None}
    war_s3 = {0: None, 1: None}
    war_pc = {0: None, 1: None, 2: None}
    war_pb = None
    pH = [ps["D"], ps["E"], ps["F"]]
    pBC2 = ps["C"]
    yfin = list(yinit)
    wd_lds = {}
    hm_done_by_e = {}
    for e in range(E):
        wb = e % 2
        w2b = e % 3
        wgv = wgup[wb].rearrange("p (k c) -> p k c", k=8)
        wg_ld = p.dma("gpsimd", wgv,
                      dr["wf1_gupT"][e].rearrange("(k p) c -> p k c", p=128),
                      deps=[war_wg[wb]], tag=f"{name}wg_{e}")
        wd_ld = p.dma("gpsimd", wdn[w2b].rearrange("p (k c) -> p k c", k=4),
                      dr["wf1_downT"][e].rearrange("(k p) c -> p k c", p=128),
                      deps=[war_wd[w2b]], tag=f"{name}wd_{e}")
        wd_lds[e] = wd_ld
        sel_ld = p.dma("gpsimd", selb[wb][0:32, :],
                       dr["c_sel32"][:, e * 128:(e + 1) * 128],
                       deps=[war_sel[wb]], tag=f"{name}sel_{e}")
        cb_evs = [None] * 4
        hm_done = []
        for im in range(4):
            for ch in range(NCH):
                if im == 0:
                    q = ch
                    mm = p.op("tensor", lambda E_, o=pBC2, q=q,
                              w=selb[wb][0:32, :],
                              x=combfm[0:32, q * 512:(q + 1) * 512]:
                              E_.matmul(o, w, x, start=True, stop=True),
                              [sel_ld, combfm_done, war_pb],
                              tag=f"{name}cbm{e}_{q}")
                    ev = p.op("scalar", lambda E_, q=q,
                              o=cbb[wb][:, q * 512:(q + 1) * 512],
                              i=pBC2: E_.activation(o, i, AF.Identity),
                              [mm, war_cb[wb]], tag=f"{name}cbe{e}_{q}")
                    war_pb = ev
                    cb_evs[q] = ev
                hs = 2 * (im * NCH + ch)
                hb1 = hs % 3
                hb3 = (hs + 1) % 3
                sb = (im * NCH + ch) % 2
                mm1s = []
                for k in range(8):
                    mm = p.op("tensor", lambda E_, o=pH[hb1],
                              w=wgv[:, k, im * 128:(im + 1) * 128],
                              x=xfm3[:, k, ch * CW:(ch + 1) * CW],
                              s=(k == 0), t=(k == 7):
                              E_.matmul(o, w, x, start=s, stop=t),
                              [wg_ld, war_pc[hb1]], tag=f"{name}g1_{e}_{im}_{ch}_{k}")
                    mm1s.append(mm)
                ev1 = p.op("scalar", lambda E_, o=s1stg[sb], i=pH[hb1]:
                           E_.activation(o, i, AF.Silu),
                           [mm1s[-1], war_s1[sb]], tag=f"{name}g1e_{e}_{im}_{ch}")
                war_pc[hb1] = ev1
                mm3s = []
                for k in range(8):
                    mm = p.op("tensor", lambda E_, o=pH[hb3],
                              w=wgv[:, k, (4 + im) * 128:(5 + im) * 128],
                              x=xfm3[:, k, ch * CW:(ch + 1) * CW],
                              s=(k == 0), t=(k == 7):
                              E_.matmul(o, w, x, start=s, stop=t),
                              [wg_ld, war_pc[hb3]], tag=f"{name}g3_{e}_{im}_{ch}_{k}")
                    mm3s.append(mm)
                ev3 = p.op("scalar", lambda E_, o=s3stg[sb], i=pH[hb3]:
                           E_.activation(o, i, AF.Identity),
                           [mm3s[-1], war_s3[sb]], tag=f"{name}g3e_{e}_{im}_{ch}")
                war_pc[hb3] = ev3
                hv = hm[wb][:, im, ch * CW:(ch + 1) * CW]
                m1 = p.op("vector", lambda E_, o=hv, a=s1stg[sb], b=s3stg[sb]:
                          E_.tensor_tensor(o, a, b, ALU.mult),
                          [ev1, ev3, war_hm[wb]], tag=f"{name}gm_{e}_{im}_{ch}")
                war_s1[sb] = m1
                war_s3[sb] = m1
                m2 = p.op("gpsimd", lambda E_, o=hv, a=hv,
                          b=cbb[wb][:, ch * CW:(ch + 1) * CW]:
                          E_.tensor_tensor(o, a, b, ALU.mult),
                          [m1, cb_evs[ch]], tag=f"{name}gs_{e}_{im}_{ch}")
                hm_done.append(m2)
        war_cb[wb] = hm_done
        war_sel[wb] = cb_evs
        hm_done_by_e[e] = hm_done
        if e % GE == GE - 1:
            group = [e - 1, e]
            new_yfin = []
            for tt in range(16):
                pb = tt % 2
                mms = []
                for gi, ee in enumerate(group):
                    wbb = ee % 2
                    wdv = wdn[ee % 3].rearrange("p (k c) -> p k c", k=4)
                    for half in range(2):
                        po = pY[pb][:, half * 512:(half + 1) * 512]
                        for k in range(4):
                            mm = p.op("tensor", lambda E_, o=po,
                                      w=hm[wbb][:, k, tt * 128:(tt + 1) * 128],
                                      x=wdv[:, k, half * 512:(half + 1) * 512],
                                      s=(gi == 0 and k == 0),
                                      t=(gi == 1 and k == 3):
                                      E_.matmul(o, w, x, start=s, stop=t),
                                      [wd_lds[ee],
                                       hm_done_by_e[ee] if tt == 0 else None,
                                       war_py[pb]],
                                      tag=f"{name}dn{ee}_{tt}_{half}_{k}")
                            mms.append(mm)
                ad = p.op("vector", lambda E_, tt=tt, pb=pb:
                          E_.tensor_tensor(yacc[:, tt, :], pY[pb],
                                           yacc[:, tt, :], ALU.add),
                          [mms, yfin[tt]], tag=f"{name}ya{e}_{tt}")
                war_py[pb] = ad
                new_yfin.append(ad)
            yfin = new_yfin
            war_wg[0] = hm_done_by_e[e - 1]
            war_wg[1] = hm_done_by_e[e]
            war_hm[0] = new_yfin
            war_hm[1] = new_yfin
            war_wd[(e - 1) % 3] = new_yfin
            war_wd[e % 3] = new_yfin
    sl.free("d_wg_0", "d_wg_1", "d_wd_0", "d_wd_1", "d_wd_2", "d_hm_0",
            "d_hm_1", "d_cb_0", "d_cb_1", "d_s1_0", "d_s1_1", "d_s3_0",
            "d_s3_1", "d_selb0", "d_selb1", "d_xfm", "d_combfm")
    p.barrier()

    def src_fn(tt):
        return yacc[:, tt, :], [yfin[tt]]

    readers6 = []
    stores = ln_res_add_store(p, nc, dr, sl, ps, src_fn, x_in_hbm, x_out_hbm,
                              readers_out=readers6, name=name)
    sl.free("d_yacc")
    p.barrier()
    return stores


STAGE_EMITTERS["l1_ffn"] = emit_l1_ffn
